# version 01
# baseline (speedup 1.0000x reference)
import numpy as np
import ml_dtypes
from contextlib import ExitStack
import concourse.bass as bass
import concourse.mybir as mybir
from concourse.bass_utils import run_bass_kernel_spmd

F32 = mybir.dt.float32
BF16 = mybir.dt.bfloat16
I32 = mybir.dt.int32
AF = mybir.ActivationFunctionType
ALU = mybir.AluOpType
BF = ml_dtypes.bfloat16
AX = mybir.AxisListType.X

HID = 2048
INTER = 4096
NH = 64
FFH = 5632
T = 2048
CS = 128
NCH = T // CS
NTT = T // 128
NKD = HID // 128
NHT = FFH // 128
EPS = 1e-5


class Pr:
    def __init__(self, nc, es):
        self.nc, self.es = nc, es
        self.h, self.c = {}, {}

    def sem(self, *names):
        for n in names:
            self.h[n] = self.es.enter_context(self.nc.semaphore(n))
            self.c[n] = 0

    def inc(self, instr, s, n=1):
        instr.then_inc(self.h[s], n)
        self.c[s] += n
        return self.c[s]

    def w(self, eng, s, v=None):
        v = self.c[s] if v is None else v
        if v > 0:
            eng.wait_ge(self.h[s], v)


def build_program():
    nc = bass.Bass("TRN2", target_bir_lowering=False)
    dt_ = nc.dram_tensor
    xo = dt_("xo", [T, HID], F32, kind="ExternalInput").ap()
    xc = dt_("xc", [T, HID], F32, kind="ExternalInput").ap()
    rmaskd = dt_("rmaskd", [1, 1], F32, kind="ExternalInput").ap()
    wip = dt_("wip", [HID, 8512], F32, kind="ExternalInput").ap()
    convw = dt_("convw", [4352, 4], F32, kind="ExternalInput").ap()
    convbr = dt_("convbr", [1, 4352], F32, kind="ExternalInput").ap()
    dtbr = dt_("dtbr", [1, NH], F32, kind="ExternalInput").ap()
    alogr = dt_("alogr", [1, NH], F32, kind="ExternalInput").ap()
    dvecr = dt_("dvecr", [1, NH], F32, kind="ExternalInput").ap()
    nwr = dt_("nwr", [1, INTER], F32, kind="ExternalInput").ap()
    wout = dt_("wout", [INTER, HID], F32, kind="ExternalInput").ap()
    wgt = dt_("wgt", [HID, FFH], I32, kind="ExternalInput").ap()
    wut = dt_("wut", [HID, FFH], I32, kind="ExternalInput").ap()
    wdt = dt_("wdt", [FFH, HID], I32, kind="ExternalInput").ap()
    sgr = dt_("sgr", [1, FFH], F32, kind="ExternalInput").ap()
    sur = dt_("sur", [1, FFH], F32, kind="ExternalInput").ap()
    sdr = dt_("sdr", [1, HID], F32, kind="ExternalInput").ap()
    idbf_d = dt_("idbf_d", [128, 128], BF16, kind="ExternalInput").ap()
    idf_d = dt_("idf_d", [128, 128], F32, kind="ExternalInput").ap()
    trip_d = dt_("trip_d", [128, 128], F32, kind="ExternalInput").ap()
    trin_d = dt_("trin_d", [128, 128], F32, kind="ExternalInput").ap()
    maskt_d = dt_("maskt_d", [128, 128], BF16, kind="ExternalInput").ap()
    onesr_d = dt_("onesr_d", [1, 128], F32, kind="ExternalInput").ap()
    onesrb_d = dt_("onesrb_d", [1, 128], BF16, kind="ExternalInput").ap()
    onescb_d = dt_("onescb_d", [128, 1], BF16, kind="ExternalInput").ap()
    n128_d = dt_("n128_d", [1, 128], BF16, kind="ExternalInput").ap()
    x_sp = dt_("x_sp", [INTER, T], BF16, kind="Internal").ap()
    gu_sp = dt_("gu_sp", [FFH, T], BF16, kind="Internal").ap()
    sg_sp = dt_("sg_sp", [T, INTER], BF16, kind="Internal").ap()
    g_sp = dt_("g_sp", [T, HID], BF16, kind="Internal").ap()
    pp_sp = dt_("pp_sp", [T, HID], F32, kind="Internal").ap()
    ygt_sp = dt_("ygt_sp", [INTER, T], BF16, kind="Internal").ap()
    scrH = dt_("scrH", [NCH, NH * CS], BF16, kind="Internal").ap()
    scrL = dt_("scrL", [NCH, NH * CS], BF16, kind="Internal").ap()
    scrX = dt_("scrX", [16, 128], F32, kind="Internal").ap()
    outd = dt_("outd", [T, HID], F32, kind="ExternalOutput").ap()

    es = ExitStack()
    sbo = lambda name, shape, dt: es.enter_context(nc.sbuf_tensor(name, shape, dt))

    id_bf = sbo("id_bf", [128, 128], BF16)
    id_f = sbo("id_f", [128, 128], F32)
    triP = sbo("triP", [128, 128], F32)
    triN = sbo("triN", [128, 128], F32)
    maskT = sbo("maskT", [128, 128], BF16)
    ones_r = sbo("ones_r", [1, 128], F32)
    ones_rb = sbo("ones_rb", [1, 128], BF16)
    ones_cb = sbo("ones_cb", [128, 1], BF16)
    n128r = sbo("n128r", [1, 128], BF16)
    ones11 = sbo("ones11", [1, 1], F32)
    convw_s = sbo("convw_s", [128, 140], F32)
    convb_c = sbo("convb_c", [128, 35], F32)
    sg_c = sbo("sg_c", [128, NHT], F32)
    su_c = sbo("su_c", [128, NHT], F32)
    dtb_c = sbo("dtb_c", [64, 1], F32)
    A_c = sbo("A_c", [64, 1], F32)
    D_b = sbo("D_b", [128, 64], F32)
    rmask_c = sbo("rmask_c", [128, 1], F32)
    NWb = sbo("NWb", [128, INTER], F32)
    Bt = sbo("Bt", [128, T], BF16)
    Ctn = sbo("Ctn", [128, T], BF16)
    dts = sbo("dts", [64, T], F32)
    adts = sbo("adts", [64, T], F32)
    R = sbo("Rst", [128, INTER], F32)
    hist = sbo("hist", [128, 105], F32)

    P = Pr(nc, es)
    with nc.Block() as block:
        P.sem("d0", "d1", "d2", "d3", "d4", "g0", "g1", "w0", "w1", "w2", "w3", "tS", "vS", "sS")

        def barrier():
            for engname in ("sync", "gpsimd", "vector", "scalar", "tensor"):
                @getattr(block, engname)
                def _(e):
                    for s in ("d0", "d1", "d2", "d3", "d4", "g0", "g1", "w0", "w1", "w2", "w3", "tS", "vS", "sS"):
                        P.w(e, s)

        with nc.sbuf_tensor("r_cb", [1, 4352], F32, allow_name_mangling=True) as r_cb, \
             nc.sbuf_tensor("r_sg", [1, FFH], F32, allow_name_mangling=True) as r_sg, \
             nc.sbuf_tensor("r_su", [1, FFH], F32, allow_name_mangling=True) as r_su, \
             nc.sbuf_tensor("r_sm", [1, 193], F32, allow_name_mangling=True) as r_sm, \
             nc.sbuf_tensor("r_nw", [1, INTER], F32, allow_name_mangling=True) as r_nw, \
             nc.sbuf_tensor("t_c64", [64, 1], F32, allow_name_mangling=True) as t_c64, \
             nc.psum_tensor("ps_set", [128, 512], F32, allow_name_mangling=True) as ps_set:

            @block.sync
            def _(sync):
                for dst, src in [(id_bf, idbf_d), (id_f, idf_d), (triP, trip_d),
                                 (triN, trin_d), (maskT, maskt_d), (ones_r, onesr_d),
                                 (ones_rb, onesrb_d), (ones_cb, onescb_d), (n128r, n128_d),
                                 (r_cb, convbr), (r_sg, sgr), (r_su, sur), (r_nw, nwr)]:
                    P.inc(sync.dma_start(out=dst[:], in_=src), "d0", 16)
                P.inc(sync.dma_start(out=r_sm[:, 0:64], in_=dtbr), "d0", 16)
                P.inc(sync.dma_start(out=r_sm[:, 64:128], in_=alogr), "d0", 16)
                P.inc(sync.dma_start(out=r_sm[:, 128:192], in_=dvecr), "d0", 16)
                P.inc(sync.dma_start(out=r_sm[:, 192:193], in_=rmaskd), "d0", 16)
                for j in range(34):
                    P.inc(sync.dma_start(out=convw_s[:, j * 4:(j + 1) * 4],
                                         in_=convw[j * 128:(j + 1) * 128, :]), "d0", 16)
            dma_setup = P.c["d0"]

            @block.vector
            def _(v):
                P.w(v, "d0", dma_setup)
                v.memset(ones11[:], 1.0)
                v.memset(hist[:], 0.0)
                P.inc(v.memset(R[:], 0.0), "vS", 1)

            def rows_to_cols(row_ap, n, dst_fn):
                j = 0
                while j < n:
                    m = min(8, n - j)
                    @block.tensor
                    def _(t, j=j, m=m):
                        P.w(t, "vS")
                        for i in range(m):
                            P.inc(t.matmul(ps_set[:, j + i:j + i + 1],
                                           row_ap[0:1, (j + i) * 128:(j + i + 1) * 128],
                                           ones11[:], start=True, stop=True), "tS", 1)
                    @block.vector
                    def _(v, j=j, m=m):
                        P.w(v, "tS")
                        for i in range(m):
                            P.inc(v.tensor_copy(dst_fn(j + i), ps_set[:, j + i:j + i + 1]), "vS", 1)
                    j += m

            rows_to_cols(r_cb, 34, lambda j: convb_c[:, j:j + 1])
            rows_to_cols(r_sg, NHT, lambda j: sg_c[:, j:j + 1])
            rows_to_cols(r_su, NHT, lambda j: su_c[:, j:j + 1])

            @block.tensor
            def _(t):
                P.w(t, "vS")
                P.inc(t.matmul(ps_set[0:64, 0:1], r_sm[0:1, 0:64], ones11[:], start=True, stop=True), "tS", 1)
                P.inc(t.matmul(ps_set[0:64, 1:2], r_sm[0:1, 64:128], ones11[:], start=True, stop=True), "tS", 1)
                P.inc(t.matmul(ps_set[:, 64:128], ones_r[:], r_sm[0:1, 128:192], start=True, stop=True), "tS", 1)
                P.inc(t.matmul(ps_set[:, 128:129], ones_r[:], r_sm[0:1, 192:193], start=True, stop=True), "tS", 1)
            @block.vector
            def _(v):
                P.w(v, "tS")
                v.tensor_copy(dtb_c[:], ps_set[0:64, 0:1])
                v.tensor_copy(t_c64[:], ps_set[0:64, 1:2])
                v.tensor_copy(D_b[:], ps_set[:, 64:128])
                P.inc(v.tensor_copy(rmask_c[:], ps_set[:, 128:129]), "vS", 1)
            @block.scalar
            def _(s):
                P.w(s, "vS")
                P.inc(s.activation(A_c[:], t_c64[:], AF.Exp), "sS", 1)
            @block.vector
            def _(v):
                P.w(v, "sS")
                P.inc(v.tensor_scalar(A_c[:], A_c[:], -1.0, None, ALU.mult), "vS", 1)
            for jj in range(INTER // 512):
                @block.tensor
                def _(t, jj=jj):
                    P.w(t, "vS")
                    P.inc(t.matmul(ps_set[:, 0:512], ones_r[:],
                                   r_nw[0:1, jj * 512:(jj + 1) * 512], start=True, stop=True), "tS", 1)
                @block.vector
                def _(v, jj=jj):
                    P.w(v, "tS")
                    P.inc(v.tensor_copy(NWb[:, jj * 512:(jj + 1) * 512], ps_set[:, 0:512]), "vS", 1)

        def emit_ln(x_dram, collect_xsum, xnT, xsrb):
            with nc.sbuf_tensor("ln_xb", [128, 2 * HID], F32, allow_name_mangling=True) as xb, \
                 nc.sbuf_tensor("ln_xc", [128, 2 * HID], F32, allow_name_mangling=True) as xcb, \
                 nc.sbuf_tensor("ln_xn", [128, 2 * HID], BF16, allow_name_mangling=True) as xnb, \
                 nc.sbuf_tensor("ln_sm", [128, 8], F32, allow_name_mangling=True) as sm, \
                 nc.sbuf_tensor("ln_xs", [128, 16], F32, allow_name_mangling=True) as xsc, \
                 nc.sbuf_tensor("ln_xr", [16, 128], F32, allow_name_mangling=True) as xrow, \
                 nc.sbuf_tensor("ln_x1", [1, T], F32, allow_name_mangling=True) as xsr, \
                 nc.psum_tensor("ln_pt", [128, 2048], BF16, allow_name_mangling=True) as ps_t, \
                 nc.psum_tensor("ln_pf", [128, 128], F32, allow_name_mangling=True) as ps_f:
                barrier()
                free_x = [0, 0]
                free_xc = [0, 0]
                free_xn = [0, 0]
                free_ps = 0
                rdy = [0] * NTT
                for i in range(NTT):
                    sl = i % 2
                    @block.sync
                    def _(sync, i=i, sl=sl):
                        P.w(sync, "vS", free_x[sl])
                        rdy[i] = P.inc(sync.dma_start(out=xb[:, sl * HID:(sl + 1) * HID],
                                                      in_=x_dram[i * 128:(i + 1) * 128, :]), "d0" if sl == 0 else "d1", 16)
                    @block.vector
                    def _(v, i=i, sl=sl):
                        P.w(v, "d0" if sl == 0 else "d1", rdy[i])
                        xbs = xb[:, sl * HID:(sl + 1) * HID]
                        P.inc(v.reduce_sum(sm[:, 0:1], xbs, axis=AX), "vS", 1)
                        P.w(v, "vS")
                        P.inc(v.tensor_scalar(sm[:, 1:2], sm[:, 0:1], -1.0 / HID, None, ALU.mult), "vS", 1)
                        P.w(v, "vS")
                        P.w(v, "vS", free_xc[sl])
                        P.inc(v.tensor_scalar(xcb[:, sl * HID:(sl + 1) * HID], xbs,
                                              sm[:, 1:2], None, ALU.add), "vS", 1)
                    @block.scalar
                    def _(s, sl=sl):
                        P.w(s, "vS")
                        P.inc(s.activation(xb[:, sl * HID:(sl + 1) * HID],
                                           xcb[:, sl * HID:(sl + 1) * HID], AF.Square), "sS", 1)
                    @block.vector
                    def _(v, sl=sl):
                        P.w(v, "sS")
                        P.inc(v.reduce_sum(sm[:, 2:3], xb[:, sl * HID:(sl + 1) * HID], axis=AX), "vS", 1)
                        P.w(v, "vS")
                        free_x[sl] = P.inc(v.tensor_scalar(sm[:, 3:4], sm[:, 2:3],
                                                           1.0 / HID, EPS, ALU.mult, ALU.add), "vS", 1)
                    @block.scalar
                    def _(s):
                        P.w(s, "vS")
                        P.inc(s.activation(sm[:, 4:5], sm[:, 3:4], AF.Ln), "sS", 1)
                        P.w(s, "sS")
                        P.inc(s.activation(sm[:, 5:6], sm[:, 4:5], AF.Exp, scale=-0.5), "sS", 1)
                    @block.vector
                    def _(v, i=i, sl=sl):
                        P.w(v, "sS")
                        P.w(v, "tS", free_xn[sl])
                        free_xc[sl] = P.inc(v.tensor_scalar(xnb[:, sl * HID:(sl + 1) * HID],
                                                            xcb[:, sl * HID:(sl + 1) * HID],
                                                            sm[:, 5:6], None, ALU.mult), "vS", 1)
                        if collect_xsum:
                            P.w(v, "vS")
                            P.inc(v.reduce_sum(xsc[:, i:i + 1], xnb[:, sl * HID:(sl + 1) * HID],
                                               axis=AX), "vS", 1)
                    @block.tensor
                    def _(t, sl=sl):
                        P.w(t, "vS")
                        P.w(t, "sS", free_ps)
                        for j in range(16):
                            ins = t.transpose(ps_t[:, j * 128:(j + 1) * 128],
                                              xnb[:, sl * HID + j * 128: sl * HID + (j + 1) * 128], id_bf[:])
                        P.inc(ins, "tS", 1)
                        free_xn[sl] = P.c["tS"]
                    @block.scalar
                    def _(s, i=i):
                        P.w(s, "tS")
                        xv = xnT.rearrange("p (k t) -> p k t", t=T)
                        P.inc(s.copy(xv[:, :, i * 128:(i + 1) * 128],
                                     ps_t.rearrange("p (k t) -> p k t", t=128)), "sS", 1)
                    free_ps = P.c["sS"]
                if collect_xsum:
                    @block.tensor
                    def _(t):
                        P.w(t, "vS")
                        P.inc(t.transpose(ps_f[0:16, :], xsc[:], id_f[:]), "tS", 1)
                    @block.vector
                    def _(v):
                        P.w(v, "tS")
                        P.inc(v.tensor_copy(xrow[:], ps_f[0:16, :]), "vS", 1)
                    @block.scalar
                    def _(s):
                        P.w(s, "vS")
                        P.inc(s.dma_start(out=scrX, in_=xrow[:]), "d2", 16)
                        P.w(s, "d2")
                        P.inc(s.dma_start(out=xsr[:], in_=scrX.rearrange("i p -> (i p)")[None, :]), "d2", 16)
                    @block.vector
                    def _(v):
                        P.w(v, "d2")
                        P.inc(v.tensor_copy(xsrb[:], xsr[:]), "vS", 1)

        def emit_inprojT(is_ctx, xnT):
            with nc.sbuf_tensor("ipt_w", [128, 2 * 2048], BF16, allow_name_mangling=True) as wst, \
                 nc.sbuf_tensor("ipt_xr", [128, 2 * 2051], F32, allow_name_mangling=True) as xbcr, \
                 nc.sbuf_tensor("ipt_cv", [128, 2048], F32, allow_name_mangling=True) as cvt, \
                 nc.sbuf_tensor("ipt_cs", [128, 2 * 2048], BF16, allow_name_mangling=True) as convs, \
                 nc.sbuf_tensor("ipt_t1", [64, 2048], F32, allow_name_mangling=True) as tA, \
                 nc.sbuf_tensor("ipt_t2", [64, 2048], F32, allow_name_mangling=True) as tB, \
                 nc.psum_tensor("ipt_ps", [128, 2 * 2048], F32, allow_name_mangling=True) as ps_ip:
                barrier()
                stat_rdy = [0] * 35
                free_w = [0, 0]
                free_ps = [0, 0]
                free_cv = 0
                free_cs = [0, 0]
                mm_done = [0] * 35
                ev_done = [0] * 35
                xv = xnT.rearrange("p (k t) -> p k t", t=T)
                for j in range(36):
                    if j < 35:
                        cw = 128 if j < 34 else 64
                        @block.gpsimd
                        def _(g, j=j, cw=cw):
                            P.w(g, "tS", free_w[j % 2])
                            last = 0
                            for k in range(NKD):
                                last = P.inc(g.dma_start(
                                    out=wst[:, (j % 2) * 2048 + k * 128:(j % 2) * 2048 + k * 128 + cw],
                                    in_=wip[k * 128:(k + 1) * 128, INTER + j * 128: INTER + j * 128 + cw]), "g0" if j % 2 == 0 else "g1", 16)
                            stat_rdy[j] = last
                    if j == 0:
                        continue
                    jj = j - 1
                    cw = 128 if jj < 34 else 64
                    sl = jj % 2
                    @block.tensor
                    def _(t, jj=jj, cw=cw, sl=sl):
                        P.w(t, "g0" if jj % 2 == 0 else "g1", stat_rdy[jj])
                        P.w(t, "sS", free_ps[sl])
                        for k in range(NKD):
                            for n in range(4):
                                ins = t.matmul(ps_ip[0:cw, sl * 2048 + n * 512: sl * 2048 + (n + 1) * 512],
                                               wst[:, sl * 2048 + k * 128: sl * 2048 + k * 128 + cw],
                                               xv[:, k, n * 512:(n + 1) * 512],
                                               start=(k == 0), stop=(k == NKD - 1))
                        mm_done[jj] = P.inc(ins, "tS", 1)
                        free_w[sl] = P.c["tS"]
                    @block.scalar
                    def _(s, jj=jj, cw=cw, sl=sl):
                        P.w(s, "tS", mm_done[jj])
                        free_ps[sl] = P.inc(s.copy(xbcr[0:cw, sl * 2051 + 3: sl * 2051 + 2051],
                                                   ps_ip[0:cw, sl * 2048: sl * 2048 + 2048]), "sS", 1)
                        ev_done[jj] = P.c["sS"]
                    if jj < 34:
                        @block.vector
                        def _(v, jj=jj, sl=sl):
                            P.w(v, "sS", ev_done[jj])
                            P.inc(v.tensor_copy(xbcr[:, sl * 2051: sl * 2051 + 3], hist[:, jj * 3:(jj + 1) * 3]), "vS", 1)
                            P.w(v, "vS")
                            if is_ctx:
                                P.inc(v.tensor_copy(hist[:, jj * 3:(jj + 1) * 3],
                                              xbcr[:, sl * 2051 + 2048: sl * 2051 + 2051]), "vS", 1)
                                P.w(v, "vS")
                            P.w(v, "sS", free_cv)
                            P.inc(v.tensor_scalar(cvt[:], xbcr[:, sl * 2051: sl * 2051 + 2048],
                                            convw_s[:, jj * 4:jj * 4 + 1], None, ALU.mult), "vS", 1)
                            P.w(v, "vS")
                            for kk in range(1, 4):
                                P.inc(v.scalar_tensor_tensor(cvt[:], xbcr[:, sl * 2051 + kk: sl * 2051 + kk + 2048],
                                                       convw_s[:, jj * 4 + kk:jj * 4 + kk + 1], cvt[:],
                                                       ALU.mult, ALU.add), "vS", 1)
                                P.w(v, "vS")
                        @block.scalar
                        def _(s, jj=jj, sl=sl):
                            P.w(s, "vS")
                            if jj == 32:
                                dst = Bt[:]
                            elif jj == 33:
                                dst = Ctn[:]
                            else:
                                P.w(s, "w0" if sl == 0 else "w1", free_cs[sl])
                                dst = convs[:, sl * 2048:(sl + 1) * 2048]
                            free_cv = P.inc(s.activation(dst, cvt[:], AF.Silu,
                                                         bias=convb_c[:, jj:jj + 1]), "sS", 1)
                            if jj < 32:
                                P.w(s, "sS")
                                free_cs[sl] = P.inc(s.dma_start(
                                    out=x_sp[jj * 128:(jj + 1) * 128, :],
                                    in_=convs[:, sl * 2048:(sl + 1) * 2048]), "w0" if sl == 0 else "w1", 16)
                    else:
                        @block.scalar
                        def _(s, sl=sl):
                            P.w(s, "sS", ev_done[34])
                            src = xbcr[0:64, sl * 2051 + 3: sl * 2051 + 2051]
                            P.inc(s.activation(tA[:], src, AF.Abs, bias=dtb_c[:]), "sS", 1)
                            P.w(s, "sS")
                            P.inc(s.activation(tB[:], tA[:], AF.Exp, scale=-1.0), "sS", 1)
                            P.w(s, "sS")
                            P.inc(s.activation(tA[:], tB[:], AF.Ln, bias=1.0), "sS", 1)
                            P.w(s, "sS")
                            P.inc(s.activation(tB[:], src, AF.Relu, bias=dtb_c[:]), "sS", 1)
                        @block.vector
                        def _(v):
                            P.w(v, "sS")
                            P.inc(v.tensor_tensor(dts[:], tA[:], tB[:], ALU.add), "vS", 1)
                            P.w(v, "vS")
                            P.inc(v.tensor_scalar(adts[:], dts[:], A_c[:], None, ALU.mult), "vS", 1)

        def emit_prepass(dtT_all, dsT_all, etb_all, sdo_all, negA_all, need_flats):
            with nc.sbuf_tensor("pp_aT", [128, 64], F32, allow_name_mangling=True) as adtT_s, \
                 nc.sbuf_tensor("pp_tc", [64, 1], F32, allow_name_mangling=True) as tc, \
                 nc.sbuf_tensor("pp_tr", [1, 64], F32, allow_name_mangling=True) as totr, \
                 nc.sbuf_tensor("pp_acf", [64, 128], F32, allow_name_mangling=True) as acf, \
                 nc.sbuf_tensor("pp_ach", [64, 128], BF16, allow_name_mangling=True) as ach, \
                 nc.sbuf_tensor("pp_acg", [64, 128], F32, allow_name_mangling=True) as achf, \
                 nc.sbuf_tensor("pp_acl", [64, 128], BF16, allow_name_mangling=True) as acl, \
                 nc.psum_tensor("pp_ps", [128, 512], F32, allow_name_mangling=True) as pp:
                barrier()
                for c in range(NCH):
                    @block.tensor
                    def _(t, c=c):
                        P.w(t, "vS"); P.w(t, "sS")
                        t.transpose(pp[:, 0:64], dts[0:64, c * 128:(c + 1) * 128], id_f[0:64, 0:64])
                        P.inc(t.transpose(pp[:, 64:128], adts[0:64, c * 128:(c + 1) * 128],
                                          id_f[0:64, 0:64]), "tS", 1)
                    @block.vector
                    def _(v, c=c):
                        P.w(v, "tS")
                        v.tensor_copy(dtT_all[:, c * 64:(c + 1) * 64], pp[:, 0:64])
                        v.tensor_copy(adtT_s[:], pp[:, 64:128])
                        P.inc(v.reduce_sum(tc[:], adts[0:64, c * 128:(c + 1) * 128], axis=AX), "vS", 1)
                    @block.tensor
                    def _(t):
                        P.w(t, "vS")
                        P.inc(t.matmul(pp[0:1, 128:192], tc[:], id_f[0:64, 0:64], start=True, stop=True), "tS", 1)
                    @block.vector
                    def _(v):
                        P.w(v, "tS")
                        P.inc(v.tensor_copy(totr[:], pp[0:1, 128:192]), "vS", 1)
                    @block.tensor
                    def _(t):
                        P.w(t, "vS")
                        t.matmul(pp[:, 192:256], triN[:], adtT_s[:], start=True, stop=False)
                        t.matmul(pp[:, 192:256], ones_r[:], totr[:], start=False, stop=True)
                        t.matmul(pp[:, 256:320], ones_r[:], totr[:], start=True, stop=True)
                        t.matmul(pp[:, 320:384], triN[:], adtT_s[:], start=True, stop=True)
                        P.inc(t.matmul(pp[0:64, 384:512], adtT_s[:], triP[:], start=True, stop=True), "tS", 1)
                    @block.scalar
                    def _(s, c=c):
                        P.w(s, "tS")
                        s.activation(dsT_all[:, c * 64:(c + 1) * 64], pp[:, 192:256], AF.Exp)
                        P.inc(s.activation(etb_all[:, c * 64:(c + 1) * 64], pp[:, 256:320], AF.Exp), "sS", 1)
                    @block.vector
                    def _(v, c=c):
                        P.w(v, "tS")
                        v.tensor_copy(negA_all[:, c * 64:(c + 1) * 64], pp[:, 320:384])
                        P.inc(v.tensor_copy(acf[:], pp[0:64, 384:512]), "vS", 1)
                    @block.scalar
                    def _(s, c=c):
                        P.w(s, "vS")
                        P.inc(s.activation(sdo_all[:, c * 64:(c + 1) * 64],
                                           negA_all[:, c * 64:(c + 1) * 64], AF.Exp, scale=-1.0), "sS", 1)
                    if need_flats:
                        @block.vector
                        def _(v):
                            P.w(v, "vS")
                            P.w(v, "w0")
                            P.inc(v.tensor_copy(ach[:], acf[:]), "vS", 1)
                        @block.scalar
                        def _(s):
                            P.w(s, "vS")
                            P.inc(s.copy(achf[:], ach[:]), "sS", 1)
                        @block.vector
                        def _(v):
                            P.w(v, "sS")
                            P.inc(v.tensor_tensor(acl[:], acf[:], achf[:], ALU.subtract), "vS", 1)
                        @block.scalar
                        def _(s, c=c):
                            P.w(s, "vS")
                            P.inc(s.dma_start(out=scrH[c:c + 1, :].rearrange("a (h l) -> (a h) l", l=128),
                                              in_=ach[:]), "w0", 16)
                            P.inc(s.dma_start(out=scrL[c:c + 1, :].rearrange("a (h l) -> (a h) l", l=128),
                                              in_=acl[:]), "w0", 16)

        def emit_ssd(full, dtT_all, dsT_all, etb_all, sdo_all, negA_all):
            with ExitStack() as ss:
                sbl = lambda name, shape, dt: ss.enter_context(nc.sbuf_tensor(name, shape, dt, allow_name_mangling=True))
                psl = lambda name, shape, dt: ss.enter_context(nc.psum_tensor(name, shape, dt, allow_name_mangling=True))
                xch = sbl("sd_xch", [128, 2 * INTER], BF16)
                xchT = sbl("sd_xchT", [128, INTER], BF16)
                xd = sbl("sd_xd", [128, INTER], BF16)
                Bsb = sbl("sd_Bsb", [128, 128], BF16)
                t512 = sbl("sd_t5", [128, 512], F32)
                pS = psl("sd_pS", [128, 512], F32)
                pxt = psl("sd_pxt", [128, 1024], BF16)
                barrier()
                if full:
                    xrd = sbl("sd_xrd", [128, INTER], BF16)
                    cb_b = sbl("sd_cb", [128, 128], BF16)
                    Eb = sbl("sd_E", [128, 1024], BF16)
                    Mb = sbl("sd_M", [128, 1024], BF16)
                    flhl = sbl("sd_f", [1, 2 * 2048], BF16)
                    ybuf = sbl("sd_y", [128, INTER], F32)
                    sgc = sbl("sd_sg", [128, INTER], BF16)
                    sqf = sbl("sd_sq", [128, INTER], F32)
                    ygb = sbl("sd_yg", [128, INTER], BF16)
                    ygT = sbl("sd_ygT", [128, INTER], BF16)
                    ssm = sbl("sd_sm", [128, 8], F32)
                    Rbf = sbl("sd_Rb", [128, INTER], BF16)
                    psE = psl("sd_pE", [128, 2048], F32)
                    pyd = psl("sd_pyd", [128, 512], F32)
                    pyo = psl("sd_pyo", [128, 512], F32)
                xch_rdy = [0] * NCH
                sg_rdy = [0] * NCH
                xch_free = [0, 0]
                sg_free = 0
                pass
                NG = NCH * 8
                flat_rdy = [0] * (NG + 2)
                flat_free = [0, 0]
                rbf_rdy = [0]
                if full:
                    @block.scalar
                    def _(s):
                        P.w(s, "vS")
                        rbf_rdy[0] = P.inc(s.copy(Rbf[:], R[:]), "sS", 1)
                    @block.sync
                    def _(sync):
                        pass
                        for G in range(2):
                            par = G % 2
                            sem = "d3" if par == 0 else "d4"
                            cG, gG = G // 8, G % 8
                            P.inc(sync.dma_start(out=flhl[0:1, par * 2048: par * 2048 + 1024],
                                           in_=scrH[cG:cG + 1, gG * 1024:(gG + 1) * 1024]), sem, 16)
                            flat_rdy[G] = P.inc(sync.dma_start(
                                out=flhl[0:1, par * 2048 + 1024: par * 2048 + 2048],
                                in_=scrL[cG:cG + 1, gG * 1024:(gG + 1) * 1024]), sem, 16)
                for c in range(NCH + 1):
                    if c < NCH:
                        sl = c % 2
                        @block.sync
                        def _(sync, c=c, sl=sl):
                            if c == 0:
                                pass
                            P.w(sync, "vS", xch_free[sl])
                            xvv = x_sp.rearrange("(i p) j -> p i j", p=128)
                            last = 0
                            for q in range(4):
                                last = P.inc(sync.dma_start(
                                    out=xch.rearrange("p (s i j) -> p s i j", s=2, j=128)[:, sl, q * 8:(q + 1) * 8, :],
                                    in_=xvv[:, q * 8:(q + 1) * 8, c * 128:(c + 1) * 128]), "d0" if sl == 0 else "d1", 16)
                            xch_rdy[c] = last
                    if c == 0:
                        continue
                    cc = c - 1
                    if full:
                        @block.sync
                        def _(sync, cc=cc):
                            P.w(sync, "vS", sg_free)
                            sg_rdy[cc] = P.inc(sync.dma_start(out=sgc[:],
                                                              in_=sg_sp[cc * 128:(cc + 1) * 128, :]), "d2", 16)
                    sl = cc % 2
                    skip_state = full and (cc == NCH - 1)
                    for q in range(4):
                        @block.tensor
                        def _(t, cc=cc, sl=sl, q=q):
                            P.w(t, "d0" if sl == 0 else "d1", xch_rdy[cc])
                            P.w(t, "vS")
                            for j8 in range(8):
                                ins = t.transpose(pxt[:, j8 * 128:(j8 + 1) * 128],
                                                  xch[:, sl * INTER + (q * 8 + j8) * 128:
                                                      sl * INTER + (q * 8 + j8 + 1) * 128], id_bf[:])
                            P.inc(ins, "tS", 1)
                        @block.vector
                        def _(v, q=q):
                            P.w(v, "tS")
                            P.inc(v.tensor_copy(xchT[:, q * 1024:(q + 1) * 1024], pxt[:]), "vS", 1)
                    @block.vector
                    def _(v, cc=cc, sl=sl, skip_state=skip_state):
                        P.w(v, "vS")
                        x3 = xchT.rearrange("p (h q) -> p h q", q=64)
                        P.inc(v.tensor_tensor(xd.rearrange("p (h q) -> p h q", q=64), x3,
                                        dtT_all[:, cc * 64:(cc + 1) * 64][:, :, None].broadcast_to([128, 64, 64]),
                                        ALU.mult), "vS", 1)
                        P.w(v, "vS")
                        if full:
                            P.inc(v.tensor_tensor(xrd.rearrange("p (h q) -> p h q", q=64), x3,
                                            D_b[:, :, None].broadcast_to([128, 64, 64]), ALU.mult), "vS", 1)
                            P.w(v, "vS")
                        if not skip_state:
                            P.inc(v.tensor_tensor(
                                xch.rearrange("p (s f) -> p s f", s=2)[:, sl, :].rearrange("p (h q) -> p h q", q=64),
                                xd.rearrange("p (h q) -> p h q", q=64),
                                dsT_all[:, cc * 64:(cc + 1) * 64][:, :, None].broadcast_to([128, 64, 64]),
                                ALU.mult), "vS", 1)
                    xdd_ap = xch[:, sl * INTER:(sl + 1) * INTER]
                    @block.tensor
                    def _(t, cc=cc):
                        P.w(t, "vS")
                        P.inc(t.transpose(pxt[:, 0:128], Bt[:, cc * 128:(cc + 1) * 128], id_bf[:]), "tS", 1)
                    @block.vector
                    def _(v):
                        P.w(v, "tS")
                        P.inc(v.tensor_copy(Bsb[:], pxt[:, 0:128]), "vS", 1)
                    if full:
                        @block.tensor
                        def _(t, cc=cc):
                            P.w(t, "vS"); P.w(t, "sS")
                            P.inc(t.matmul(psE[:, 1024:1152], Bt[:, cc * 128:(cc + 1) * 128],
                                           Ctn[:, cc * 128:(cc + 1) * 128], start=True, stop=True), "tS", 1)
                        @block.scalar
                        def _(s):
                            P.w(s, "tS")
                            P.inc(s.copy(cb_b[:], psE[:, 1024:1152]), "sS", 1)
                    for g in range(8):
                        G = cc * 8 + g
                        par = g % 2
                        if full:
                            @block.tensor
                            def _(t, cc=cc, sl=sl, g=g, par=par, G=G):
                                P.w(t, "d3" if par == 0 else "d4", flat_rdy[G])
                                P.w(t, "sS")
                                for i in range(8):
                                    h = g * 8 + i
                                    dst = psE[:, par * 1024 + i * 128: par * 1024 + (i + 1) * 128]
                                    t.matmul(dst, id_bf[:], maskT[:], start=(i % 4 == 0), stop=False)
                                    t.matmul(dst, ones_rb[:],
                                             flhl[0:1, par * 2048 + i * 128: par * 2048 + (i + 1) * 128],
                                             start=False, stop=False)
                                    ins = t.matmul(dst, ones_rb[:],
                                                   flhl[0:1, par * 2048 + 1024 + i * 128: par * 2048 + 1024 + (i + 1) * 128],
                                                   start=False, stop=(i % 4 == 3))
                                flat_free[par] = P.inc(ins, "tS", 1)
                            if G + 2 < NG:
                                @block.sync
                                def _(sync, G=G, par=par):
                                    P.w(sync, "tS", flat_free[par])
                                    G2 = G + 2
                                    cG, gG = G2 // 8, G2 % 8
                                    P.inc(sync.dma_start(out=flhl[0:1, par * 2048: par * 2048 + 1024],
                                                   in_=scrH[cG:cG + 1, gG * 1024:(gG + 1) * 1024]), "d3" if par == 0 else "d4", 16)
                                    flat_rdy[G2] = P.inc(sync.dma_start(
                                        out=flhl[0:1, par * 2048 + 1024: par * 2048 + 2048],
                                        in_=scrL[cG:cG + 1, gG * 1024:(gG + 1) * 1024]), "d3" if par == 0 else "d4", 16)
                            @block.scalar
                            def _(s, cc=cc, g=g, par=par):
                                P.w(s, "tS")
                                P.w(s, "vS")
                                for i in range(8):
                                    h = g * 8 + i
                                    ins = s.activation(Eb[:, i * 128:(i + 1) * 128],
                                                       psE[:, par * 1024 + i * 128: par * 1024 + (i + 1) * 128],
                                                       AF.Exp, bias=negA_all[:, cc * 64 + h: cc * 64 + h + 1])
                                P.inc(ins, "sS", 1)
                            @block.vector
                            def _(v):
                                P.w(v, "sS")
                                P.w(v, "tS")
                                P.inc(v.tensor_tensor(Mb.rearrange("p (a f) -> p a f", f=128),
                                                      Eb.rearrange("p (a f) -> p a f", f=128),
                                                      cb_b[:, None, :].broadcast_to([128, 8, 128]),
                                                      ALU.mult), "vS", 1)
                        @block.tensor
                        def _(t, cc=cc, g=g, skip_state=skip_state):
                            P.w(t, "vS")
                            if full:
                                P.w(t, "sS", rbf_rdy[0])
                            ins = None
                            for i in range(8):
                                h = g * 8 + i
                                if full:
                                    t.matmul(pyd[:, i * 64:(i + 1) * 64], Mb[:, i * 128:(i + 1) * 128],
                                             xd[:, h * 64:(h + 1) * 64], start=(i == 0), stop=False)
                                    t.matmul(pyd[:, i * 64:(i + 1) * 64], id_bf[:],
                                             xrd[:, h * 64:(h + 1) * 64], start=False, stop=(i == 7))
                                    ins = t.matmul(pyo[:, i * 64:(i + 1) * 64], Ctn[:, cc * 128:(cc + 1) * 128],
                                                   Rbf[:, h * 64:(h + 1) * 64], start=(i == 0), stop=(i == 7))
                                if not skip_state:
                                    ins = t.matmul(pS[:, i * 64:(i + 1) * 64], Bsb[:],
                                                   xdd_ap[:, h * 64:(h + 1) * 64], start=(i == 0), stop=(i == 7))
                            P.inc(ins, "tS", 1)
                        @block.vector
                        def _(v, cc=cc, g=g, skip_state=skip_state):
                            P.w(v, "tS")
                            if full:
                                P.inc(v.tensor_tensor(t512.rearrange("p (h q) -> p h q", q=64),
                                                pyo.rearrange("p (h q) -> p h q", q=64),
                                                sdo_all[:, cc * 64 + g * 8: cc * 64 + (g + 1) * 8][:, :, None].broadcast_to([128, 8, 64]),
                                                ALU.mult), "vS", 1)
                                P.w(v, "vS")
                                P.inc(v.tensor_tensor(ybuf[:, g * 512:(g + 1) * 512], t512[:], pyd[:], ALU.add), "vS", 1)
                                P.w(v, "vS")
                            if not skip_state:
                                P.inc(v.tensor_tensor(t512.rearrange("p (h q) -> p h q", q=64),
                                                R[:, g * 512:(g + 1) * 512].rearrange("p (h q) -> p h q", q=64),
                                                etb_all[:, cc * 64 + g * 8: cc * 64 + (g + 1) * 8][:, :, None].broadcast_to([128, 8, 64]),
                                                ALU.mult), "vS", 1)
                                P.w(v, "vS")
                                P.inc(v.tensor_tensor(R[:, g * 512:(g + 1) * 512], t512[:], pS[:], ALU.add), "vS", 1)
                    xch_free[sl] = P.c["vS"]
                    if not full:
                        continue
                    if not skip_state:
                        @block.scalar
                        def _(s):
                            P.w(s, "vS")
                            rbf_rdy[0] = P.inc(s.copy(Rbf[:], R[:]), "sS", 1)
                    @block.scalar
                    def _(s):
                        P.w(s, "vS")
                        P.inc(s.activation(sqf[:], ybuf[:], AF.Square), "sS", 1)
                    @block.vector
                    def _(v):
                        P.w(v, "sS")
                        P.inc(v.reduce_sum(ssm[:, 0:1], sqf[:], axis=AX), "vS", 1)
                        P.w(v, "vS")
                        P.inc(v.tensor_scalar(ssm[:, 1:2], ssm[:, 0:1], 1.0 / INTER, EPS,
                                              ALU.mult, ALU.add), "vS", 1)
                    @block.scalar
                    def _(s):
                        P.w(s, "vS")
                        P.inc(s.activation(ssm[:, 2:3], ssm[:, 1:2], AF.Ln), "sS", 1)
                        P.w(s, "sS")
                        P.inc(s.activation(ssm[:, 3:4], ssm[:, 2:3], AF.Exp, scale=-0.5), "sS", 1)
                    @block.vector
                    def _(v, cc=cc):
                        P.w(v, "sS")
                        P.w(v, "d2", sg_rdy[cc])
                        P.inc(v.tensor_scalar(sqf[:], ybuf[:], ssm[:, 3:4], None, ALU.mult), "vS", 1)
                        P.w(v, "vS")
                        P.inc(v.tensor_tensor(ygb[:], sqf[:], sgc[:], ALU.mult), "vS", 1)
                    sg_free = P.c["vS"]
                    for q in range(4):
                        @block.tensor
                        def _(t, q=q):
                            P.w(t, "vS")
                            for j8 in range(8):
                                ins = t.transpose(pxt[:, j8 * 128:(j8 + 1) * 128],
                                                  ygb[:, (q * 8 + j8) * 128:(q * 8 + j8 + 1) * 128], id_bf[:])
                            P.inc(ins, "tS", 1)
                        @block.vector
                        def _(v, q=q):
                            P.w(v, "tS")
                            P.w(v, "w0")
                            P.inc(v.tensor_copy(ygT[:, q * 1024:(q + 1) * 1024], pxt[:]), "vS", 1)
                    @block.scalar
                    def _(s, cc=cc):
                        P.w(s, "vS")
                        yvv = ygt_sp.rearrange("(i p) j -> p i j", p=128)
                        for q in range(4):
                            P.inc(s.dma_start(out=yvv[:, q * 8:(q + 1) * 8, cc * 128:(cc + 1) * 128],
                                              in_=ygT.rearrange("p (i j) -> p i j", j=128)[:, q * 8:(q + 1) * 8, :]),
                                  "w0", 16)
                if not full:
                    @block.vector
                    def _(v):
                        P.w(v, "vS")
                        P.inc(v.tensor_scalar(R[:], R[:], rmask_c[:], None, ALU.mult), "vS", 1)

        def emit_p2(xnT):
            with nc.sbuf_tensor("p2_w", [128, NKD * 1024], BF16, allow_name_mangling=True) as wq, \
                 nc.sbuf_tensor("p2_t", [128, 1024], F32, allow_name_mangling=True) as tmpf, \
                 nc.sbuf_tensor("p2_s", [128, 2 * 1024], BF16, allow_name_mangling=True) as sgt, \
                 nc.psum_tensor("p2_ps", [128, 2 * 1024], F32, allow_name_mangling=True) as ps2:
                barrier()
                free_w = 0
                w_rdy = [0] * 4
                free_sg = [0, 0]
                free_ps = [0, 0]
                free_tmp = 0
                xv = xnT.rearrange("p (k t) -> p k t", t=T)
                wv2 = wip.rearrange("(k p) c -> p k c", p=128)
                for q in range(4):
                    @block.gpsimd
                    def _(g, q=q):
                        P.w(g, "tS", free_w)
                        last = 0
                        for piece in range(4):
                            last = P.inc(g.dma_start(
                                out=wq.rearrange("p (k c) -> p k c", c=1024)[:, piece * 4:(piece + 1) * 4, :],
                                in_=wv2[:, piece * 4:(piece + 1) * 4, q * 1024:(q + 1) * 1024]), "g0", 16)
                        w_rdy[q] = last
                    for t_i in range(NTT):
                        sl = t_i % 2
                        @block.tensor
                        def _(t, q=q, t_i=t_i, sl=sl):
                            P.w(t, "g0", w_rdy[q])
                            P.w(t, "sS", free_ps[sl])
                            for k in range(NKD):
                                for n in range(2):
                                    ins = t.matmul(ps2[:, sl * 1024 + n * 512: sl * 1024 + (n + 1) * 512],
                                                   xv[:, k, t_i * 128:(t_i + 1) * 128],
                                                   wq.rearrange("p (k c) -> p k c", c=1024)[:, k, n * 512:(n + 1) * 512],
                                                   start=(k == 0), stop=(k == NKD - 1))
                            P.inc(ins, "tS", 1)
                        if t_i == NTT - 1:
                            free_w = P.c["tS"]
                        @block.scalar
                        def _(s, sl=sl):
                            P.w(s, "tS")
                            P.w(s, "vS", free_tmp)
                            free_ps[sl] = P.inc(s.activation(tmpf[:], ps2[:, sl * 1024:(sl + 1) * 1024],
                                                             AF.Silu), "sS", 1)
                        @block.vector
                        def _(v, q=q, sl=sl):
                            P.w(v, "sS")
                            P.w(v, "w0" if sl == 0 else "w1", free_sg[sl])
                            P.inc(v.tensor_tensor(sgt[:, sl * 1024:(sl + 1) * 1024], tmpf[:],
                                                  NWb[:, q * 1024:(q + 1) * 1024], ALU.mult), "vS", 1)
                        free_tmp = P.c["vS"]
                        @block.scalar
                        def _(s, q=q, t_i=t_i, sl=sl):
                            P.w(s, "vS")
                            free_sg[sl] = P.inc(s.dma_start(
                                out=sg_sp[t_i * 128:(t_i + 1) * 128, q * 1024:(q + 1) * 1024],
                                in_=sgt[:, sl * 1024:(sl + 1) * 1024]), "w0" if sl == 0 else "w1", 16)

        def emit_p4(xnT, xsrb):
            with nc.sbuf_tensor("p4_wg", [128, 2 * 2048], BF16, allow_name_mangling=True) as wg, \
                 nc.sbuf_tensor("p4_wu", [128, 2 * 2048], BF16, allow_name_mangling=True) as wu, \
                 nc.sbuf_tensor("p4_g", [128, HID], F32, allow_name_mangling=True) as gsf, \
                 nc.sbuf_tensor("p4_g2", [128, HID], F32, allow_name_mangling=True) as gs2, \
                 nc.sbuf_tensor("p4_gu", [128, 2 * HID], BF16, allow_name_mangling=True) as gub, \
                 nc.psum_tensor("p4_ps", [128, 2 * 2048], F32, allow_name_mangling=True) as ps4:
                barrier()
                w_rdy = [0] * NHT
                free_w = [0, 0]
                free_gu = [0, 0]
                mmg = [0] * NHT
                mmu = [0] * NHT
                xv = xnT.rearrange("p (k t) -> p k t", t=T)
                for ht in range(NHT + 1):
                    if ht < NHT:
                        @block.gpsimd
                        def _(g, ht=ht):
                            P.w(g, "tS", free_w[ht % 2])
                            last = 0
                            for k in range(NKD):
                                last = P.inc(g.dma_start(
                                    out=wg[:, (ht % 2) * 2048 + k * 128:(ht % 2) * 2048 + (k + 1) * 128],
                                    in_=wgt[k * 128:(k + 1) * 128, ht * 128:(ht + 1) * 128]), "g0" if ht % 2 == 0 else "g1", 16)
                                last = P.inc(g.dma_start(
                                    out=wu[:, (ht % 2) * 2048 + k * 128:(ht % 2) * 2048 + (k + 1) * 128],
                                    in_=wut[k * 128:(k + 1) * 128, ht * 128:(ht + 1) * 128]), "g0" if ht % 2 == 0 else "g1", 16)
                            w_rdy[ht] = last
                    if ht == 0:
                        continue
                    hh = ht - 1
                    sl = hh % 2
                    @block.tensor
                    def _(t, hh=hh, sl=sl):
                        P.w(t, "g0" if hh % 2 == 0 else "g1", w_rdy[hh])
                        P.w(t, "vS")
                        for (wbuf, half) in [(wg, 0), (wu, 1)]:
                            for k in range(NKD):
                                for n in range(4):
                                    t.matmul(ps4[:, half * 2048 + n * 512: half * 2048 + (n + 1) * 512],
                                             wbuf[:, sl * 2048 + k * 128: sl * 2048 + (k + 1) * 128],
                                             xv[:, k, n * 512:(n + 1) * 512],
                                             start=(k == 0), stop=False)
                            for n in range(4):
                                ins = t.matmul(ps4[:, half * 2048 + n * 512: half * 2048 + (n + 1) * 512],
                                               n128r[:], xsrb[0:1, n * 512:(n + 1) * 512],
                                               start=False, stop=True)
                            if half == 0:
                                mmg[hh] = P.inc(ins, "tS", 1)
                            else:
                                mmu[hh] = P.inc(ins, "tS", 1)
                        free_w[sl] = P.c["tS"]
                    @block.vector
                    def _(v, hh=hh):
                        P.w(v, "tS", mmg[hh])
                        P.inc(v.tensor_scalar(gsf[:], ps4[:, 0:2048], sg_c[:, hh:hh + 1], None, ALU.mult), "vS", 1)
                    @block.scalar
                    def _(s):
                        P.w(s, "vS")
                        P.inc(s.activation(gs2[:], gsf[:], AF.Silu), "sS", 1)
                    @block.vector
                    def _(v, hh=hh, sl=sl):
                        P.w(v, "tS", mmu[hh])
                        P.w(v, "sS")
                        P.inc(v.tensor_scalar(gsf[:], ps4[:, 2048:4096], su_c[:, hh:hh + 1], None, ALU.mult), "vS", 1)
                        P.w(v, "vS")
                        P.w(v, "w0" if sl == 0 else "w1", free_gu[sl])
                        P.inc(v.tensor_tensor(gub[:, sl * HID:(sl + 1) * HID], gs2[:], gsf[:], ALU.mult), "vS", 1)
                    @block.scalar
                    def _(s, hh=hh, sl=sl):
                        P.w(s, "vS")
                        free_gu[sl] = P.inc(s.dma_start(out=gu_sp[hh * 128:(hh + 1) * 128, :],
                                                        in_=gub[:, sl * HID:(sl + 1) * HID]), "w0" if sl == 0 else "w1", 16)

        def emit_p5():
            KH = 11
            with nc.sbuf_tensor("p5_wd", [128, KH * 2048], BF16, allow_name_mangling=True) as wdb, \
                 nc.sbuf_tensor("p5_gs", [128, 2 * KH * 128], BF16, allow_name_mangling=True) as gst, \
                 nc.sbuf_tensor("p5_pl", [128, 2 * HID], F32, allow_name_mangling=True) as pld, \
                 nc.sbuf_tensor("p5_tf", [128, HID], F32, allow_name_mangling=True) as tf5, \
                 nc.sbuf_tensor("p5_gc", [128, 20], F32, allow_name_mangling=True) as gcol, \
                 nc.sbuf_tensor("p5_pd", [128, 2 * HID], F32, allow_name_mangling=True) as pdst, \
                 nc.sbuf_tensor("p5_gb", [128, 2 * HID], BF16, allow_name_mangling=True) as gb5, \
                 nc.sbuf_tensor("p5_sd", [128, HID], F32, allow_name_mangling=True) as SDb, \
                 nc.sbuf_tensor("p5_r", [1, HID], F32, allow_name_mangling=True) as r_sd5, \
                 nc.psum_tensor("p5_ps", [128, 2 * 1024], F32, allow_name_mangling=True) as ps5, \
                 nc.psum_tensor("p5_pg", [128, 512], F32, allow_name_mangling=True) as psg5:
                barrier()
                pass
                @block.sync
                def _(sync):
                    P.inc(sync.dma_start(out=r_sd5[:], in_=sdr), "d0", 16)
                sd_rdy = P.c["d0"]
                for jj in range(HID // 512):
                    @block.tensor
                    def _(t, jj=jj):
                        P.w(t, "d0", sd_rdy)
                        P.w(t, "vS")
                        P.inc(t.matmul(psg5[:, 0:512], ones_r[:],
                                       r_sd5[0:1, jj * 512:(jj + 1) * 512], start=True, stop=True), "tS", 1)
                    @block.vector
                    def _(v, jj=jj):
                        P.w(v, "tS")
                        P.inc(v.tensor_copy(SDb[:, jj * 512:(jj + 1) * 512], psg5[:, 0:512]), "vS", 1)
                wv5 = wdt.rearrange("(k p) c -> p k c", p=128)
                free_gst = [0, 0]
                free_pl = [0, 0]
                free_pd = [0, 0]
                free_gb = [0, 0]
                ev_ps = [("vS", 0), ("vS", 0)]
                for pz in range(4):
                    w_rdy = [0]
                    @block.gpsimd
                    def _(g, pz=pz):
                        P.w(g, "tS")
                        for piece in range(3):
                            k0 = piece * 4
                            kn = min(4, KH - k0)
                            w_rdy[0] = P.inc(g.dma_start(
                                out=wdb.rearrange("p (k c) -> p k c", c=2048)[:, k0:k0 + kn, :],
                                in_=wv5[:, pz * KH + k0: pz * KH + k0 + kn, :]), "g0", 16)
                    if pz > 0:
                        @block.sync
                        def _(sync):
                            P.w(sync, "w0"); P.w(sync, "w1")
                    gst_rdy = [0] * NTT
                    pl_rdy = [0] * NTT
                    for t_i in range(NTT):
                        sl = t_i % 2
                        @block.sync
                        def _(sync, pz=pz, t_i=t_i, sl=sl):
                            if pz == 0 and t_i == 0:
                                pass
                            P.w(sync, "tS", free_gst[sl])
                            last = 0
                            for k in range(KH):
                                last = P.inc(sync.dma_start(
                                    out=gst[:, sl * (KH * 128) + k * 128: sl * (KH * 128) + (k + 1) * 128],
                                    in_=gu_sp[(pz * KH + k) * 128:(pz * KH + k + 1) * 128,
                                              t_i * 128:(t_i + 1) * 128]), "d1" if sl == 0 else "d2", 16)
                            gst_rdy[t_i] = last
                            if pz > 0:
                                P.w(sync, "vS", free_pl[sl])
                                pass
                                pl_rdy[t_i] = P.inc(sync.dma_start(
                                    out=pld[:, sl * HID:(sl + 1) * HID],
                                    in_=pp_sp[t_i * 128:(t_i + 1) * 128, :]), "d3" if sl == 0 else "d4", 16)
                        for dh in range(2):
                            @block.tensor
                            def _(t, pz=pz, t_i=t_i, sl=sl, dh=dh):
                                P.w(t, "g0", w_rdy[0])
                                P.w(t, "d1" if sl == 0 else "d2", gst_rdy[t_i])
                                P.w(t, ev_ps[sl][0], ev_ps[sl][1])
                                for k in range(KH):
                                    for n in range(2):
                                        ins = t.matmul(
                                            ps5[:, sl * 1024 + n * 512: sl * 1024 + (n + 1) * 512],
                                            gst[:, sl * (KH * 128) + k * 128: sl * (KH * 128) + (k + 1) * 128],
                                            wdb.rearrange("p (k c) -> p k c", c=2048)[:, k, dh * 1024 + n * 512: dh * 1024 + (n + 1) * 512],
                                            start=(k == 0), stop=(k == KH - 1))
                                P.inc(ins, "tS", 1)
                            if pz == 0:
                                @block.scalar
                                def _(s, t_i=t_i, sl=sl, dh=dh):
                                    P.w(s, "tS")
                                    if dh == 0:
                                        P.w(s, "w0" if sl == 0 else "w1", free_pd[sl])
                                    c = P.inc(s.copy(pdst[:, sl * HID + dh * 1024: sl * HID + (dh + 1) * 1024],
                                                     ps5[:, sl * 1024: sl * 1024 + 1024]), "sS", 1)
                                    ev_ps[sl] = ("sS", c)
                            elif pz < 3:
                                @block.vector
                                def _(v, t_i=t_i, sl=sl, dh=dh):
                                    P.w(v, "tS")
                                    P.w(v, "d3" if sl == 0 else "d4", pl_rdy[t_i])
                                    if dh == 0:
                                        P.w(v, "w0" if sl == 0 else "w1", free_pd[sl])
                                    c = P.inc(v.tensor_tensor(
                                        pdst[:, sl * HID + dh * 1024: sl * HID + (dh + 1) * 1024],
                                        ps5[:, sl * 1024: sl * 1024 + 1024],
                                        pld[:, sl * HID + dh * 1024: sl * HID + (dh + 1) * 1024],
                                        ALU.add), "vS", 1)
                                    ev_ps[sl] = ("vS", c)
                                    if dh == 1:
                                        free_pl[sl] = c
                            else:
                                @block.vector
                                def _(v, t_i=t_i, sl=sl, dh=dh):
                                    P.w(v, "tS")
                                    P.w(v, "d3" if sl == 0 else "d4", pl_rdy[t_i])
                                    P.w(v, "sS")
                                    c = P.inc(v.tensor_tensor(
                                        tf5[:, dh * 1024:(dh + 1) * 1024],
                                        ps5[:, sl * 1024: sl * 1024 + 1024],
                                        pld[:, sl * HID + dh * 1024: sl * HID + (dh + 1) * 1024],
                                        ALU.add), "vS", 1)
                                    ev_ps[sl] = ("vS", c)
                                    if dh == 1:
                                        free_pl[sl] = c
                        @block.tensor
                        def _(t, pz=pz, sl=sl):
                            P.w(t, "vS")
                            for k in range(KH):
                                ins = t.matmul(psg5[:, 0:1],
                                               gst[:, sl * (KH * 128) + k * 128: sl * (KH * 128) + (k + 1) * 128],
                                               ones_cb[:], start=(k == 0), stop=(k == KH - 1))
                            P.inc(ins, "tS", 1)
                            free_gst[sl] = P.c["tS"]
                        if pz == 0:
                            @block.vector
                            def _(v, t_i=t_i):
                                P.w(v, "tS")
                                P.inc(v.tensor_copy(gcol[:, t_i:t_i + 1], psg5[:, 0:1]), "vS", 1)
                        else:
                            @block.vector
                            def _(v, t_i=t_i):
                                P.w(v, "tS")
                                P.inc(v.tensor_tensor(gcol[:, t_i:t_i + 1], psg5[:, 0:1],
                                                      gcol[:, t_i:t_i + 1], ALU.add), "vS", 1)
                        if pz < 3:
                            @block.scalar
                            def _(s, pz=pz, t_i=t_i, sl=sl):
                                P.w(s, ev_ps[sl][0], ev_ps[sl][1])
                                free_pd[sl] = P.inc(s.dma_start(out=pp_sp[t_i * 128:(t_i + 1) * 128, :],
                                                                in_=pdst[:, sl * HID:(sl + 1) * HID]), "w0" if sl == 0 else "w1", 16)
                        else:
                            @block.vector
                            def _(v, t_i=t_i, sl=sl):
                                P.w(v, "vS")
                                P.inc(v.tensor_scalar(gcol[:, 17:18], gcol[:, t_i:t_i + 1], 128.0, None, ALU.mult), "vS", 1)
                                P.w(v, "vS")
                                P.inc(v.tensor_scalar(tf5[:], tf5[:], gcol[:, 17:18], None, ALU.subtract), "vS", 1)
                                P.w(v, "vS")
                                P.inc(v.tensor_tensor(tf5[:], tf5[:], SDb[:], ALU.mult), "vS", 1)
                            @block.scalar
                            def _(s, t_i=t_i, sl=sl):
                                P.w(s, "vS")
                                P.w(s, "w2" if sl == 0 else "w3", free_gb[sl])
                                P.inc(s.activation(gb5[:, sl * HID:(sl + 1) * HID], tf5[:], AF.Silu), "sS", 1)
                                P.w(s, "sS")
                                free_gb[sl] = P.inc(s.dma_start(out=g_sp[t_i * 128:(t_i + 1) * 128, :],
                                                                in_=gb5[:, sl * HID:(sl + 1) * HID]), "w2" if sl == 0 else "w3", 16)

        def emit_p7():
            with nc.sbuf_tensor("p7_w", [128, NKD * 2048], BF16, allow_name_mangling=True) as woh, \
                 nc.sbuf_tensor("p7_yg", [128, 2 * 2048], BF16, allow_name_mangling=True) as ygs, \
                 nc.sbuf_tensor("p7_pl", [128, 2 * HID], F32, allow_name_mangling=True) as pol, \
                 nc.sbuf_tensor("p7_g", [128, 2 * HID], BF16, allow_name_mangling=True) as gl7, \
                 nc.sbuf_tensor("p7_x", [128, 2 * HID], F32, allow_name_mangling=True) as xl7, \
                 nc.sbuf_tensor("p7_o", [128, 2 * HID], F32, allow_name_mangling=True) as ob7, \
                 nc.psum_tensor("p7_ps", [128, 2 * 2048], F32, allow_name_mangling=True) as ps7:
                barrier()
                wv7 = wout.rearrange("(k p) c -> p k c", p=128)
                free_st = [0, 0]
                free_ps = [("sS", 0), ("sS", 0)]
                free_ld = [0, 0]
                free_ob = [("w0", 0), ("w1", 0)]
                for half in range(2):
                    w_rdy = [0]
                    @block.gpsimd
                    def _(g, half=half):
                        P.w(g, "tS")
                        last = 0
                        for piece in range(4):
                            last = P.inc(g.dma_start(
                                out=woh.rearrange("p (k c) -> p k c", c=2048)[:, piece * 4:(piece + 1) * 4, :],
                                in_=wv7[:, half * NKD + piece * 4: half * NKD + (piece + 1) * 4, :]), "g0", 16)
                        w_rdy[0] = last
                    st_rdy = [0] * NTT
                    ld_rdy = [0] * NTT
                    for t_i in range(NTT):
                        sl = t_i % 2
                        @block.sync
                        def _(sync, half=half, t_i=t_i, sl=sl):
                            if half == 0 and t_i == 0:
                                pass
                            P.w(sync, "tS", free_st[sl])
                            last = 0
                            for k in range(NKD):
                                last = P.inc(sync.dma_start(
                                    out=ygs[:, sl * 2048 + k * 128: sl * 2048 + (k + 1) * 128],
                                    in_=ygt_sp[(half * NKD + k) * 128:(half * NKD + k + 1) * 128,
                                               t_i * 128:(t_i + 1) * 128]), "d0" if sl == 0 else "d1", 16)
                            st_rdy[t_i] = last
                            if half == 1:
                                if t_i == 0:
                                    P.w(sync, "w0"); P.w(sync, "w1")
                                P.w(sync, "vS", free_ld[sl])
                                P.inc(sync.dma_start(out=pol[:, sl * HID:(sl + 1) * HID],
                                                     in_=pp_sp[t_i * 128:(t_i + 1) * 128, :]), "d2" if sl == 0 else "d3", 16)
                                P.inc(sync.dma_start(out=gl7[:, sl * HID:(sl + 1) * HID],
                                                     in_=g_sp[t_i * 128:(t_i + 1) * 128, :]), "d2" if sl == 0 else "d3", 16)
                                ld_rdy[t_i] = P.inc(sync.dma_start(out=xl7[:, sl * HID:(sl + 1) * HID],
                                                                   in_=xo[t_i * 128:(t_i + 1) * 128, :]), "d2" if sl == 0 else "d3", 16)
                        @block.tensor
                        def _(t, half=half, t_i=t_i, sl=sl):
                            P.w(t, "g0", w_rdy[0])
                            P.w(t, "d0" if sl == 0 else "d1", st_rdy[t_i])
                            P.w(t, free_ps[sl][0], free_ps[sl][1])
                            for k in range(NKD):
                                for n in range(4):
                                    ins = t.matmul(ps7[:, sl * 2048 + n * 512: sl * 2048 + (n + 1) * 512],
                                                   ygs[:, sl * 2048 + k * 128: sl * 2048 + (k + 1) * 128],
                                                   woh.rearrange("p (k c) -> p k c", c=2048)[:, k, n * 512:(n + 1) * 512],
                                                   start=(k == 0), stop=(k == NKD - 1))
                            P.inc(ins, "tS", 1)
                            free_st[sl] = P.c["tS"]
                        if half == 0:
                            @block.scalar
                            def _(s, t_i=t_i, sl=sl):
                                P.w(s, "tS")
                                P.w(s, free_ob[sl][0], free_ob[sl][1])
                                P.inc(s.copy(ob7[:, sl * HID:(sl + 1) * HID], ps7[:, sl * 2048: sl * 2048 + 2048]), "sS", 1)
                                free_ps[sl] = ("sS", P.c["sS"])
                                P.w(s, "sS")
                                sem = "w0" if sl == 0 else "w1"
                                free_ob[sl] = (sem, P.inc(s.dma_start(out=pp_sp[t_i * 128:(t_i + 1) * 128, :],
                                                                      in_=ob7[:, sl * HID:(sl + 1) * HID]), sem, 16))
                        else:
                            @block.vector
                            def _(v, t_i=t_i, sl=sl):
                                P.w(v, "tS")
                                P.w(v, "d2" if sl == 0 else "d3", ld_rdy[t_i])
                                P.w(v, free_ob[sl][0], free_ob[sl][1])
                                obs = ob7[:, sl * HID:(sl + 1) * HID]
                                P.inc(v.tensor_tensor(obs, ps7[:, sl * 2048: sl * 2048 + 2048],
                                                pol[:, sl * HID:(sl + 1) * HID], ALU.add), "vS", 1)
                                P.w(v, "vS")
                                free_ps[sl] = ("vS", P.c["vS"])
                                P.inc(v.tensor_tensor(obs, obs, gl7[:, sl * HID:(sl + 1) * HID], ALU.mult), "vS", 1)
                                P.w(v, "vS")
                                P.inc(v.tensor_tensor(obs, obs, xl7[:, sl * HID:(sl + 1) * HID], ALU.add), "vS", 1)
                                free_ld[sl] = P.c["vS"]
                            @block.scalar
                            def _(s, t_i=t_i, sl=sl):
                                P.w(s, "vS")
                                sem = "w2" if sl == 0 else "w3"
                                free_ob[sl] = (sem, P.inc(s.dma_start(out=outd[t_i * 128:(t_i + 1) * 128, :],
                                                                      in_=ob7[:, sl * HID:(sl + 1) * HID]), sem, 16))

        with nc.sbuf_tensor("xnT", [128, NKD * T], BF16, allow_name_mangling=True) as xnT, \
             nc.sbuf_tensor("xsrb", [1, T], BF16, allow_name_mangling=True) as xsrb:
            emit_ln(xc, False, xnT, xsrb)
            emit_inprojT(True, xnT)
            with nc.sbuf_tensor("c_d1", [128, NCH * 64], F32, allow_name_mangling=True) as dtT_all, \
                 nc.sbuf_tensor("c_d2", [128, NCH * 64], F32, allow_name_mangling=True) as dsT_all, \
                 nc.sbuf_tensor("c_d3", [128, NCH * 64], F32, allow_name_mangling=True) as etb_all, \
                 nc.sbuf_tensor("c_d4", [128, NCH * 64], F32, allow_name_mangling=True) as sdo_all, \
                 nc.sbuf_tensor("c_d5", [128, NCH * 64], F32, allow_name_mangling=True) as negA_all:
                emit_prepass(dtT_all, dsT_all, etb_all, sdo_all, negA_all, False)
                emit_ssd(False, dtT_all, dsT_all, etb_all, sdo_all, negA_all)
            emit_ln(xo, True, xnT, xsrb)
            emit_p2(xnT)
            emit_inprojT(False, xnT)
            emit_p4(xnT, xsrb)
        emit_p5()
        with nc.sbuf_tensor("o_d1", [128, NCH * 64], F32, allow_name_mangling=True) as dtT_all, \
             nc.sbuf_tensor("o_d2", [128, NCH * 64], F32, allow_name_mangling=True) as dsT_all, \
             nc.sbuf_tensor("o_d3", [128, NCH * 64], F32, allow_name_mangling=True) as etb_all, \
             nc.sbuf_tensor("o_d4", [128, NCH * 64], F32, allow_name_mangling=True) as sdo_all, \
             nc.sbuf_tensor("o_d5", [128, NCH * 64], F32, allow_name_mangling=True) as negA_all:
            emit_prepass(dtT_all, dsT_all, etb_all, sdo_all, negA_all, True)
            emit_ssd(True, dtT_all, dsT_all, etb_all, sdo_all, negA_all)
        emit_p7()

        @block.gpsimd
        def _(g):
            pass
            for s in ("d0", "d1", "d2", "d3", "d4", "g0", "g1", "w0", "w1", "w2", "w3", "tS", "vS", "sS"): P.w(g, s)
            pass

    es.close()
    return nc


def kernel(**inputs):
    xs = np.ascontiguousarray(np.asarray(inputs["L0_attn_hidden_states"], dtype=np.float32))
    wip = np.ascontiguousarray(np.asarray(inputs["L0_attn_in_proj_w"], dtype=np.float32))
    convw = np.ascontiguousarray(np.asarray(inputs["L0_attn_conv_w"], dtype=np.float32))
    convb = np.ascontiguousarray(np.asarray(inputs["L0_attn_conv_b"], dtype=np.float32).reshape(1, -1))
    dtb = np.ascontiguousarray(np.asarray(inputs["L0_attn_dt_bias"], dtype=np.float32).reshape(1, -1))
    alog = np.ascontiguousarray(np.asarray(inputs["L0_attn_A_log"], dtype=np.float32).reshape(1, -1))
    dvec = np.ascontiguousarray(np.asarray(inputs["L0_attn_D"], dtype=np.float32).reshape(1, -1))
    nw = np.ascontiguousarray(np.asarray(inputs["L0_attn_norm_weight"], dtype=np.float32).reshape(1, -1))
    wout = np.ascontiguousarray(np.asarray(inputs["L0_attn_out_proj_w"], dtype=np.float32))
    wgt = np.ascontiguousarray(np.asarray(inputs["L0_ffn_wq_gate"], dtype=np.int32).T)
    wut = np.ascontiguousarray(np.asarray(inputs["L0_ffn_wq_up"], dtype=np.int32).T)
    wdt = np.ascontiguousarray(np.asarray(inputs["L0_ffn_wq_down"], dtype=np.int32).T)
    sg = np.ascontiguousarray(np.asarray(inputs["L0_ffn_s_gate"], dtype=np.float32).reshape(1, -1))
    su = np.ascontiguousarray(np.asarray(inputs["L0_ffn_s_up"], dtype=np.float32).reshape(1, -1))
    sd = np.ascontiguousarray(np.asarray(inputs["L0_ffn_s_down"], dtype=np.float32).reshape(1, -1))

    idbf = np.eye(128, dtype=np.float32).astype(BF)
    idf = np.eye(128, dtype=np.float32)
    s_i = np.arange(128)[:, None]
    l_i = np.arange(128)[None, :]
    trip = (s_i <= l_i).astype(np.float32)
    trin = -trip
    maskt = np.where(s_i > l_i, np.float32(-1e9), np.float32(0.0)).astype(BF)
    onesr = np.ones((1, 128), np.float32)
    onesrb = np.ones((1, 128), np.float32).astype(BF)
    onescb = np.ones((128, 1), np.float32).astype(BF)
    n128 = np.full((1, 128), -128.0, np.float32).astype(BF)

    zeros_ctx = np.zeros((T, HID), np.float32)
    nc = build_program()
    in_maps = []
    for core in range(8):
        b, h = core // 2, core % 2
        xo_ = np.ascontiguousarray(xs[b, h * T:(h + 1) * T, :])
        xc_ = np.ascontiguousarray(xs[b, 0:T, :]) if h == 1 else zeros_ctx
        rm = np.array([[1.0 if h == 1 else 0.0]], np.float32)
        in_maps.append({
            "xo": xo_, "xc": xc_, "rmaskd": rm, "wip": wip, "convw": convw,
            "convbr": convb, "dtbr": dtb, "alogr": alog, "dvecr": dvec, "nwr": nw,
            "wout": wout, "wgt": wgt, "wut": wut, "wdt": wdt, "sgr": sg, "sur": su,
            "sdr": sd, "idbf_d": idbf, "idf_d": idf, "trip_d": trip, "trin_d": trin,
            "maskt_d": maskt, "onesr_d": onesr, "onesrb_d": onesrb,
            "onescb_d": onescb, "n128_d": n128,
        })
    res = run_bass_kernel_spmd(nc, in_maps, core_ids=list(range(8)))
    out = np.empty((4, 4096, HID), np.float32)
    for core in range(8):
        b, h = core // 2, core % 2
        out[b, h * T:(h + 1) * T, :] = res.results[core]["outd"]
    return out


# version 02
# speedup vs baseline: 1.0038x; 1.0038x over previous
import numpy as np
import ml_dtypes
from contextlib import ExitStack
import concourse.bass as bass
import concourse.mybir as mybir
from concourse.bass_utils import run_bass_kernel_spmd

F32 = mybir.dt.float32
BF16 = mybir.dt.bfloat16
I32 = mybir.dt.int32
AF = mybir.ActivationFunctionType
ALU = mybir.AluOpType
BF = ml_dtypes.bfloat16
AX = mybir.AxisListType.X

HID = 2048
INTER = 4096
NH = 64
FFH = 5632
T = 2048
CS = 128
NCH = T // CS
NTT = T // 128
NKD = HID // 128
NHT = FFH // 128
EPS = 1e-5


class Pr:
    def __init__(self, nc, es):
        self.nc, self.es = nc, es
        self.h, self.c = {}, {}

    def sem(self, *names):
        for n in names:
            self.h[n] = self.es.enter_context(self.nc.semaphore(n))
            self.c[n] = 0

    def inc(self, instr, s, n=1):
        instr.then_inc(self.h[s], n)
        self.c[s] += n
        return self.c[s]

    def w(self, eng, s, v=None):
        v = self.c[s] if v is None else v
        if v > 0:
            eng.wait_ge(self.h[s], v)


def build_program():
    nc = bass.Bass("TRN2", target_bir_lowering=False)
    dt_ = nc.dram_tensor
    xo = dt_("xo", [T, HID], F32, kind="ExternalInput").ap()
    xc = dt_("xc", [T, HID], F32, kind="ExternalInput").ap()
    rmaskd = dt_("rmaskd", [1, 1], F32, kind="ExternalInput").ap()
    wip = dt_("wip", [HID, 8512], F32, kind="ExternalInput").ap()
    convw = dt_("convw", [4352, 4], F32, kind="ExternalInput").ap()
    convbr = dt_("convbr", [1, 4352], F32, kind="ExternalInput").ap()
    dtbr = dt_("dtbr", [1, NH], F32, kind="ExternalInput").ap()
    alogr = dt_("alogr", [1, NH], F32, kind="ExternalInput").ap()
    dvecr = dt_("dvecr", [1, NH], F32, kind="ExternalInput").ap()
    nwr = dt_("nwr", [1, INTER], F32, kind="ExternalInput").ap()
    wout = dt_("wout", [INTER, HID], F32, kind="ExternalInput").ap()
    wgt = dt_("wgt", [HID, FFH], I32, kind="ExternalInput").ap()
    wut = dt_("wut", [HID, FFH], I32, kind="ExternalInput").ap()
    wdt = dt_("wdt", [FFH, HID], I32, kind="ExternalInput").ap()
    sgr = dt_("sgr", [1, FFH], F32, kind="ExternalInput").ap()
    sur = dt_("sur", [1, FFH], F32, kind="ExternalInput").ap()
    sdr = dt_("sdr", [1, HID], F32, kind="ExternalInput").ap()
    idbf_d = dt_("idbf_d", [128, 128], BF16, kind="ExternalInput").ap()
    idf_d = dt_("idf_d", [128, 128], F32, kind="ExternalInput").ap()
    trip_d = dt_("trip_d", [128, 128], F32, kind="ExternalInput").ap()
    trin_d = dt_("trin_d", [128, 128], F32, kind="ExternalInput").ap()
    maskt_d = dt_("maskt_d", [128, 128], BF16, kind="ExternalInput").ap()
    onesr_d = dt_("onesr_d", [1, 128], F32, kind="ExternalInput").ap()
    onesrb_d = dt_("onesrb_d", [1, 128], BF16, kind="ExternalInput").ap()
    onescb_d = dt_("onescb_d", [128, 1], BF16, kind="ExternalInput").ap()
    n128_d = dt_("n128_d", [1, 128], BF16, kind="ExternalInput").ap()
    x_sp = dt_("x_sp", [INTER, T], BF16, kind="Internal").ap()
    gu_sp = dt_("gu_sp", [FFH, T], BF16, kind="Internal").ap()
    sg_sp = dt_("sg_sp", [T, INTER], BF16, kind="Internal").ap()
    g_sp = dt_("g_sp", [T, HID], BF16, kind="Internal").ap()
    pp_sp = dt_("pp_sp", [T, HID], F32, kind="Internal").ap()
    ygt_sp = dt_("ygt_sp", [INTER, T], BF16, kind="Internal").ap()
    scrH = dt_("scrH", [NCH, NH * CS], BF16, kind="Internal").ap()
    scrL = dt_("scrL", [NCH, NH * CS], BF16, kind="Internal").ap()
    scrX = dt_("scrX", [16, 128], F32, kind="Internal").ap()
    outd = dt_("outd", [T, HID], F32, kind="ExternalOutput").ap()

    es = ExitStack()
    sbo = lambda name, shape, dt: es.enter_context(nc.sbuf_tensor(name, shape, dt))

    id_bf = sbo("id_bf", [128, 128], BF16)
    id_f = sbo("id_f", [128, 128], F32)
    triP = sbo("triP", [128, 128], F32)
    triN = sbo("triN", [128, 128], F32)
    maskT = sbo("maskT", [128, 128], BF16)
    ones_r = sbo("ones_r", [1, 128], F32)
    ones_rb = sbo("ones_rb", [1, 128], BF16)
    ones_cb = sbo("ones_cb", [128, 1], BF16)
    n128r = sbo("n128r", [1, 128], BF16)
    ones11 = sbo("ones11", [1, 1], F32)
    convw_s = sbo("convw_s", [128, 140], F32)
    convb_c = sbo("convb_c", [128, 35], F32)
    sg_c = sbo("sg_c", [128, NHT], F32)
    su_c = sbo("su_c", [128, NHT], F32)
    dtb_c = sbo("dtb_c", [64, 1], F32)
    A_c = sbo("A_c", [64, 1], F32)
    D_b = sbo("D_b", [128, 64], F32)
    rmask_c = sbo("rmask_c", [128, 1], F32)
    NWb = sbo("NWb", [128, INTER], F32)
    Bt = sbo("Bt", [128, T], BF16)
    Ctn = sbo("Ctn", [128, T], BF16)
    dts = sbo("dts", [64, T], F32)
    adts = sbo("adts", [64, T], F32)
    R = sbo("Rst", [128, INTER], F32)
    hist = sbo("hist", [128, 105], F32)

    P = Pr(nc, es)
    with nc.Block() as block:
        P.sem("d0", "d1", "d2", "d3", "d4", "g0", "g1", "w0", "w1", "w2", "w3", "tS", "vS", "sS")

        def barrier():
            for engname in ("sync", "gpsimd", "vector", "scalar", "tensor"):
                @getattr(block, engname)
                def _(e):
                    for s in ("d0", "d1", "d2", "d3", "d4", "g0", "g1", "w0", "w1", "w2", "w3", "tS", "vS", "sS"):
                        P.w(e, s)

        with nc.sbuf_tensor("r_cb", [1, 4352], F32, allow_name_mangling=True) as r_cb, \
             nc.sbuf_tensor("r_sg", [1, FFH], F32, allow_name_mangling=True) as r_sg, \
             nc.sbuf_tensor("r_su", [1, FFH], F32, allow_name_mangling=True) as r_su, \
             nc.sbuf_tensor("r_sm", [1, 193], F32, allow_name_mangling=True) as r_sm, \
             nc.sbuf_tensor("r_nw", [1, INTER], F32, allow_name_mangling=True) as r_nw, \
             nc.sbuf_tensor("t_c64", [64, 1], F32, allow_name_mangling=True) as t_c64, \
             nc.psum_tensor("ps_set", [128, 512], F32, allow_name_mangling=True) as ps_set:

            @block.sync
            def _(sync):
                for dst, src in [(id_bf, idbf_d), (id_f, idf_d), (triP, trip_d),
                                 (triN, trin_d), (maskT, maskt_d), (ones_r, onesr_d),
                                 (ones_rb, onesrb_d), (ones_cb, onescb_d), (n128r, n128_d),
                                 (r_cb, convbr), (r_sg, sgr), (r_su, sur), (r_nw, nwr)]:
                    P.inc(sync.dma_start(out=dst[:], in_=src), "d0", 16)
                P.inc(sync.dma_start(out=r_sm[:, 0:64], in_=dtbr), "d0", 16)
                P.inc(sync.dma_start(out=r_sm[:, 64:128], in_=alogr), "d0", 16)
                P.inc(sync.dma_start(out=r_sm[:, 128:192], in_=dvecr), "d0", 16)
                P.inc(sync.dma_start(out=r_sm[:, 192:193], in_=rmaskd), "d0", 16)
                for j in range(34):
                    P.inc(sync.dma_start(out=convw_s[:, j * 4:(j + 1) * 4],
                                         in_=convw[j * 128:(j + 1) * 128, :]), "d0", 16)
            dma_setup = P.c["d0"]

            @block.vector
            def _(v):
                P.w(v, "d0", dma_setup)
                v.memset(ones11[:], 1.0)
                v.memset(hist[:], 0.0)
                P.inc(v.memset(R[:], 0.0), "vS", 1)

            def rows_to_cols(row_ap, n, dst_fn):
                j = 0
                while j < n:
                    m = min(8, n - j)
                    @block.tensor
                    def _(t, j=j, m=m):
                        P.w(t, "vS")
                        for i in range(m):
                            P.inc(t.matmul(ps_set[:, j + i:j + i + 1],
                                           row_ap[0:1, (j + i) * 128:(j + i + 1) * 128],
                                           ones11[:], start=True, stop=True), "tS", 1)
                    @block.vector
                    def _(v, j=j, m=m):
                        P.w(v, "tS")
                        for i in range(m):
                            P.inc(v.tensor_copy(dst_fn(j + i), ps_set[:, j + i:j + i + 1]), "vS", 1)
                    j += m

            rows_to_cols(r_cb, 34, lambda j: convb_c[:, j:j + 1])
            rows_to_cols(r_sg, NHT, lambda j: sg_c[:, j:j + 1])
            rows_to_cols(r_su, NHT, lambda j: su_c[:, j:j + 1])

            @block.tensor
            def _(t):
                P.w(t, "vS")
                P.inc(t.matmul(ps_set[0:64, 0:1], r_sm[0:1, 0:64], ones11[:], start=True, stop=True), "tS", 1)
                P.inc(t.matmul(ps_set[0:64, 1:2], r_sm[0:1, 64:128], ones11[:], start=True, stop=True), "tS", 1)
                P.inc(t.matmul(ps_set[:, 64:128], ones_r[:], r_sm[0:1, 128:192], start=True, stop=True), "tS", 1)
                P.inc(t.matmul(ps_set[:, 128:129], ones_r[:], r_sm[0:1, 192:193], start=True, stop=True), "tS", 1)
            @block.vector
            def _(v):
                P.w(v, "tS")
                v.tensor_copy(dtb_c[:], ps_set[0:64, 0:1])
                v.tensor_copy(t_c64[:], ps_set[0:64, 1:2])
                v.tensor_copy(D_b[:], ps_set[:, 64:128])
                P.inc(v.tensor_copy(rmask_c[:], ps_set[:, 128:129]), "vS", 1)
            @block.scalar
            def _(s):
                P.w(s, "vS")
                P.inc(s.activation(A_c[:], t_c64[:], AF.Exp), "sS", 1)
            @block.vector
            def _(v):
                P.w(v, "sS")
                P.inc(v.tensor_scalar(A_c[:], A_c[:], -1.0, None, ALU.mult), "vS", 1)
            for jj in range(INTER // 512):
                @block.tensor
                def _(t, jj=jj):
                    P.w(t, "vS")
                    P.inc(t.matmul(ps_set[:, 0:512], ones_r[:],
                                   r_nw[0:1, jj * 512:(jj + 1) * 512], start=True, stop=True), "tS", 1)
                @block.vector
                def _(v, jj=jj):
                    P.w(v, "tS")
                    P.inc(v.tensor_copy(NWb[:, jj * 512:(jj + 1) * 512], ps_set[:, 0:512]), "vS", 1)

        def emit_ln(x_dram, collect_xsum, xnT, xsrb):
            with nc.sbuf_tensor("ln_xb", [128, 2 * HID], F32, allow_name_mangling=True) as xb, \
                 nc.sbuf_tensor("ln_xc", [128, 2 * HID], F32, allow_name_mangling=True) as xcb, \
                 nc.sbuf_tensor("ln_xn", [128, 2 * HID], BF16, allow_name_mangling=True) as xnb, \
                 nc.sbuf_tensor("ln_sm", [128, 8], F32, allow_name_mangling=True) as sm, \
                 nc.sbuf_tensor("ln_xs", [128, 16], F32, allow_name_mangling=True) as xsc, \
                 nc.sbuf_tensor("ln_xr", [16, 128], F32, allow_name_mangling=True) as xrow, \
                 nc.sbuf_tensor("ln_x1", [1, T], F32, allow_name_mangling=True) as xsr, \
                 nc.psum_tensor("ln_pt", [128, 2048], BF16, allow_name_mangling=True) as ps_t, \
                 nc.psum_tensor("ln_pf", [128, 128], F32, allow_name_mangling=True) as ps_f:
                barrier()
                free_x = [0, 0]
                free_xc = [0, 0]
                free_xn = [0, 0]
                free_ps = 0
                rdy = [0] * NTT
                for i in range(NTT):
                    sl = i % 2
                    @block.sync
                    def _(sync, i=i, sl=sl):
                        P.w(sync, "vS", free_x[sl])
                        rdy[i] = P.inc(sync.dma_start(out=xb[:, sl * HID:(sl + 1) * HID],
                                                      in_=x_dram[i * 128:(i + 1) * 128, :]), "d0" if sl == 0 else "d1", 16)
                    @block.vector
                    def _(v, i=i, sl=sl):
                        P.w(v, "d0" if sl == 0 else "d1", rdy[i])
                        xbs = xb[:, sl * HID:(sl + 1) * HID]
                        P.inc(v.reduce_sum(sm[:, 0:1], xbs, axis=AX), "vS", 1)
                        P.w(v, "vS")
                        P.inc(v.tensor_scalar(sm[:, 1:2], sm[:, 0:1], -1.0 / HID, None, ALU.mult), "vS", 1)
                        P.w(v, "vS")
                        P.w(v, "vS", free_xc[sl])
                        P.inc(v.tensor_scalar(xcb[:, sl * HID:(sl + 1) * HID], xbs,
                                              sm[:, 1:2], None, ALU.add), "vS", 1)
                    @block.scalar
                    def _(s, sl=sl):
                        P.w(s, "vS")
                        P.inc(s.activation(xb[:, sl * HID:(sl + 1) * HID],
                                           xcb[:, sl * HID:(sl + 1) * HID], AF.Square), "sS", 1)
                    @block.vector
                    def _(v, sl=sl):
                        P.w(v, "sS")
                        P.inc(v.reduce_sum(sm[:, 2:3], xb[:, sl * HID:(sl + 1) * HID], axis=AX), "vS", 1)
                        P.w(v, "vS")
                        free_x[sl] = P.inc(v.tensor_scalar(sm[:, 3:4], sm[:, 2:3],
                                                           1.0 / HID, EPS, ALU.mult, ALU.add), "vS", 1)
                    @block.scalar
                    def _(s):
                        P.w(s, "vS")
                        P.inc(s.activation(sm[:, 4:5], sm[:, 3:4], AF.Ln), "sS", 1)
                        P.w(s, "sS")
                        P.inc(s.activation(sm[:, 5:6], sm[:, 4:5], AF.Exp, scale=-0.5), "sS", 1)
                    @block.vector
                    def _(v, i=i, sl=sl):
                        P.w(v, "sS")
                        P.w(v, "tS", free_xn[sl])
                        free_xc[sl] = P.inc(v.tensor_scalar(xnb[:, sl * HID:(sl + 1) * HID],
                                                            xcb[:, sl * HID:(sl + 1) * HID],
                                                            sm[:, 5:6], None, ALU.mult), "vS", 1)
                        if collect_xsum:
                            P.w(v, "vS")
                            P.inc(v.reduce_sum(xsc[:, i:i + 1], xnb[:, sl * HID:(sl + 1) * HID],
                                               axis=AX), "vS", 1)
                    @block.tensor
                    def _(t, sl=sl):
                        P.w(t, "vS")
                        P.w(t, "sS", free_ps)
                        for j in range(16):
                            ins = t.transpose(ps_t[:, j * 128:(j + 1) * 128],
                                              xnb[:, sl * HID + j * 128: sl * HID + (j + 1) * 128], id_bf[:])
                        P.inc(ins, "tS", 1)
                        free_xn[sl] = P.c["tS"]
                    @block.scalar
                    def _(s, i=i):
                        P.w(s, "tS")
                        xv = xnT.rearrange("p (k t) -> p k t", t=T)
                        P.inc(s.copy(xv[:, :, i * 128:(i + 1) * 128],
                                     ps_t.rearrange("p (k t) -> p k t", t=128)), "sS", 1)
                    free_ps = P.c["sS"]
                if collect_xsum:
                    @block.tensor
                    def _(t):
                        P.w(t, "vS")
                        P.inc(t.transpose(ps_f[0:16, :], xsc[:], id_f[:]), "tS", 1)
                    @block.vector
                    def _(v):
                        P.w(v, "tS")
                        P.inc(v.tensor_copy(xrow[:], ps_f[0:16, :]), "vS", 1)
                    @block.scalar
                    def _(s):
                        P.w(s, "vS")
                        P.inc(s.dma_start(out=scrX, in_=xrow[:]), "d2", 16)
                        P.w(s, "d2")
                        P.inc(s.dma_start(out=xsr[:], in_=scrX.rearrange("i p -> (i p)")[None, :]), "d2", 16)
                    @block.vector
                    def _(v):
                        P.w(v, "d2")
                        P.inc(v.tensor_copy(xsrb[:], xsr[:]), "vS", 1)

        def emit_inprojT(is_ctx, xnT):
            with nc.sbuf_tensor("ipt_w", [128, 2 * 2048], BF16, allow_name_mangling=True) as wst, \
                 nc.sbuf_tensor("ipt_xr", [128, 2 * 2051], F32, allow_name_mangling=True) as xbcr, \
                 nc.sbuf_tensor("ipt_cv", [128, 2048], F32, allow_name_mangling=True) as cvt, \
                 nc.sbuf_tensor("ipt_cs", [128, 2 * 2048], BF16, allow_name_mangling=True) as convs, \
                 nc.sbuf_tensor("ipt_t1", [64, 2048], F32, allow_name_mangling=True) as tA, \
                 nc.sbuf_tensor("ipt_t2", [64, 2048], F32, allow_name_mangling=True) as tB, \
                 nc.psum_tensor("ipt_ps", [128, 2 * 2048], F32, allow_name_mangling=True) as ps_ip:
                barrier()
                stat_rdy = [0] * 35
                free_w = [0, 0]
                free_ps = [0, 0]
                free_cv = 0
                free_cs = [0, 0]
                mm_done = [0] * 35
                ev_done = [0] * 35
                xv = xnT.rearrange("p (k t) -> p k t", t=T)
                for j in range(36):
                    if j < 35:
                        cw = 128 if j < 34 else 64
                        @block.gpsimd
                        def _(g, j=j, cw=cw):
                            P.w(g, "tS", free_w[j % 2])
                            last = 0
                            for k in range(NKD):
                                last = P.inc(g.dma_start(
                                    out=wst[:, (j % 2) * 2048 + k * 128:(j % 2) * 2048 + k * 128 + cw],
                                    in_=wip[k * 128:(k + 1) * 128, INTER + j * 128: INTER + j * 128 + cw]), "g0" if j % 2 == 0 else "g1", 16)
                            stat_rdy[j] = last
                    if j == 0:
                        continue
                    jj = j - 1
                    cw = 128 if jj < 34 else 64
                    sl = jj % 2
                    @block.tensor
                    def _(t, jj=jj, cw=cw, sl=sl):
                        P.w(t, "g0" if jj % 2 == 0 else "g1", stat_rdy[jj])
                        P.w(t, "sS", free_ps[sl])
                        for k in range(NKD):
                            for n in range(4):
                                ins = t.matmul(ps_ip[0:cw, sl * 2048 + n * 512: sl * 2048 + (n + 1) * 512],
                                               wst[:, sl * 2048 + k * 128: sl * 2048 + k * 128 + cw],
                                               xv[:, k, n * 512:(n + 1) * 512],
                                               start=(k == 0), stop=(k == NKD - 1))
                        mm_done[jj] = P.inc(ins, "tS", 1)
                        free_w[sl] = P.c["tS"]
                    @block.scalar
                    def _(s, jj=jj, cw=cw, sl=sl):
                        P.w(s, "tS", mm_done[jj])
                        free_ps[sl] = P.inc(s.copy(xbcr[0:cw, sl * 2051 + 3: sl * 2051 + 2051],
                                                   ps_ip[0:cw, sl * 2048: sl * 2048 + 2048]), "sS", 1)
                        ev_done[jj] = P.c["sS"]
                    if jj < 34:
                        @block.vector
                        def _(v, jj=jj, sl=sl):
                            P.w(v, "sS", ev_done[jj])
                            P.inc(v.tensor_copy(xbcr[:, sl * 2051: sl * 2051 + 3], hist[:, jj * 3:(jj + 1) * 3]), "vS", 1)
                            P.w(v, "vS")
                            if is_ctx:
                                P.inc(v.tensor_copy(hist[:, jj * 3:(jj + 1) * 3],
                                              xbcr[:, sl * 2051 + 2048: sl * 2051 + 2051]), "vS", 1)
                                P.w(v, "vS")
                            P.w(v, "sS", free_cv)
                            P.inc(v.tensor_scalar(cvt[:], xbcr[:, sl * 2051: sl * 2051 + 2048],
                                            convw_s[:, jj * 4:jj * 4 + 1], None, ALU.mult), "vS", 1)
                            P.w(v, "vS")
                            for kk in range(1, 4):
                                P.inc(v.scalar_tensor_tensor(cvt[:], xbcr[:, sl * 2051 + kk: sl * 2051 + kk + 2048],
                                                       convw_s[:, jj * 4 + kk:jj * 4 + kk + 1], cvt[:],
                                                       ALU.mult, ALU.add), "vS", 1)
                                P.w(v, "vS")
                        @block.scalar
                        def _(s, jj=jj, sl=sl):
                            P.w(s, "vS")
                            if jj == 32:
                                dst = Bt[:]
                            elif jj == 33:
                                dst = Ctn[:]
                            else:
                                P.w(s, "w0" if sl == 0 else "w1", free_cs[sl])
                                dst = convs[:, sl * 2048:(sl + 1) * 2048]
                            free_cv = P.inc(s.activation(dst, cvt[:], AF.Silu,
                                                         bias=convb_c[:, jj:jj + 1]), "sS", 1)
                            if jj < 32:
                                P.w(s, "sS")
                                free_cs[sl] = P.inc(s.dma_start(
                                    out=x_sp[jj * 128:(jj + 1) * 128, :],
                                    in_=convs[:, sl * 2048:(sl + 1) * 2048]), "w0" if sl == 0 else "w1", 16)
                    else:
                        @block.scalar
                        def _(s, sl=sl):
                            P.w(s, "sS", ev_done[34])
                            src = xbcr[0:64, sl * 2051 + 3: sl * 2051 + 2051]
                            P.inc(s.activation(tA[:], src, AF.Abs, bias=dtb_c[:]), "sS", 1)
                            P.w(s, "sS")
                            P.inc(s.activation(tB[:], tA[:], AF.Exp, scale=-1.0), "sS", 1)
                            P.w(s, "sS")
                            P.inc(s.activation(tA[:], tB[:], AF.Ln, bias=1.0), "sS", 1)
                            P.w(s, "sS")
                            P.inc(s.activation(tB[:], src, AF.Relu, bias=dtb_c[:]), "sS", 1)
                        @block.vector
                        def _(v):
                            P.w(v, "sS")
                            P.inc(v.tensor_tensor(dts[:], tA[:], tB[:], ALU.add), "vS", 1)
                            P.w(v, "vS")
                            P.inc(v.tensor_scalar(adts[:], dts[:], A_c[:], None, ALU.mult), "vS", 1)

        def emit_prepass(dtT_all, dsT_all, etb_all, sdo_all, negA_all, need_flats):
            with nc.sbuf_tensor("pp_aT", [128, 64], F32, allow_name_mangling=True) as adtT_s, \
                 nc.sbuf_tensor("pp_tc", [64, 1], F32, allow_name_mangling=True) as tc, \
                 nc.sbuf_tensor("pp_tr", [1, 64], F32, allow_name_mangling=True) as totr, \
                 nc.sbuf_tensor("pp_acf", [64, 128], F32, allow_name_mangling=True) as acf, \
                 nc.sbuf_tensor("pp_ach", [64, 128], BF16, allow_name_mangling=True) as ach, \
                 nc.sbuf_tensor("pp_acg", [64, 128], F32, allow_name_mangling=True) as achf, \
                 nc.sbuf_tensor("pp_acl", [64, 128], BF16, allow_name_mangling=True) as acl, \
                 nc.psum_tensor("pp_ps", [128, 512], F32, allow_name_mangling=True) as pp:
                barrier()
                for c in range(NCH):
                    @block.tensor
                    def _(t, c=c):
                        P.w(t, "vS"); P.w(t, "sS")
                        t.transpose(pp[:, 0:64], dts[0:64, c * 128:(c + 1) * 128], id_f[0:64, 0:64])
                        P.inc(t.transpose(pp[:, 64:128], adts[0:64, c * 128:(c + 1) * 128],
                                          id_f[0:64, 0:64]), "tS", 1)
                    @block.vector
                    def _(v, c=c):
                        P.w(v, "tS")
                        v.tensor_copy(dtT_all[:, c * 64:(c + 1) * 64], pp[:, 0:64])
                        v.tensor_copy(adtT_s[:], pp[:, 64:128])
                        P.inc(v.reduce_sum(tc[:], adts[0:64, c * 128:(c + 1) * 128], axis=AX), "vS", 1)
                    @block.tensor
                    def _(t):
                        P.w(t, "vS")
                        P.inc(t.matmul(pp[0:1, 128:192], tc[:], id_f[0:64, 0:64], start=True, stop=True), "tS", 1)
                    @block.vector
                    def _(v):
                        P.w(v, "tS")
                        P.inc(v.tensor_copy(totr[:], pp[0:1, 128:192]), "vS", 1)
                    @block.tensor
                    def _(t):
                        P.w(t, "vS")
                        t.matmul(pp[:, 192:256], triN[:], adtT_s[:], start=True, stop=False)
                        t.matmul(pp[:, 192:256], ones_r[:], totr[:], start=False, stop=True)
                        t.matmul(pp[:, 256:320], ones_r[:], totr[:], start=True, stop=True)
                        t.matmul(pp[:, 320:384], triN[:], adtT_s[:], start=True, stop=True)
                        P.inc(t.matmul(pp[0:64, 384:512], adtT_s[:], triP[:], start=True, stop=True), "tS", 1)
                    @block.scalar
                    def _(s, c=c):
                        P.w(s, "tS")
                        s.activation(dsT_all[:, c * 64:(c + 1) * 64], pp[:, 192:256], AF.Exp)
                        P.inc(s.activation(etb_all[:, c * 64:(c + 1) * 64], pp[:, 256:320], AF.Exp), "sS", 1)
                    @block.vector
                    def _(v, c=c):
                        P.w(v, "tS")
                        v.tensor_copy(negA_all[:, c * 64:(c + 1) * 64], pp[:, 320:384])
                        P.inc(v.tensor_copy(acf[:], pp[0:64, 384:512]), "vS", 1)
                    @block.scalar
                    def _(s, c=c):
                        P.w(s, "vS")
                        P.inc(s.activation(sdo_all[:, c * 64:(c + 1) * 64],
                                           negA_all[:, c * 64:(c + 1) * 64], AF.Exp, scale=-1.0), "sS", 1)
                    if need_flats:
                        @block.vector
                        def _(v):
                            P.w(v, "vS")
                            P.w(v, "w0")
                            P.inc(v.tensor_copy(ach[:], acf[:]), "vS", 1)
                        @block.scalar
                        def _(s):
                            P.w(s, "vS")
                            P.inc(s.copy(achf[:], ach[:]), "sS", 1)
                        @block.vector
                        def _(v):
                            P.w(v, "sS")
                            P.inc(v.tensor_tensor(acl[:], acf[:], achf[:], ALU.subtract), "vS", 1)
                        @block.scalar
                        def _(s, c=c):
                            P.w(s, "vS")
                            P.inc(s.dma_start(out=scrH[c:c + 1, :].rearrange("a (h l) -> (a h) l", l=128),
                                              in_=ach[:]), "w0", 16)
                            P.inc(s.dma_start(out=scrL[c:c + 1, :].rearrange("a (h l) -> (a h) l", l=128),
                                              in_=acl[:]), "w0", 16)

        def emit_ssd(full, dtT_all, dsT_all, etb_all, sdo_all, negA_all):
            with ExitStack() as ss:
                sbl = lambda name, shape, dt: ss.enter_context(nc.sbuf_tensor(name, shape, dt, allow_name_mangling=True))
                psl = lambda name, shape, dt: ss.enter_context(nc.psum_tensor(name, shape, dt, allow_name_mangling=True))
                xch = sbl("sd_xch", [128, 2 * INTER], BF16)
                xchT = sbl("sd_xchT", [128, INTER], BF16)
                xd = sbl("sd_xd", [128, INTER], BF16)
                Bsb = sbl("sd_Bsb", [128, 128], BF16)
                t512 = sbl("sd_t5", [128, 512], F32)
                pS = psl("sd_pS", [128, 512], F32)
                pxt = psl("sd_pxt", [128, 1024], BF16)
                barrier()
                if full:
                    xrd = sbl("sd_xrd", [128, INTER], BF16)
                    cb_b = sbl("sd_cb", [128, 128], BF16)
                    Eb = sbl("sd_E", [128, 1024], BF16)
                    Mb = sbl("sd_M", [128, 1024], BF16)
                    flhl = sbl("sd_f", [1, 2 * 2048], BF16)
                    ybuf = sbl("sd_y", [128, INTER], F32)
                    sgc = sbl("sd_sg", [128, INTER], BF16)
                    sqf = sbl("sd_sq", [128, INTER], F32)
                    ygb = sbl("sd_yg", [128, INTER], BF16)
                    ygT = sbl("sd_ygT", [128, INTER], BF16)
                    ssm = sbl("sd_sm", [128, 8], F32)
                    Rbf = sbl("sd_Rb", [128, INTER], BF16)
                    psE = psl("sd_pE", [128, 2048], F32)
                    pyd = psl("sd_pyd", [128, 512], F32)
                    pyo = psl("sd_pyo", [128, 512], F32)
                xch_rdy = [0] * NCH
                sg_rdy = [0] * NCH
                xch_free = [0, 0]
                sg_free = 0
                pass
                NG = NCH * 8
                flat_rdy = [0] * (NG + 2)
                flat_free = [0, 0]
                rbf_rdy = [0]
                e_free = [0, 0]
                if full:
                    @block.scalar
                    def _(s):
                        P.w(s, "vS")
                        rbf_rdy[0] = P.inc(s.copy(Rbf[:], R[:]), "sS", 1)
                    @block.sync
                    def _(sync):
                        pass
                        for G in range(2):
                            par = G % 2
                            sem = "d3" if par == 0 else "d4"
                            cG, gG = G // 8, G % 8
                            P.inc(sync.dma_start(out=flhl[0:1, par * 2048: par * 2048 + 1024],
                                           in_=scrH[cG:cG + 1, gG * 1024:(gG + 1) * 1024]), sem, 16)
                            flat_rdy[G] = P.inc(sync.dma_start(
                                out=flhl[0:1, par * 2048 + 1024: par * 2048 + 2048],
                                in_=scrL[cG:cG + 1, gG * 1024:(gG + 1) * 1024]), sem, 16)
                for c in range(NCH + 1):
                    if c < NCH:
                        sl = c % 2
                        @block.sync
                        def _(sync, c=c, sl=sl):
                            if c == 0:
                                pass
                            P.w(sync, "vS", xch_free[sl])
                            xvv = x_sp.rearrange("(i p) j -> p i j", p=128)
                            last = 0
                            for q in range(4):
                                last = P.inc(sync.dma_start(
                                    out=xch.rearrange("p (s i j) -> p s i j", s=2, j=128)[:, sl, q * 8:(q + 1) * 8, :],
                                    in_=xvv[:, q * 8:(q + 1) * 8, c * 128:(c + 1) * 128]), "d0" if sl == 0 else "d1", 16)
                            xch_rdy[c] = last
                    if c == 0:
                        continue
                    cc = c - 1
                    if full:
                        @block.sync
                        def _(sync, cc=cc):
                            P.w(sync, "vS", sg_free)
                            sg_rdy[cc] = P.inc(sync.dma_start(out=sgc[:],
                                                              in_=sg_sp[cc * 128:(cc + 1) * 128, :]), "d2", 16)
                    sl = cc % 2
                    skip_state = full and (cc == NCH - 1)
                    for q in range(4):
                        @block.tensor
                        def _(t, cc=cc, sl=sl, q=q):
                            P.w(t, "d0" if sl == 0 else "d1", xch_rdy[cc])
                            P.w(t, "vS")
                            for j8 in range(8):
                                ins = t.transpose(pxt[:, j8 * 128:(j8 + 1) * 128],
                                                  xch[:, sl * INTER + (q * 8 + j8) * 128:
                                                      sl * INTER + (q * 8 + j8 + 1) * 128], id_bf[:])
                            P.inc(ins, "tS", 1)
                        @block.vector
                        def _(v, q=q):
                            P.w(v, "tS")
                            P.inc(v.tensor_copy(xchT[:, q * 1024:(q + 1) * 1024], pxt[:]), "vS", 1)
                    @block.vector
                    def _(v, cc=cc, sl=sl, skip_state=skip_state):
                        P.w(v, "vS")
                        x3 = xchT.rearrange("p (h q) -> p h q", q=64)
                        P.inc(v.tensor_tensor(xd.rearrange("p (h q) -> p h q", q=64), x3,
                                        dtT_all[:, cc * 64:(cc + 1) * 64][:, :, None].broadcast_to([128, 64, 64]),
                                        ALU.mult), "vS", 1)
                        P.w(v, "vS")
                        if full:
                            P.inc(v.tensor_tensor(xrd.rearrange("p (h q) -> p h q", q=64), x3,
                                            D_b[:, :, None].broadcast_to([128, 64, 64]), ALU.mult), "vS", 1)
                            P.w(v, "vS")
                        if not skip_state:
                            P.inc(v.tensor_tensor(
                                xch.rearrange("p (s f) -> p s f", s=2)[:, sl, :].rearrange("p (h q) -> p h q", q=64),
                                xd.rearrange("p (h q) -> p h q", q=64),
                                dsT_all[:, cc * 64:(cc + 1) * 64][:, :, None].broadcast_to([128, 64, 64]),
                                ALU.mult), "vS", 1)
                    xdd_ap = xch[:, sl * INTER:(sl + 1) * INTER]
                    @block.tensor
                    def _(t, cc=cc):
                        P.w(t, "vS")
                        P.inc(t.transpose(pxt[:, 0:128], Bt[:, cc * 128:(cc + 1) * 128], id_bf[:]), "tS", 1)
                    @block.vector
                    def _(v):
                        P.w(v, "tS")
                        P.inc(v.tensor_copy(Bsb[:], pxt[:, 0:128]), "vS", 1)
                    if full:
                        @block.tensor
                        def _(t, cc=cc):
                            P.w(t, "vS"); P.w(t, "sS")
                            P.inc(t.matmul(psE[:, 1024:1152], Bt[:, cc * 128:(cc + 1) * 128],
                                           Ctn[:, cc * 128:(cc + 1) * 128], start=True, stop=True), "tS", 1)
                        @block.scalar
                        def _(s):
                            P.w(s, "tS")
                            P.inc(s.copy(cb_b[:], psE[:, 1024:1152]), "sS", 1)
                        e_free[0] = e_free[1] = P.c["sS"]
                    for g in range(8):
                        G = cc * 8 + g
                        par = g % 2
                        if full:
                            @block.tensor
                            def _(t, cc=cc, sl=sl, g=g, par=par, G=G):
                                P.w(t, "d3" if par == 0 else "d4", flat_rdy[G])
                                P.w(t, "sS", e_free[par])
                                for i in range(8):
                                    h = g * 8 + i
                                    dst = psE[:, par * 1024 + i * 128: par * 1024 + (i + 1) * 128]
                                    t.matmul(dst, id_bf[:], maskT[:], start=(i % 4 == 0), stop=False)
                                    t.matmul(dst, ones_rb[:],
                                             flhl[0:1, par * 2048 + i * 128: par * 2048 + (i + 1) * 128],
                                             start=False, stop=False)
                                    ins = t.matmul(dst, ones_rb[:],
                                                   flhl[0:1, par * 2048 + 1024 + i * 128: par * 2048 + 1024 + (i + 1) * 128],
                                                   start=False, stop=(i % 4 == 3))
                                flat_free[par] = P.inc(ins, "tS", 1)
                            if G + 2 < NG:
                                @block.sync
                                def _(sync, G=G, par=par):
                                    P.w(sync, "tS", flat_free[par])
                                    G2 = G + 2
                                    cG, gG = G2 // 8, G2 % 8
                                    P.inc(sync.dma_start(out=flhl[0:1, par * 2048: par * 2048 + 1024],
                                                   in_=scrH[cG:cG + 1, gG * 1024:(gG + 1) * 1024]), "d3" if par == 0 else "d4", 16)
                                    flat_rdy[G2] = P.inc(sync.dma_start(
                                        out=flhl[0:1, par * 2048 + 1024: par * 2048 + 2048],
                                        in_=scrL[cG:cG + 1, gG * 1024:(gG + 1) * 1024]), "d3" if par == 0 else "d4", 16)
                            @block.scalar
                            def _(s, cc=cc, g=g, par=par):
                                P.w(s, "tS")
                                P.w(s, "vS")
                                for i in range(8):
                                    h = g * 8 + i
                                    ins = s.activation(Eb[:, i * 128:(i + 1) * 128],
                                                       psE[:, par * 1024 + i * 128: par * 1024 + (i + 1) * 128],
                                                       AF.Exp, bias=negA_all[:, cc * 64 + h: cc * 64 + h + 1])
                                e_free[par] = P.inc(ins, "sS", 1)
                            @block.vector
                            def _(v):
                                P.w(v, "sS")
                                P.w(v, "tS")
                                P.inc(v.tensor_tensor(Mb.rearrange("p (a f) -> p a f", f=128),
                                                      Eb.rearrange("p (a f) -> p a f", f=128),
                                                      cb_b[:, None, :].broadcast_to([128, 8, 128]),
                                                      ALU.mult), "vS", 1)
                        @block.tensor
                        def _(t, cc=cc, g=g, skip_state=skip_state):
                            P.w(t, "vS")
                            if full:
                                P.w(t, "sS", rbf_rdy[0])
                            ins = None
                            for i in range(8):
                                h = g * 8 + i
                                if full:
                                    t.matmul(pyd[:, i * 64:(i + 1) * 64], Mb[:, i * 128:(i + 1) * 128],
                                             xd[:, h * 64:(h + 1) * 64], start=(i == 0), stop=False)
                                    t.matmul(pyd[:, i * 64:(i + 1) * 64], id_bf[:],
                                             xrd[:, h * 64:(h + 1) * 64], start=False, stop=(i == 7))
                                    ins = t.matmul(pyo[:, i * 64:(i + 1) * 64], Ctn[:, cc * 128:(cc + 1) * 128],
                                                   Rbf[:, h * 64:(h + 1) * 64], start=(i == 0), stop=(i == 7))
                                if not skip_state:
                                    ins = t.matmul(pS[:, i * 64:(i + 1) * 64], Bsb[:],
                                                   xdd_ap[:, h * 64:(h + 1) * 64], start=(i == 0), stop=(i == 7))
                            P.inc(ins, "tS", 1)
                        @block.vector
                        def _(v, cc=cc, g=g, skip_state=skip_state):
                            P.w(v, "tS")
                            if full:
                                P.inc(v.tensor_tensor(t512.rearrange("p (h q) -> p h q", q=64),
                                                pyo.rearrange("p (h q) -> p h q", q=64),
                                                sdo_all[:, cc * 64 + g * 8: cc * 64 + (g + 1) * 8][:, :, None].broadcast_to([128, 8, 64]),
                                                ALU.mult), "vS", 1)
                                P.w(v, "vS")
                                P.inc(v.tensor_tensor(ybuf[:, g * 512:(g + 1) * 512], t512[:], pyd[:], ALU.add), "vS", 1)
                                P.w(v, "vS")
                            if not skip_state:
                                P.inc(v.tensor_tensor(t512.rearrange("p (h q) -> p h q", q=64),
                                                R[:, g * 512:(g + 1) * 512].rearrange("p (h q) -> p h q", q=64),
                                                etb_all[:, cc * 64 + g * 8: cc * 64 + (g + 1) * 8][:, :, None].broadcast_to([128, 8, 64]),
                                                ALU.mult), "vS", 1)
                                P.w(v, "vS")
                                P.inc(v.tensor_tensor(R[:, g * 512:(g + 1) * 512], t512[:], pS[:], ALU.add), "vS", 1)
                    xch_free[sl] = P.c["vS"]
                    if not full:
                        continue
                    if not skip_state:
                        @block.scalar
                        def _(s):
                            P.w(s, "vS")
                            rbf_rdy[0] = P.inc(s.copy(Rbf[:], R[:]), "sS", 1)
                    @block.scalar
                    def _(s):
                        P.w(s, "vS")
                        P.inc(s.activation(sqf[:], ybuf[:], AF.Square), "sS", 1)
                    @block.vector
                    def _(v):
                        P.w(v, "sS")
                        P.inc(v.reduce_sum(ssm[:, 0:1], sqf[:], axis=AX), "vS", 1)
                        P.w(v, "vS")
                        P.inc(v.tensor_scalar(ssm[:, 1:2], ssm[:, 0:1], 1.0 / INTER, EPS,
                                              ALU.mult, ALU.add), "vS", 1)
                    @block.scalar
                    def _(s):
                        P.w(s, "vS")
                        P.inc(s.activation(ssm[:, 2:3], ssm[:, 1:2], AF.Ln), "sS", 1)
                        P.w(s, "sS")
                        P.inc(s.activation(ssm[:, 3:4], ssm[:, 2:3], AF.Exp, scale=-0.5), "sS", 1)
                    @block.vector
                    def _(v, cc=cc):
                        P.w(v, "sS")
                        P.w(v, "d2", sg_rdy[cc])
                        P.inc(v.tensor_scalar(sqf[:], ybuf[:], ssm[:, 3:4], None, ALU.mult), "vS", 1)
                        P.w(v, "vS")
                        P.inc(v.tensor_tensor(ygb[:], sqf[:], sgc[:], ALU.mult), "vS", 1)
                    sg_free = P.c["vS"]
                    for q in range(4):
                        @block.tensor
                        def _(t, q=q):
                            P.w(t, "vS")
                            for j8 in range(8):
                                ins = t.transpose(pxt[:, j8 * 128:(j8 + 1) * 128],
                                                  ygb[:, (q * 8 + j8) * 128:(q * 8 + j8 + 1) * 128], id_bf[:])
                            P.inc(ins, "tS", 1)
                        @block.vector
                        def _(v, q=q):
                            P.w(v, "tS")
                            P.w(v, "w0")
                            P.inc(v.tensor_copy(ygT[:, q * 1024:(q + 1) * 1024], pxt[:]), "vS", 1)
                    @block.scalar
                    def _(s, cc=cc):
                        P.w(s, "vS")
                        yvv = ygt_sp.rearrange("(i p) j -> p i j", p=128)
                        for q in range(4):
                            P.inc(s.dma_start(out=yvv[:, q * 8:(q + 1) * 8, cc * 128:(cc + 1) * 128],
                                              in_=ygT.rearrange("p (i j) -> p i j", j=128)[:, q * 8:(q + 1) * 8, :]),
                                  "w0", 16)
                if not full:
                    @block.vector
                    def _(v):
                        P.w(v, "vS")
                        P.inc(v.tensor_scalar(R[:], R[:], rmask_c[:], None, ALU.mult), "vS", 1)

        def emit_p2(xnT):
            with nc.sbuf_tensor("p2_w", [128, NKD * 1024], BF16, allow_name_mangling=True) as wq, \
                 nc.sbuf_tensor("p2_t", [128, 1024], F32, allow_name_mangling=True) as tmpf, \
                 nc.sbuf_tensor("p2_s", [128, 2 * 1024], BF16, allow_name_mangling=True) as sgt, \
                 nc.psum_tensor("p2_ps", [128, 2 * 1024], F32, allow_name_mangling=True) as ps2:
                barrier()
                free_w = 0
                w_rdy = [0] * 4
                free_sg = [0, 0]
                free_ps = [0, 0]
                free_tmp = 0
                xv = xnT.rearrange("p (k t) -> p k t", t=T)
                wv2 = wip.rearrange("(k p) c -> p k c", p=128)
                for q in range(4):
                    @block.gpsimd
                    def _(g, q=q):
                        P.w(g, "tS", free_w)
                        last = 0
                        for piece in range(4):
                            last = P.inc(g.dma_start(
                                out=wq.rearrange("p (k c) -> p k c", c=1024)[:, piece * 4:(piece + 1) * 4, :],
                                in_=wv2[:, piece * 4:(piece + 1) * 4, q * 1024:(q + 1) * 1024]), "g0", 16)
                        w_rdy[q] = last
                    for t_i in range(NTT):
                        sl = t_i % 2
                        @block.tensor
                        def _(t, q=q, t_i=t_i, sl=sl):
                            P.w(t, "g0", w_rdy[q])
                            P.w(t, "sS", free_ps[sl])
                            for k in range(NKD):
                                for n in range(2):
                                    ins = t.matmul(ps2[:, sl * 1024 + n * 512: sl * 1024 + (n + 1) * 512],
                                                   xv[:, k, t_i * 128:(t_i + 1) * 128],
                                                   wq.rearrange("p (k c) -> p k c", c=1024)[:, k, n * 512:(n + 1) * 512],
                                                   start=(k == 0), stop=(k == NKD - 1))
                            P.inc(ins, "tS", 1)
                        if t_i == NTT - 1:
                            free_w = P.c["tS"]
                        @block.scalar
                        def _(s, sl=sl):
                            P.w(s, "tS")
                            P.w(s, "vS", free_tmp)
                            free_ps[sl] = P.inc(s.activation(tmpf[:], ps2[:, sl * 1024:(sl + 1) * 1024],
                                                             AF.Silu), "sS", 1)
                        @block.vector
                        def _(v, q=q, sl=sl):
                            P.w(v, "sS")
                            P.w(v, "w0" if sl == 0 else "w1", free_sg[sl])
                            P.inc(v.tensor_tensor(sgt[:, sl * 1024:(sl + 1) * 1024], tmpf[:],
                                                  NWb[:, q * 1024:(q + 1) * 1024], ALU.mult), "vS", 1)
                        free_tmp = P.c["vS"]
                        @block.scalar
                        def _(s, q=q, t_i=t_i, sl=sl):
                            P.w(s, "vS")
                            free_sg[sl] = P.inc(s.dma_start(
                                out=sg_sp[t_i * 128:(t_i + 1) * 128, q * 1024:(q + 1) * 1024],
                                in_=sgt[:, sl * 1024:(sl + 1) * 1024]), "w0" if sl == 0 else "w1", 16)

        def emit_p4(xnT, xsrb):
            with nc.sbuf_tensor("p4_wg", [128, 2 * 2048], BF16, allow_name_mangling=True) as wg, \
                 nc.sbuf_tensor("p4_wu", [128, 2 * 2048], BF16, allow_name_mangling=True) as wu, \
                 nc.sbuf_tensor("p4_g", [128, HID], F32, allow_name_mangling=True) as gsf, \
                 nc.sbuf_tensor("p4_g2", [128, HID], F32, allow_name_mangling=True) as gs2, \
                 nc.sbuf_tensor("p4_gu", [128, 2 * HID], BF16, allow_name_mangling=True) as gub, \
                 nc.psum_tensor("p4_ps", [128, 2 * 2048], F32, allow_name_mangling=True) as ps4:
                barrier()
                w_rdy = [0] * NHT
                free_w = [0, 0]
                free_gu = [0, 0]
                mmg = [0] * NHT
                mmu = [0] * NHT
                xv = xnT.rearrange("p (k t) -> p k t", t=T)
                for ht in range(NHT + 1):
                    if ht < NHT:
                        @block.gpsimd
                        def _(g, ht=ht):
                            P.w(g, "tS", free_w[ht % 2])
                            last = 0
                            for k in range(NKD):
                                last = P.inc(g.dma_start(
                                    out=wg[:, (ht % 2) * 2048 + k * 128:(ht % 2) * 2048 + (k + 1) * 128],
                                    in_=wgt[k * 128:(k + 1) * 128, ht * 128:(ht + 1) * 128]), "g0" if ht % 2 == 0 else "g1", 16)
                                last = P.inc(g.dma_start(
                                    out=wu[:, (ht % 2) * 2048 + k * 128:(ht % 2) * 2048 + (k + 1) * 128],
                                    in_=wut[k * 128:(k + 1) * 128, ht * 128:(ht + 1) * 128]), "g0" if ht % 2 == 0 else "g1", 16)
                            w_rdy[ht] = last
                    if ht == 0:
                        continue
                    hh = ht - 1
                    sl = hh % 2
                    @block.tensor
                    def _(t, hh=hh, sl=sl):
                        P.w(t, "g0" if hh % 2 == 0 else "g1", w_rdy[hh])
                        P.w(t, "vS")
                        for (wbuf, half) in [(wg, 0), (wu, 1)]:
                            for k in range(NKD):
                                for n in range(4):
                                    t.matmul(ps4[:, half * 2048 + n * 512: half * 2048 + (n + 1) * 512],
                                             wbuf[:, sl * 2048 + k * 128: sl * 2048 + (k + 1) * 128],
                                             xv[:, k, n * 512:(n + 1) * 512],
                                             start=(k == 0), stop=False)
                            for n in range(4):
                                ins = t.matmul(ps4[:, half * 2048 + n * 512: half * 2048 + (n + 1) * 512],
                                               n128r[:], xsrb[0:1, n * 512:(n + 1) * 512],
                                               start=False, stop=True)
                            if half == 0:
                                mmg[hh] = P.inc(ins, "tS", 1)
                            else:
                                mmu[hh] = P.inc(ins, "tS", 1)
                        free_w[sl] = P.c["tS"]
                    @block.vector
                    def _(v, hh=hh):
                        P.w(v, "tS", mmg[hh])
                        P.inc(v.tensor_scalar(gsf[:], ps4[:, 0:2048], sg_c[:, hh:hh + 1], None, ALU.mult), "vS", 1)
                    @block.scalar
                    def _(s):
                        P.w(s, "vS")
                        P.inc(s.activation(gs2[:], gsf[:], AF.Silu), "sS", 1)
                    @block.vector
                    def _(v, hh=hh, sl=sl):
                        P.w(v, "tS", mmu[hh])
                        P.w(v, "sS")
                        P.inc(v.tensor_scalar(gsf[:], ps4[:, 2048:4096], su_c[:, hh:hh + 1], None, ALU.mult), "vS", 1)
                        P.w(v, "vS")
                        P.w(v, "w0" if sl == 0 else "w1", free_gu[sl])
                        P.inc(v.tensor_tensor(gub[:, sl * HID:(sl + 1) * HID], gs2[:], gsf[:], ALU.mult), "vS", 1)
                    @block.scalar
                    def _(s, hh=hh, sl=sl):
                        P.w(s, "vS")
                        free_gu[sl] = P.inc(s.dma_start(out=gu_sp[hh * 128:(hh + 1) * 128, :],
                                                        in_=gub[:, sl * HID:(sl + 1) * HID]), "w0" if sl == 0 else "w1", 16)

        def emit_p5():
            KH = 11
            with nc.sbuf_tensor("p5_wd", [128, KH * 2048], BF16, allow_name_mangling=True) as wdb, \
                 nc.sbuf_tensor("p5_gs", [128, 2 * KH * 128], BF16, allow_name_mangling=True) as gst, \
                 nc.sbuf_tensor("p5_pl", [128, 2 * HID], F32, allow_name_mangling=True) as pld, \
                 nc.sbuf_tensor("p5_tf", [128, HID], F32, allow_name_mangling=True) as tf5, \
                 nc.sbuf_tensor("p5_gc", [128, 20], F32, allow_name_mangling=True) as gcol, \
                 nc.sbuf_tensor("p5_pd", [128, 2 * HID], F32, allow_name_mangling=True) as pdst, \
                 nc.sbuf_tensor("p5_gb", [128, 2 * HID], BF16, allow_name_mangling=True) as gb5, \
                 nc.sbuf_tensor("p5_sd", [128, HID], F32, allow_name_mangling=True) as SDb, \
                 nc.sbuf_tensor("p5_r", [1, HID], F32, allow_name_mangling=True) as r_sd5, \
                 nc.psum_tensor("p5_ps", [128, 2 * 1024], F32, allow_name_mangling=True) as ps5, \
                 nc.psum_tensor("p5_pg", [128, 512], F32, allow_name_mangling=True) as psg5:
                barrier()
                pass
                @block.sync
                def _(sync):
                    P.inc(sync.dma_start(out=r_sd5[:], in_=sdr), "d0", 16)
                sd_rdy = P.c["d0"]
                for jj in range(HID // 512):
                    @block.tensor
                    def _(t, jj=jj):
                        P.w(t, "d0", sd_rdy)
                        P.w(t, "vS")
                        P.inc(t.matmul(psg5[:, 0:512], ones_r[:],
                                       r_sd5[0:1, jj * 512:(jj + 1) * 512], start=True, stop=True), "tS", 1)
                    @block.vector
                    def _(v, jj=jj):
                        P.w(v, "tS")
                        P.inc(v.tensor_copy(SDb[:, jj * 512:(jj + 1) * 512], psg5[:, 0:512]), "vS", 1)
                wv5 = wdt.rearrange("(k p) c -> p k c", p=128)
                free_gst = [0, 0]
                free_pl = [0, 0]
                free_pd = [0, 0]
                free_gb = [0, 0]
                ev_ps = [("vS", 0), ("vS", 0)]
                for pz in range(4):
                    w_rdy = [0]
                    @block.gpsimd
                    def _(g, pz=pz):
                        P.w(g, "tS")
                        for piece in range(3):
                            k0 = piece * 4
                            kn = min(4, KH - k0)
                            w_rdy[0] = P.inc(g.dma_start(
                                out=wdb.rearrange("p (k c) -> p k c", c=2048)[:, k0:k0 + kn, :],
                                in_=wv5[:, pz * KH + k0: pz * KH + k0 + kn, :]), "g0", 16)
                    if pz > 0:
                        @block.sync
                        def _(sync):
                            P.w(sync, "w0"); P.w(sync, "w1")
                    gst_rdy = [0] * NTT
                    pl_rdy = [0] * NTT
                    for t_o in range(NTT + 1):
                        if t_o < NTT:
                            t_i, sl = t_o, t_o % 2
                            @block.sync
                            def _(sync, pz=pz, t_i=t_i, sl=sl):
                                P.w(sync, "tS", free_gst[sl])
                                last = 0
                                for k in range(KH):
                                    last = P.inc(sync.dma_start(
                                        out=gst[:, sl * (KH * 128) + k * 128: sl * (KH * 128) + (k + 1) * 128],
                                        in_=gu_sp[(pz * KH + k) * 128:(pz * KH + k + 1) * 128,
                                                  t_i * 128:(t_i + 1) * 128]), "d1" if sl == 0 else "d2", 16)
                                gst_rdy[t_i] = last
                                if pz > 0:
                                    P.w(sync, "vS", free_pl[sl])
                                    pl_rdy[t_i] = P.inc(sync.dma_start(
                                        out=pld[:, sl * HID:(sl + 1) * HID],
                                        in_=pp_sp[t_i * 128:(t_i + 1) * 128, :]), "d3" if sl == 0 else "d4", 16)
                        if t_o == 0:
                            continue
                        t_i = t_o - 1
                        sl = t_i % 2
                        for dh in range(2):
                            @block.tensor
                            def _(t, pz=pz, t_i=t_i, sl=sl, dh=dh):
                                P.w(t, "g0", w_rdy[0])
                                P.w(t, "d1" if sl == 0 else "d2", gst_rdy[t_i])
                                P.w(t, ev_ps[sl][0], ev_ps[sl][1])
                                for k in range(KH):
                                    for n in range(2):
                                        ins = t.matmul(
                                            ps5[:, sl * 1024 + n * 512: sl * 1024 + (n + 1) * 512],
                                            gst[:, sl * (KH * 128) + k * 128: sl * (KH * 128) + (k + 1) * 128],
                                            wdb.rearrange("p (k c) -> p k c", c=2048)[:, k, dh * 1024 + n * 512: dh * 1024 + (n + 1) * 512],
                                            start=(k == 0), stop=(k == KH - 1))
                                P.inc(ins, "tS", 1)
                            if pz == 0:
                                @block.scalar
                                def _(s, t_i=t_i, sl=sl, dh=dh):
                                    P.w(s, "tS")
                                    if dh == 0:
                                        P.w(s, "w0" if sl == 0 else "w1", free_pd[sl])
                                    c = P.inc(s.copy(pdst[:, sl * HID + dh * 1024: sl * HID + (dh + 1) * 1024],
                                                     ps5[:, sl * 1024: sl * 1024 + 1024]), "sS", 1)
                                    ev_ps[sl] = ("sS", c)
                            elif pz < 3:
                                @block.vector
                                def _(v, t_i=t_i, sl=sl, dh=dh):
                                    P.w(v, "tS")
                                    P.w(v, "d3" if sl == 0 else "d4", pl_rdy[t_i])
                                    if dh == 0:
                                        P.w(v, "w0" if sl == 0 else "w1", free_pd[sl])
                                    c = P.inc(v.tensor_tensor(
                                        pdst[:, sl * HID + dh * 1024: sl * HID + (dh + 1) * 1024],
                                        ps5[:, sl * 1024: sl * 1024 + 1024],
                                        pld[:, sl * HID + dh * 1024: sl * HID + (dh + 1) * 1024],
                                        ALU.add), "vS", 1)
                                    ev_ps[sl] = ("vS", c)
                                    if dh == 1:
                                        free_pl[sl] = c
                            else:
                                @block.vector
                                def _(v, t_i=t_i, sl=sl, dh=dh):
                                    P.w(v, "tS")
                                    P.w(v, "d3" if sl == 0 else "d4", pl_rdy[t_i])
                                    P.w(v, "sS")
                                    c = P.inc(v.tensor_tensor(
                                        tf5[:, dh * 1024:(dh + 1) * 1024],
                                        ps5[:, sl * 1024: sl * 1024 + 1024],
                                        pld[:, sl * HID + dh * 1024: sl * HID + (dh + 1) * 1024],
                                        ALU.add), "vS", 1)
                                    ev_ps[sl] = ("vS", c)
                                    if dh == 1:
                                        free_pl[sl] = c
                        @block.tensor
                        def _(t, pz=pz, sl=sl):
                            P.w(t, "vS")
                            for k in range(KH):
                                ins = t.matmul(psg5[:, 0:1],
                                               gst[:, sl * (KH * 128) + k * 128: sl * (KH * 128) + (k + 1) * 128],
                                               ones_cb[:], start=(k == 0), stop=(k == KH - 1))
                            P.inc(ins, "tS", 1)
                            free_gst[sl] = P.c["tS"]
                        if pz == 0:
                            @block.vector
                            def _(v, t_i=t_i):
                                P.w(v, "tS")
                                P.inc(v.tensor_copy(gcol[:, t_i:t_i + 1], psg5[:, 0:1]), "vS", 1)
                        else:
                            @block.vector
                            def _(v, t_i=t_i):
                                P.w(v, "tS")
                                P.inc(v.tensor_tensor(gcol[:, t_i:t_i + 1], psg5[:, 0:1],
                                                      gcol[:, t_i:t_i + 1], ALU.add), "vS", 1)
                        if pz < 3:
                            @block.scalar
                            def _(s, pz=pz, t_i=t_i, sl=sl):
                                P.w(s, ev_ps[sl][0], ev_ps[sl][1])
                                free_pd[sl] = P.inc(s.dma_start(out=pp_sp[t_i * 128:(t_i + 1) * 128, :],
                                                                in_=pdst[:, sl * HID:(sl + 1) * HID]), "w0" if sl == 0 else "w1", 16)
                        else:
                            @block.vector
                            def _(v, t_i=t_i, sl=sl):
                                P.w(v, "vS")
                                P.inc(v.tensor_scalar(gcol[:, 17:18], gcol[:, t_i:t_i + 1], 128.0, None, ALU.mult), "vS", 1)
                                P.w(v, "vS")
                                P.inc(v.tensor_scalar(tf5[:], tf5[:], gcol[:, 17:18], None, ALU.subtract), "vS", 1)
                                P.w(v, "vS")
                                P.inc(v.tensor_tensor(tf5[:], tf5[:], SDb[:], ALU.mult), "vS", 1)
                            @block.scalar
                            def _(s, t_i=t_i, sl=sl):
                                P.w(s, "vS")
                                P.w(s, "w2" if sl == 0 else "w3", free_gb[sl])
                                P.inc(s.activation(gb5[:, sl * HID:(sl + 1) * HID], tf5[:], AF.Silu), "sS", 1)
                                P.w(s, "sS")
                                free_gb[sl] = P.inc(s.dma_start(out=g_sp[t_i * 128:(t_i + 1) * 128, :],
                                                                in_=gb5[:, sl * HID:(sl + 1) * HID]), "w2" if sl == 0 else "w3", 16)

        def emit_p7():
            with nc.sbuf_tensor("p7_w", [128, NKD * 2048], BF16, allow_name_mangling=True) as woh, \
                 nc.sbuf_tensor("p7_yg", [128, 2 * 2048], BF16, allow_name_mangling=True) as ygs, \
                 nc.sbuf_tensor("p7_pl", [128, 2 * HID], F32, allow_name_mangling=True) as pol, \
                 nc.sbuf_tensor("p7_g", [128, 2 * HID], BF16, allow_name_mangling=True) as gl7, \
                 nc.sbuf_tensor("p7_x", [128, 2 * HID], F32, allow_name_mangling=True) as xl7, \
                 nc.sbuf_tensor("p7_o", [128, 2 * HID], F32, allow_name_mangling=True) as ob7, \
                 nc.psum_tensor("p7_ps", [128, 2 * 2048], F32, allow_name_mangling=True) as ps7:
                barrier()
                wv7 = wout.rearrange("(k p) c -> p k c", p=128)
                free_st = [0, 0]
                free_ps = [("sS", 0), ("sS", 0)]
                free_ld = [0, 0]
                free_ob = [("w0", 0), ("w1", 0)]
                for half in range(2):
                    w_rdy = [0]
                    @block.gpsimd
                    def _(g, half=half):
                        P.w(g, "tS")
                        last = 0
                        for piece in range(4):
                            last = P.inc(g.dma_start(
                                out=woh.rearrange("p (k c) -> p k c", c=2048)[:, piece * 4:(piece + 1) * 4, :],
                                in_=wv7[:, half * NKD + piece * 4: half * NKD + (piece + 1) * 4, :]), "g0", 16)
                        w_rdy[0] = last
                    st_rdy = [0] * NTT
                    ld_rdy = [0] * NTT
                    for t_o in range(NTT + 1):
                        if t_o < NTT:
                            t_i, sl = t_o, t_o % 2
                            emit_ld = True
                        else:
                            emit_ld = False
                        if emit_ld:
                            @block.sync
                            def _(sync, half=half, t_i=t_i, sl=sl):
                                P.w(sync, "tS", free_st[sl])
                                last = 0
                                for k in range(NKD):
                                    last = P.inc(sync.dma_start(
                                        out=ygs[:, sl * 2048 + k * 128: sl * 2048 + (k + 1) * 128],
                                        in_=ygt_sp[(half * NKD + k) * 128:(half * NKD + k + 1) * 128,
                                                   t_i * 128:(t_i + 1) * 128]), "d0" if sl == 0 else "d1", 16)
                                st_rdy[t_i] = last
                                if half == 1:
                                    if t_i == 0:
                                        P.w(sync, "w0"); P.w(sync, "w1")
                                    P.w(sync, "vS", free_ld[sl])
                                    P.inc(sync.dma_start(out=pol[:, sl * HID:(sl + 1) * HID],
                                                         in_=pp_sp[t_i * 128:(t_i + 1) * 128, :]), "d2" if sl == 0 else "d3", 16)
                                    P.inc(sync.dma_start(out=gl7[:, sl * HID:(sl + 1) * HID],
                                                         in_=g_sp[t_i * 128:(t_i + 1) * 128, :]), "d2" if sl == 0 else "d3", 16)
                                    ld_rdy[t_i] = P.inc(sync.dma_start(out=xl7[:, sl * HID:(sl + 1) * HID],
                                                                       in_=xo[t_i * 128:(t_i + 1) * 128, :]), "d2" if sl == 0 else "d3", 16)
                        if t_o == 0:
                            continue
                        t_i = t_o - 1
                        sl = t_i % 2
                        @block.tensor
                        def _(t, half=half, t_i=t_i, sl=sl):
                            P.w(t, "g0", w_rdy[0])
                            P.w(t, "d0" if sl == 0 else "d1", st_rdy[t_i])
                            P.w(t, free_ps[sl][0], free_ps[sl][1])
                            for k in range(NKD):
                                for n in range(4):
                                    ins = t.matmul(ps7[:, sl * 2048 + n * 512: sl * 2048 + (n + 1) * 512],
                                                   ygs[:, sl * 2048 + k * 128: sl * 2048 + (k + 1) * 128],
                                                   woh.rearrange("p (k c) -> p k c", c=2048)[:, k, n * 512:(n + 1) * 512],
                                                   start=(k == 0), stop=(k == NKD - 1))
                            P.inc(ins, "tS", 1)
                            free_st[sl] = P.c["tS"]
                        if half == 0:
                            @block.scalar
                            def _(s, t_i=t_i, sl=sl):
                                P.w(s, "tS")
                                P.w(s, free_ob[sl][0], free_ob[sl][1])
                                P.inc(s.copy(ob7[:, sl * HID:(sl + 1) * HID], ps7[:, sl * 2048: sl * 2048 + 2048]), "sS", 1)
                                free_ps[sl] = ("sS", P.c["sS"])
                                P.w(s, "sS")
                                sem = "w0" if sl == 0 else "w1"
                                free_ob[sl] = (sem, P.inc(s.dma_start(out=pp_sp[t_i * 128:(t_i + 1) * 128, :],
                                                                      in_=ob7[:, sl * HID:(sl + 1) * HID]), sem, 16))
                        else:
                            @block.vector
                            def _(v, t_i=t_i, sl=sl):
                                P.w(v, "tS")
                                P.w(v, "d2" if sl == 0 else "d3", ld_rdy[t_i])
                                P.w(v, free_ob[sl][0], free_ob[sl][1])
                                obs = ob7[:, sl * HID:(sl + 1) * HID]
                                P.inc(v.tensor_tensor(obs, ps7[:, sl * 2048: sl * 2048 + 2048],
                                                pol[:, sl * HID:(sl + 1) * HID], ALU.add), "vS", 1)
                                P.w(v, "vS")
                                free_ps[sl] = ("vS", P.c["vS"])
                                P.inc(v.tensor_tensor(obs, obs, gl7[:, sl * HID:(sl + 1) * HID], ALU.mult), "vS", 1)
                                P.w(v, "vS")
                                P.inc(v.tensor_tensor(obs, obs, xl7[:, sl * HID:(sl + 1) * HID], ALU.add), "vS", 1)
                                free_ld[sl] = P.c["vS"]
                            @block.scalar
                            def _(s, t_i=t_i, sl=sl):
                                P.w(s, "vS")
                                sem = "w2" if sl == 0 else "w3"
                                free_ob[sl] = (sem, P.inc(s.dma_start(out=outd[t_i * 128:(t_i + 1) * 128, :],
                                                                      in_=ob7[:, sl * HID:(sl + 1) * HID]), sem, 16))

        with nc.sbuf_tensor("xnT", [128, NKD * T], BF16, allow_name_mangling=True) as xnT, \
             nc.sbuf_tensor("xsrb", [1, T], BF16, allow_name_mangling=True) as xsrb:
            emit_ln(xc, False, xnT, xsrb)
            emit_inprojT(True, xnT)
            with nc.sbuf_tensor("c_d1", [128, NCH * 64], F32, allow_name_mangling=True) as dtT_all, \
                 nc.sbuf_tensor("c_d2", [128, NCH * 64], F32, allow_name_mangling=True) as dsT_all, \
                 nc.sbuf_tensor("c_d3", [128, NCH * 64], F32, allow_name_mangling=True) as etb_all, \
                 nc.sbuf_tensor("c_d4", [128, NCH * 64], F32, allow_name_mangling=True) as sdo_all, \
                 nc.sbuf_tensor("c_d5", [128, NCH * 64], F32, allow_name_mangling=True) as negA_all:
                emit_prepass(dtT_all, dsT_all, etb_all, sdo_all, negA_all, False)
                emit_ssd(False, dtT_all, dsT_all, etb_all, sdo_all, negA_all)
            emit_ln(xo, True, xnT, xsrb)
            emit_p2(xnT)
            emit_inprojT(False, xnT)
            emit_p4(xnT, xsrb)
        emit_p5()
        with nc.sbuf_tensor("o_d1", [128, NCH * 64], F32, allow_name_mangling=True) as dtT_all, \
             nc.sbuf_tensor("o_d2", [128, NCH * 64], F32, allow_name_mangling=True) as dsT_all, \
             nc.sbuf_tensor("o_d3", [128, NCH * 64], F32, allow_name_mangling=True) as etb_all, \
             nc.sbuf_tensor("o_d4", [128, NCH * 64], F32, allow_name_mangling=True) as sdo_all, \
             nc.sbuf_tensor("o_d5", [128, NCH * 64], F32, allow_name_mangling=True) as negA_all:
            emit_prepass(dtT_all, dsT_all, etb_all, sdo_all, negA_all, True)
            emit_ssd(True, dtT_all, dsT_all, etb_all, sdo_all, negA_all)
        emit_p7()

        @block.gpsimd
        def _(g):
            pass
            for s in ("d0", "d1", "d2", "d3", "d4", "g0", "g1", "w0", "w1", "w2", "w3", "tS", "vS", "sS"): P.w(g, s)
            pass

    es.close()
    return nc


def kernel(**inputs):
    xs = np.ascontiguousarray(np.asarray(inputs["L0_attn_hidden_states"], dtype=np.float32))
    wip = np.ascontiguousarray(np.asarray(inputs["L0_attn_in_proj_w"], dtype=np.float32))
    convw = np.ascontiguousarray(np.asarray(inputs["L0_attn_conv_w"], dtype=np.float32))
    convb = np.ascontiguousarray(np.asarray(inputs["L0_attn_conv_b"], dtype=np.float32).reshape(1, -1))
    dtb = np.ascontiguousarray(np.asarray(inputs["L0_attn_dt_bias"], dtype=np.float32).reshape(1, -1))
    alog = np.ascontiguousarray(np.asarray(inputs["L0_attn_A_log"], dtype=np.float32).reshape(1, -1))
    dvec = np.ascontiguousarray(np.asarray(inputs["L0_attn_D"], dtype=np.float32).reshape(1, -1))
    nw = np.ascontiguousarray(np.asarray(inputs["L0_attn_norm_weight"], dtype=np.float32).reshape(1, -1))
    wout = np.ascontiguousarray(np.asarray(inputs["L0_attn_out_proj_w"], dtype=np.float32))
    wgt = np.ascontiguousarray(np.asarray(inputs["L0_ffn_wq_gate"], dtype=np.int32).T)
    wut = np.ascontiguousarray(np.asarray(inputs["L0_ffn_wq_up"], dtype=np.int32).T)
    wdt = np.ascontiguousarray(np.asarray(inputs["L0_ffn_wq_down"], dtype=np.int32).T)
    sg = np.ascontiguousarray(np.asarray(inputs["L0_ffn_s_gate"], dtype=np.float32).reshape(1, -1))
    su = np.ascontiguousarray(np.asarray(inputs["L0_ffn_s_up"], dtype=np.float32).reshape(1, -1))
    sd = np.ascontiguousarray(np.asarray(inputs["L0_ffn_s_down"], dtype=np.float32).reshape(1, -1))

    idbf = np.eye(128, dtype=np.float32).astype(BF)
    idf = np.eye(128, dtype=np.float32)
    s_i = np.arange(128)[:, None]
    l_i = np.arange(128)[None, :]
    trip = (s_i <= l_i).astype(np.float32)
    trin = -trip
    maskt = np.where(s_i > l_i, np.float32(-1e9), np.float32(0.0)).astype(BF)
    onesr = np.ones((1, 128), np.float32)
    onesrb = np.ones((1, 128), np.float32).astype(BF)
    onescb = np.ones((128, 1), np.float32).astype(BF)
    n128 = np.full((1, 128), -128.0, np.float32).astype(BF)

    zeros_ctx = np.zeros((T, HID), np.float32)
    nc = build_program()
    in_maps = []
    for core in range(8):
        b, h = core // 2, core % 2
        xo_ = np.ascontiguousarray(xs[b, h * T:(h + 1) * T, :])
        xc_ = np.ascontiguousarray(xs[b, 0:T, :]) if h == 1 else zeros_ctx
        rm = np.array([[1.0 if h == 1 else 0.0]], np.float32)
        in_maps.append({
            "xo": xo_, "xc": xc_, "rmaskd": rm, "wip": wip, "convw": convw,
            "convbr": convb, "dtbr": dtb, "alogr": alog, "dvecr": dvec, "nwr": nw,
            "wout": wout, "wgt": wgt, "wut": wut, "wdt": wdt, "sgr": sg, "sur": su,
            "sdr": sd, "idbf_d": idbf, "idf_d": idf, "trip_d": trip, "trin_d": trin,
            "maskt_d": maskt, "onesr_d": onesr, "onesrb_d": onesrb,
            "onescb_d": onescb, "n128_d": n128,
        })
    res = run_bass_kernel_spmd(nc, in_maps, core_ids=list(range(8)))
    out = np.empty((4, 4096, HID), np.float32)
    for core in range(8):
        b, h = core // 2, core % 2
        out[b, h * T:(h + 1) * T, :] = res.results[core]["outd"]
    return out


# version 04
# speedup vs baseline: 1.0119x; 1.0081x over previous
import numpy as np
import ml_dtypes
from contextlib import ExitStack
import concourse.bass as bass
import concourse.mybir as mybir
from concourse.bass_utils import run_bass_kernel_spmd

F32 = mybir.dt.float32
BF16 = mybir.dt.bfloat16
I32 = mybir.dt.int32
AF = mybir.ActivationFunctionType
ALU = mybir.AluOpType
BF = ml_dtypes.bfloat16
AX = mybir.AxisListType.X

HID = 2048
INTER = 4096
NH = 64
FFH = 5632
T = 2048
CS = 128
NCH = T // CS
NTT = T // 128
NKD = HID // 128
NHT = FFH // 128
EPS = 1e-5


class Pr:
    def __init__(self, nc, es):
        self.nc, self.es = nc, es
        self.h, self.c = {}, {}

    def sem(self, *names):
        for n in names:
            self.h[n] = self.es.enter_context(self.nc.semaphore(n))
            self.c[n] = 0

    def inc(self, instr, s, n=1):
        instr.then_inc(self.h[s], n)
        self.c[s] += n
        return self.c[s]

    def w(self, eng, s, v=None):
        v = self.c[s] if v is None else v
        if v > 0:
            eng.wait_ge(self.h[s], v)


def build_program():
    nc = bass.Bass("TRN2", target_bir_lowering=False)
    dt_ = nc.dram_tensor
    xo = dt_("xo", [T, HID], F32, kind="ExternalInput").ap()
    xc = dt_("xc", [T, HID], F32, kind="ExternalInput").ap()
    rmaskd = dt_("rmaskd", [1, 1], F32, kind="ExternalInput").ap()
    wip = dt_("wip", [HID, 8512], F32, kind="ExternalInput").ap()
    convw = dt_("convw", [4352, 4], F32, kind="ExternalInput").ap()
    convbr = dt_("convbr", [1, 4352], F32, kind="ExternalInput").ap()
    dtbr = dt_("dtbr", [1, NH], F32, kind="ExternalInput").ap()
    alogr = dt_("alogr", [1, NH], F32, kind="ExternalInput").ap()
    dvecr = dt_("dvecr", [1, NH], F32, kind="ExternalInput").ap()
    nwr = dt_("nwr", [1, INTER], F32, kind="ExternalInput").ap()
    wout = dt_("wout", [INTER, HID], F32, kind="ExternalInput").ap()
    wgt = dt_("wgt", [HID, FFH], I32, kind="ExternalInput").ap()
    wut = dt_("wut", [HID, FFH], I32, kind="ExternalInput").ap()
    wdt = dt_("wdt", [FFH, HID], I32, kind="ExternalInput").ap()
    sgr = dt_("sgr", [1, FFH], F32, kind="ExternalInput").ap()
    sur = dt_("sur", [1, FFH], F32, kind="ExternalInput").ap()
    sdr = dt_("sdr", [1, HID], F32, kind="ExternalInput").ap()
    idbf_d = dt_("idbf_d", [128, 128], BF16, kind="ExternalInput").ap()
    idf_d = dt_("idf_d", [128, 128], F32, kind="ExternalInput").ap()
    trip_d = dt_("trip_d", [128, 128], F32, kind="ExternalInput").ap()
    trin_d = dt_("trin_d", [128, 128], F32, kind="ExternalInput").ap()
    maskt_d = dt_("maskt_d", [128, 128], BF16, kind="ExternalInput").ap()
    onesr_d = dt_("onesr_d", [1, 128], F32, kind="ExternalInput").ap()
    onesrb_d = dt_("onesrb_d", [1, 128], BF16, kind="ExternalInput").ap()
    onescb_d = dt_("onescb_d", [128, 1], BF16, kind="ExternalInput").ap()
    n128_d = dt_("n128_d", [1, 128], BF16, kind="ExternalInput").ap()
    x_sp = dt_("x_sp", [INTER, T], BF16, kind="Internal").ap()
    gu_sp = dt_("gu_sp", [FFH, T], BF16, kind="Internal").ap()
    sg_sp = dt_("sg_sp", [T, INTER], BF16, kind="Internal").ap()
    g_sp = dt_("g_sp", [T, HID], BF16, kind="Internal").ap()
    pp_sp = dt_("pp_sp", [T, HID], F32, kind="Internal").ap()
    ygt_sp = dt_("ygt_sp", [INTER, T], BF16, kind="Internal").ap()
    scrH = dt_("scrH", [NCH, NH * CS], BF16, kind="Internal").ap()
    scrL = dt_("scrL", [NCH, NH * CS], BF16, kind="Internal").ap()
    scrX = dt_("scrX", [16, 128], F32, kind="Internal").ap()
    outd = dt_("outd", [T, HID], F32, kind="ExternalOutput").ap()

    es = ExitStack()
    sbo = lambda name, shape, dt: es.enter_context(nc.sbuf_tensor(name, shape, dt))

    id_bf = sbo("id_bf", [128, 128], BF16)
    id_f = sbo("id_f", [128, 128], F32)
    triP = sbo("triP", [128, 128], F32)
    triN = sbo("triN", [128, 128], F32)
    maskT = sbo("maskT", [128, 128], BF16)
    ones_r = sbo("ones_r", [1, 128], F32)
    ones_rb = sbo("ones_rb", [1, 128], BF16)
    ones_cb = sbo("ones_cb", [128, 1], BF16)
    n128r = sbo("n128r", [1, 128], BF16)
    ones11 = sbo("ones11", [1, 1], F32)
    convw_s = sbo("convw_s", [128, 140], F32)
    convb_c = sbo("convb_c", [128, 35], F32)
    sg_c = sbo("sg_c", [128, NHT], F32)
    su_c = sbo("su_c", [128, NHT], F32)
    dtb_c = sbo("dtb_c", [64, 1], F32)
    A_c = sbo("A_c", [64, 1], F32)
    D_b = sbo("D_b", [128, 64], F32)
    rmask_c = sbo("rmask_c", [128, 1], F32)
    NWb = sbo("NWb", [128, INTER], F32)
    Bt = sbo("Bt", [128, T], BF16)
    Ctn = sbo("Ctn", [128, T], BF16)
    dts = sbo("dts", [64, T], F32)
    adts = sbo("adts", [64, T], F32)
    R = sbo("Rst", [128, INTER], F32)
    hist = sbo("hist", [128, 105], F32)

    P = Pr(nc, es)
    with nc.Block() as block:
        P.sem("d0", "d1", "d2", "d3", "d4", "g0", "g1", "w0", "w1", "w2", "w3", "tS", "vS", "sS")

        def barrier():
            for engname in ("sync", "gpsimd", "vector", "scalar", "tensor"):
                @getattr(block, engname)
                def _(e):
                    for s in ("d0", "d1", "d2", "d3", "d4", "g0", "g1", "w0", "w1", "w2", "w3", "tS", "vS", "sS"):
                        P.w(e, s)

        with nc.sbuf_tensor("r_cb", [1, 4352], F32, allow_name_mangling=True) as r_cb, \
             nc.sbuf_tensor("r_sg", [1, FFH], F32, allow_name_mangling=True) as r_sg, \
             nc.sbuf_tensor("r_su", [1, FFH], F32, allow_name_mangling=True) as r_su, \
             nc.sbuf_tensor("r_sm", [1, 193], F32, allow_name_mangling=True) as r_sm, \
             nc.sbuf_tensor("r_nw", [1, INTER], F32, allow_name_mangling=True) as r_nw, \
             nc.sbuf_tensor("t_c64", [64, 1], F32, allow_name_mangling=True) as t_c64, \
             nc.psum_tensor("ps_set", [128, 512], F32, allow_name_mangling=True) as ps_set:

            @block.sync
            def _(sync):
                for dst, src in [(id_bf, idbf_d), (id_f, idf_d), (triP, trip_d),
                                 (triN, trin_d), (maskT, maskt_d), (ones_r, onesr_d),
                                 (ones_rb, onesrb_d), (ones_cb, onescb_d), (n128r, n128_d),
                                 (r_cb, convbr), (r_sg, sgr), (r_su, sur), (r_nw, nwr)]:
                    P.inc(sync.dma_start(out=dst[:], in_=src), "d0", 16)
                P.inc(sync.dma_start(out=r_sm[:, 0:64], in_=dtbr), "d0", 16)
                P.inc(sync.dma_start(out=r_sm[:, 64:128], in_=alogr), "d0", 16)
                P.inc(sync.dma_start(out=r_sm[:, 128:192], in_=dvecr), "d0", 16)
                P.inc(sync.dma_start(out=r_sm[:, 192:193], in_=rmaskd), "d0", 16)
                for j in range(34):
                    P.inc(sync.dma_start(out=convw_s[:, j * 4:(j + 1) * 4],
                                         in_=convw[j * 128:(j + 1) * 128, :]), "d0", 16)
            dma_setup = P.c["d0"]

            @block.vector
            def _(v):
                P.w(v, "d0", dma_setup)
                v.memset(ones11[:], 1.0)
                v.memset(hist[:], 0.0)
                P.inc(v.memset(R[:], 0.0), "vS", 1)

            def rows_to_cols(row_ap, n, dst_fn):
                j = 0
                while j < n:
                    m = min(8, n - j)
                    @block.tensor
                    def _(t, j=j, m=m):
                        P.w(t, "vS")
                        for i in range(m):
                            P.inc(t.matmul(ps_set[:, j + i:j + i + 1],
                                           row_ap[0:1, (j + i) * 128:(j + i + 1) * 128],
                                           ones11[:], start=True, stop=True), "tS", 1)
                    @block.vector
                    def _(v, j=j, m=m):
                        P.w(v, "tS")
                        for i in range(m):
                            P.inc(v.tensor_copy(dst_fn(j + i), ps_set[:, j + i:j + i + 1]), "vS", 1)
                    j += m

            rows_to_cols(r_cb, 34, lambda j: convb_c[:, j:j + 1])
            rows_to_cols(r_sg, NHT, lambda j: sg_c[:, j:j + 1])
            rows_to_cols(r_su, NHT, lambda j: su_c[:, j:j + 1])

            @block.tensor
            def _(t):
                P.w(t, "vS")
                P.inc(t.matmul(ps_set[0:64, 0:1], r_sm[0:1, 0:64], ones11[:], start=True, stop=True), "tS", 1)
                P.inc(t.matmul(ps_set[0:64, 1:2], r_sm[0:1, 64:128], ones11[:], start=True, stop=True), "tS", 1)
                P.inc(t.matmul(ps_set[:, 64:128], ones_r[:], r_sm[0:1, 128:192], start=True, stop=True), "tS", 1)
                P.inc(t.matmul(ps_set[:, 128:129], ones_r[:], r_sm[0:1, 192:193], start=True, stop=True), "tS", 1)
            @block.vector
            def _(v):
                P.w(v, "tS")
                v.tensor_copy(dtb_c[:], ps_set[0:64, 0:1])
                v.tensor_copy(t_c64[:], ps_set[0:64, 1:2])
                v.tensor_copy(D_b[:], ps_set[:, 64:128])
                P.inc(v.tensor_copy(rmask_c[:], ps_set[:, 128:129]), "vS", 1)
            @block.scalar
            def _(s):
                P.w(s, "vS")
                P.inc(s.activation(A_c[:], t_c64[:], AF.Exp), "sS", 1)
            @block.vector
            def _(v):
                P.w(v, "sS")
                P.inc(v.tensor_scalar(A_c[:], A_c[:], -1.0, None, ALU.mult), "vS", 1)
            for jj in range(INTER // 512):
                @block.tensor
                def _(t, jj=jj):
                    P.w(t, "vS")
                    P.inc(t.matmul(ps_set[:, 0:512], ones_r[:],
                                   r_nw[0:1, jj * 512:(jj + 1) * 512], start=True, stop=True), "tS", 1)
                @block.vector
                def _(v, jj=jj):
                    P.w(v, "tS")
                    P.inc(v.tensor_copy(NWb[:, jj * 512:(jj + 1) * 512], ps_set[:, 0:512]), "vS", 1)

        def emit_ln(x_dram, collect_xsum, xnT, xsrb):
            with nc.sbuf_tensor("ln_xb", [128, 2 * HID], F32, allow_name_mangling=True) as xb, \
                 nc.sbuf_tensor("ln_xc", [128, 2 * HID], F32, allow_name_mangling=True) as xcb, \
                 nc.sbuf_tensor("ln_xn", [128, 2 * HID], BF16, allow_name_mangling=True) as xnb, \
                 nc.sbuf_tensor("ln_sm", [128, 8], F32, allow_name_mangling=True) as sm, \
                 nc.sbuf_tensor("ln_xs", [128, 16], F32, allow_name_mangling=True) as xsc, \
                 nc.sbuf_tensor("ln_xr", [16, 128], F32, allow_name_mangling=True) as xrow, \
                 nc.sbuf_tensor("ln_x1", [1, T], F32, allow_name_mangling=True) as xsr, \
                 nc.psum_tensor("ln_pt", [128, 2048], BF16, allow_name_mangling=True) as ps_t, \
                 nc.psum_tensor("ln_pf", [128, 128], F32, allow_name_mangling=True) as ps_f:
                barrier()
                free_x = [0, 0]
                free_xc = [0, 0]
                free_xn = [0, 0]
                free_ps = 0
                rdy = [0] * NTT
                for i in range(NTT):
                    sl = i % 2
                    @block.sync
                    def _(sync, i=i, sl=sl):
                        P.w(sync, "vS", free_x[sl])
                        rdy[i] = P.inc(sync.dma_start(out=xb[:, sl * HID:(sl + 1) * HID],
                                                      in_=x_dram[i * 128:(i + 1) * 128, :]), "d0" if sl == 0 else "d1", 16)
                    @block.vector
                    def _(v, i=i, sl=sl):
                        P.w(v, "d0" if sl == 0 else "d1", rdy[i])
                        xbs = xb[:, sl * HID:(sl + 1) * HID]
                        P.inc(v.reduce_sum(sm[:, 0:1], xbs, axis=AX), "vS", 1)
                        P.w(v, "vS")
                        P.inc(v.tensor_scalar(sm[:, 1:2], sm[:, 0:1], -1.0 / HID, None, ALU.mult), "vS", 1)
                        P.w(v, "vS")
                        P.w(v, "vS", free_xc[sl])
                        P.inc(v.tensor_scalar(xcb[:, sl * HID:(sl + 1) * HID], xbs,
                                              sm[:, 1:2], None, ALU.add), "vS", 1)
                    @block.vector
                    def _(v, sl=sl):
                        P.w(v, "vS")
                        P.inc(v.tensor_tensor(xb[:, sl * HID:(sl + 1) * HID],
                                              xcb[:, sl * HID:(sl + 1) * HID],
                                              xcb[:, sl * HID:(sl + 1) * HID], ALU.mult), "vS", 1)
                        P.w(v, "vS")
                        P.inc(v.reduce_sum(sm[:, 2:3], xb[:, sl * HID:(sl + 1) * HID], axis=AX), "vS", 1)
                        P.w(v, "vS")
                        free_x[sl] = P.inc(v.tensor_scalar(sm[:, 3:4], sm[:, 2:3],
                                                           1.0 / HID, EPS, ALU.mult, ALU.add), "vS", 1)
                    @block.scalar
                    def _(s):
                        P.w(s, "vS")
                        P.inc(s.activation(sm[:, 4:5], sm[:, 3:4], AF.Ln), "sS", 1)
                        P.w(s, "sS")
                        P.inc(s.activation(sm[:, 5:6], sm[:, 4:5], AF.Exp, scale=-0.5), "sS", 1)
                    @block.vector
                    def _(v, i=i, sl=sl):
                        P.w(v, "sS")
                        P.w(v, "tS", free_xn[sl])
                        free_xc[sl] = P.inc(v.tensor_scalar(xnb[:, sl * HID:(sl + 1) * HID],
                                                            xcb[:, sl * HID:(sl + 1) * HID],
                                                            sm[:, 5:6], None, ALU.mult), "vS", 1)
                        if collect_xsum:
                            P.w(v, "vS")
                            P.inc(v.reduce_sum(xsc[:, i:i + 1], xnb[:, sl * HID:(sl + 1) * HID],
                                               axis=AX), "vS", 1)
                    @block.tensor
                    def _(t, sl=sl):
                        P.w(t, "vS")
                        P.w(t, "sS", free_ps)
                        for j in range(16):
                            ins = t.transpose(ps_t[:, j * 128:(j + 1) * 128],
                                              xnb[:, sl * HID + j * 128: sl * HID + (j + 1) * 128], id_bf[:])
                        P.inc(ins, "tS", 1)
                        free_xn[sl] = P.c["tS"]
                    @block.scalar
                    def _(s, i=i):
                        P.w(s, "tS")
                        xv = xnT.rearrange("p (k t) -> p k t", t=T)
                        P.inc(s.copy(xv[:, :, i * 128:(i + 1) * 128],
                                     ps_t.rearrange("p (k t) -> p k t", t=128)), "sS", 1)
                    free_ps = P.c["sS"]
                if collect_xsum:
                    @block.tensor
                    def _(t):
                        P.w(t, "vS")
                        P.inc(t.transpose(ps_f[0:16, :], xsc[:], id_f[:]), "tS", 1)
                    @block.vector
                    def _(v):
                        P.w(v, "tS")
                        P.inc(v.tensor_copy(xrow[:], ps_f[0:16, :]), "vS", 1)
                    @block.scalar
                    def _(s):
                        P.w(s, "vS")
                        P.inc(s.dma_start(out=scrX, in_=xrow[:]), "d2", 16)
                        P.w(s, "d2")
                        P.inc(s.dma_start(out=xsr[:], in_=scrX.rearrange("i p -> (i p)")[None, :]), "d2", 16)
                    @block.vector
                    def _(v):
                        P.w(v, "d2")
                        P.inc(v.tensor_copy(xsrb[:], xsr[:]), "vS", 1)

        def emit_inprojT(is_ctx, xnT):
            with nc.sbuf_tensor("ipt_w", [128, 2 * 2048], BF16, allow_name_mangling=True) as wst, \
                 nc.sbuf_tensor("ipt_xr", [128, 2 * 2051], F32, allow_name_mangling=True) as xbcr, \
                 nc.sbuf_tensor("ipt_cv", [128, 2048], F32, allow_name_mangling=True) as cvt, \
                 nc.sbuf_tensor("ipt_cs", [128, 2 * 2048], BF16, allow_name_mangling=True) as convs, \
                 nc.sbuf_tensor("ipt_t1", [64, 2048], F32, allow_name_mangling=True) as tA, \
                 nc.sbuf_tensor("ipt_t2", [64, 2048], F32, allow_name_mangling=True) as tB, \
                 nc.psum_tensor("ipt_ps", [128, 2 * 2048], F32, allow_name_mangling=True) as ps_ip:
                barrier()
                stat_rdy = [0] * 35
                free_w = [0, 0]
                free_ps = [0, 0]
                free_cv = 0
                free_cs = [0, 0]
                mm_done = [0] * 35
                ev_done = [0] * 35
                xv = xnT.rearrange("p (k t) -> p k t", t=T)
                for j in range(36):
                    if j < 35:
                        cw = 128 if j < 34 else 64
                        @block.gpsimd
                        def _(g, j=j, cw=cw):
                            P.w(g, "tS", free_w[j % 2])
                            last = 0
                            for k in range(NKD):
                                last = P.inc(g.dma_start(
                                    out=wst[:, (j % 2) * 2048 + k * 128:(j % 2) * 2048 + k * 128 + cw],
                                    in_=wip[k * 128:(k + 1) * 128, INTER + j * 128: INTER + j * 128 + cw]), "g0" if j % 2 == 0 else "g1", 16)
                            stat_rdy[j] = last
                    if j == 0:
                        continue
                    jj = j - 1
                    cw = 128 if jj < 34 else 64
                    sl = jj % 2
                    @block.tensor
                    def _(t, jj=jj, cw=cw, sl=sl):
                        P.w(t, "g0" if jj % 2 == 0 else "g1", stat_rdy[jj])
                        P.w(t, "sS", free_ps[sl])
                        for k in range(NKD):
                            for n in range(4):
                                ins = t.matmul(ps_ip[0:cw, sl * 2048 + n * 512: sl * 2048 + (n + 1) * 512],
                                               wst[:, sl * 2048 + k * 128: sl * 2048 + k * 128 + cw],
                                               xv[:, k, n * 512:(n + 1) * 512],
                                               start=(k == 0), stop=(k == NKD - 1))
                        mm_done[jj] = P.inc(ins, "tS", 1)
                        free_w[sl] = P.c["tS"]
                    @block.scalar
                    def _(s, jj=jj, cw=cw, sl=sl):
                        P.w(s, "tS", mm_done[jj])
                        free_ps[sl] = P.inc(s.copy(xbcr[0:cw, sl * 2051 + 3: sl * 2051 + 2051],
                                                   ps_ip[0:cw, sl * 2048: sl * 2048 + 2048]), "sS", 1)
                        ev_done[jj] = P.c["sS"]
                    if jj < 34:
                        @block.vector
                        def _(v, jj=jj, sl=sl):
                            P.w(v, "sS", ev_done[jj])
                            P.inc(v.tensor_copy(xbcr[:, sl * 2051: sl * 2051 + 3], hist[:, jj * 3:(jj + 1) * 3]), "vS", 1)
                            P.w(v, "vS")
                            if is_ctx:
                                P.inc(v.tensor_copy(hist[:, jj * 3:(jj + 1) * 3],
                                              xbcr[:, sl * 2051 + 2048: sl * 2051 + 2051]), "vS", 1)
                                P.w(v, "vS")
                            P.w(v, "sS", free_cv)
                            P.inc(v.tensor_scalar(cvt[:], xbcr[:, sl * 2051: sl * 2051 + 2048],
                                            convw_s[:, jj * 4:jj * 4 + 1], None, ALU.mult), "vS", 1)
                            P.w(v, "vS")
                            for kk in range(1, 4):
                                P.inc(v.scalar_tensor_tensor(cvt[:], xbcr[:, sl * 2051 + kk: sl * 2051 + kk + 2048],
                                                       convw_s[:, jj * 4 + kk:jj * 4 + kk + 1], cvt[:],
                                                       ALU.mult, ALU.add), "vS", 1)
                                P.w(v, "vS")
                        @block.scalar
                        def _(s, jj=jj, sl=sl):
                            P.w(s, "vS")
                            if jj == 32:
                                dst = Bt[:]
                            elif jj == 33:
                                dst = Ctn[:]
                            else:
                                P.w(s, "w0" if sl == 0 else "w1", free_cs[sl])
                                dst = convs[:, sl * 2048:(sl + 1) * 2048]
                            free_cv = P.inc(s.activation(dst, cvt[:], AF.Silu,
                                                         bias=convb_c[:, jj:jj + 1]), "sS", 1)
                            if jj < 32:
                                P.w(s, "sS")
                                free_cs[sl] = P.inc(s.dma_start(
                                    out=x_sp[jj * 128:(jj + 1) * 128, :],
                                    in_=convs[:, sl * 2048:(sl + 1) * 2048]), "w0" if sl == 0 else "w1", 16)
                    else:
                        @block.vector
                        def _(v, sl=sl):
                            P.w(v, "sS", ev_done[34])
                            srcz = xbcr[0:64, sl * 2051 + 3: sl * 2051 + 2051]
                            P.inc(v.tensor_scalar(tB[:], srcz, dtb_c[:], None, ALU.add), "vS", 1)
                            P.w(v, "vS")
                            P.inc(v.tensor_scalar(tA[:], tB[:], -1.0, None, ALU.mult), "vS", 1)
                            P.w(v, "vS")
                            P.inc(v.tensor_tensor(tA[:], tA[:], tB[:], ALU.max), "vS", 1)
                            P.w(v, "vS")
                            P.inc(v.tensor_scalar(tB[:], tB[:], 0.0, None, ALU.max), "vS", 1)
                        @block.scalar
                        def _(s):
                            P.w(s, "vS")
                            P.inc(s.activation(tA[:], tA[:], AF.Exp, scale=-1.0), "sS", 1)
                            P.w(s, "sS")
                            P.inc(s.activation(tA[:], tA[:], AF.Ln, bias=1.0), "sS", 1)
                        @block.vector
                        def _(v):
                            P.w(v, "sS")
                            P.inc(v.tensor_tensor(dts[:], tA[:], tB[:], ALU.add), "vS", 1)
                            P.w(v, "vS")
                            P.inc(v.tensor_scalar(adts[:], dts[:], A_c[:], None, ALU.mult), "vS", 1)

        def emit_prepass(dtT_all, dsT_all, etb_all, sdo_all, negA_all, need_flats):
            with nc.sbuf_tensor("pp_aT", [128, 64], F32, allow_name_mangling=True) as adtT_s, \
                 nc.sbuf_tensor("pp_tc", [64, 1], F32, allow_name_mangling=True) as tc, \
                 nc.sbuf_tensor("pp_tr", [1, 64], F32, allow_name_mangling=True) as totr, \
                 nc.sbuf_tensor("pp_acf", [64, 128], F32, allow_name_mangling=True) as acf, \
                 nc.sbuf_tensor("pp_ach", [64, 128], BF16, allow_name_mangling=True) as ach, \
                 nc.sbuf_tensor("pp_acg", [64, 128], F32, allow_name_mangling=True) as achf, \
                 nc.sbuf_tensor("pp_acl", [64, 128], BF16, allow_name_mangling=True) as acl, \
                 nc.psum_tensor("pp_ps", [128, 512], F32, allow_name_mangling=True) as pp:
                barrier()
                for c in range(NCH):
                    @block.tensor
                    def _(t, c=c):
                        P.w(t, "vS"); P.w(t, "sS")
                        t.transpose(pp[:, 0:64], dts[0:64, c * 128:(c + 1) * 128], id_f[0:64, 0:64])
                        P.inc(t.transpose(pp[:, 64:128], adts[0:64, c * 128:(c + 1) * 128],
                                          id_f[0:64, 0:64]), "tS", 1)
                    @block.vector
                    def _(v, c=c):
                        P.w(v, "tS")
                        v.tensor_copy(dtT_all[:, c * 64:(c + 1) * 64], pp[:, 0:64])
                        v.tensor_copy(adtT_s[:], pp[:, 64:128])
                        P.inc(v.reduce_sum(tc[:], adts[0:64, c * 128:(c + 1) * 128], axis=AX), "vS", 1)
                    @block.tensor
                    def _(t):
                        P.w(t, "vS")
                        P.inc(t.matmul(pp[0:1, 128:192], tc[:], id_f[0:64, 0:64], start=True, stop=True), "tS", 1)
                    @block.vector
                    def _(v):
                        P.w(v, "tS")
                        P.inc(v.tensor_copy(totr[:], pp[0:1, 128:192]), "vS", 1)
                    @block.tensor
                    def _(t):
                        P.w(t, "vS")
                        t.matmul(pp[:, 192:256], triN[:], adtT_s[:], start=True, stop=False)
                        t.matmul(pp[:, 192:256], ones_r[:], totr[:], start=False, stop=True)
                        t.matmul(pp[:, 256:320], ones_r[:], totr[:], start=True, stop=True)
                        t.matmul(pp[:, 320:384], triN[:], adtT_s[:], start=True, stop=True)
                        P.inc(t.matmul(pp[0:64, 384:512], adtT_s[:], triP[:], start=True, stop=True), "tS", 1)
                    @block.scalar
                    def _(s, c=c):
                        P.w(s, "tS")
                        s.activation(dsT_all[:, c * 64:(c + 1) * 64], pp[:, 192:256], AF.Exp)
                        P.inc(s.activation(etb_all[:, c * 64:(c + 1) * 64], pp[:, 256:320], AF.Exp), "sS", 1)
                    @block.vector
                    def _(v, c=c):
                        P.w(v, "tS")
                        v.tensor_copy(negA_all[:, c * 64:(c + 1) * 64], pp[:, 320:384])
                        P.inc(v.tensor_copy(acf[:], pp[0:64, 384:512]), "vS", 1)
                    @block.scalar
                    def _(s, c=c):
                        P.w(s, "vS")
                        P.inc(s.activation(sdo_all[:, c * 64:(c + 1) * 64],
                                           negA_all[:, c * 64:(c + 1) * 64], AF.Exp, scale=-1.0), "sS", 1)
                    if need_flats:
                        @block.vector
                        def _(v):
                            P.w(v, "vS")
                            P.w(v, "w0")
                            P.inc(v.tensor_copy(ach[:], acf[:]), "vS", 1)
                        @block.scalar
                        def _(s):
                            P.w(s, "vS")
                            P.inc(s.copy(achf[:], ach[:]), "sS", 1)
                        @block.vector
                        def _(v):
                            P.w(v, "sS")
                            P.inc(v.tensor_tensor(acl[:], acf[:], achf[:], ALU.subtract), "vS", 1)
                        @block.scalar
                        def _(s, c=c):
                            P.w(s, "vS")
                            P.inc(s.dma_start(out=scrH[c:c + 1, :].rearrange("a (h l) -> (a h) l", l=128),
                                              in_=ach[:]), "w0", 16)
                            P.inc(s.dma_start(out=scrL[c:c + 1, :].rearrange("a (h l) -> (a h) l", l=128),
                                              in_=acl[:]), "w0", 16)

        def emit_ssd(full, dtT_all, dsT_all, etb_all, sdo_all, negA_all):
            with ExitStack() as ss:
                sbl = lambda name, shape, dt: ss.enter_context(nc.sbuf_tensor(name, shape, dt, allow_name_mangling=True))
                psl = lambda name, shape, dt: ss.enter_context(nc.psum_tensor(name, shape, dt, allow_name_mangling=True))
                xch = sbl("sd_xch", [128, 2 * INTER], BF16)
                xchT = sbl("sd_xchT", [128, INTER], BF16)
                xd = sbl("sd_xd", [128, INTER], BF16)
                Bsb = sbl("sd_Bsb", [128, 128], BF16)
                t512 = sbl("sd_t5", [128, 512], F32)
                pS = psl("sd_pS", [128, 512], F32)
                pxt = psl("sd_pxt", [128, 1024], BF16)
                barrier()
                if full:
                    xrd = sbl("sd_xrd", [128, INTER], BF16)
                    cb_b = sbl("sd_cb", [128, 128], BF16)
                    Eb = sbl("sd_E", [128, 1024], BF16)
                    Mb = sbl("sd_M", [128, 1024], BF16)
                    flhl = sbl("sd_f", [1, 2 * 2048], BF16)
                    ybuf = sbl("sd_y", [128, INTER], F32)
                    sgc = sbl("sd_sg", [128, INTER], BF16)
                    sqf = sbl("sd_sq", [128, INTER], F32)
                    ygb = sbl("sd_yg", [128, INTER], BF16)
                    ygT = sbl("sd_ygT", [128, INTER], BF16)
                    ssm = sbl("sd_sm", [128, 8], F32)
                    Rbf = sbl("sd_Rb", [128, INTER], BF16)
                    psE = psl("sd_pE", [128, 2048], F32)
                    pyd = psl("sd_pyd", [128, 512], F32)
                    pyo = psl("sd_pyo", [128, 512], F32)
                xch_rdy = [0] * NCH
                sg_rdy = [0] * NCH
                xch_free = [0, 0]
                sg_free = 0
                pass
                NG = NCH * 8
                flat_rdy = [0] * (NG + 2)
                flat_free = [0, 0]
                rbf_rdy = [0]
                e_free = [0, 0]
                if full:
                    @block.scalar
                    def _(s):
                        P.w(s, "vS")
                        rbf_rdy[0] = P.inc(s.copy(Rbf[:], R[:]), "sS", 1)
                    @block.sync
                    def _(sync):
                        pass
                        for G in range(2):
                            par = G % 2
                            sem = "d3" if par == 0 else "d4"
                            cG, gG = G // 8, G % 8
                            P.inc(sync.dma_start(out=flhl[0:1, par * 2048: par * 2048 + 1024],
                                           in_=scrH[cG:cG + 1, gG * 1024:(gG + 1) * 1024]), sem, 16)
                            flat_rdy[G] = P.inc(sync.dma_start(
                                out=flhl[0:1, par * 2048 + 1024: par * 2048 + 2048],
                                in_=scrL[cG:cG + 1, gG * 1024:(gG + 1) * 1024]), sem, 16)
                for c in range(NCH + 1):
                    if c < NCH:
                        sl = c % 2
                        @block.sync
                        def _(sync, c=c, sl=sl):
                            if c == 0:
                                pass
                            P.w(sync, "vS", xch_free[sl])
                            xvv = x_sp.rearrange("(i p) j -> p i j", p=128)
                            last = 0
                            for q in range(4):
                                last = P.inc(sync.dma_start(
                                    out=xch.rearrange("p (s i j) -> p s i j", s=2, j=128)[:, sl, q * 8:(q + 1) * 8, :],
                                    in_=xvv[:, q * 8:(q + 1) * 8, c * 128:(c + 1) * 128]), "d0" if sl == 0 else "d1", 16)
                            xch_rdy[c] = last
                    if c == 0:
                        continue
                    cc = c - 1
                    if full:
                        @block.sync
                        def _(sync, cc=cc):
                            P.w(sync, "vS", sg_free)
                            sg_rdy[cc] = P.inc(sync.dma_start(out=sgc[:],
                                                              in_=sg_sp[cc * 128:(cc + 1) * 128, :]), "d2", 16)
                    sl = cc % 2
                    skip_state = full and (cc == NCH - 1)
                    for q in range(4):
                        @block.tensor
                        def _(t, cc=cc, sl=sl, q=q):
                            P.w(t, "d0" if sl == 0 else "d1", xch_rdy[cc])
                            P.w(t, "vS")
                            for j8 in range(8):
                                ins = t.transpose(pxt[:, j8 * 128:(j8 + 1) * 128],
                                                  xch[:, sl * INTER + (q * 8 + j8) * 128:
                                                      sl * INTER + (q * 8 + j8 + 1) * 128], id_bf[:])
                            P.inc(ins, "tS", 1)
                        @block.vector
                        def _(v, q=q):
                            P.w(v, "tS")
                            P.inc(v.tensor_copy(xchT[:, q * 1024:(q + 1) * 1024], pxt[:]), "vS", 1)
                    @block.vector
                    def _(v, cc=cc, sl=sl, skip_state=skip_state):
                        P.w(v, "vS")
                        x3 = xchT.rearrange("p (h q) -> p h q", q=64)
                        P.inc(v.tensor_tensor(xd.rearrange("p (h q) -> p h q", q=64), x3,
                                        dtT_all[:, cc * 64:(cc + 1) * 64][:, :, None].broadcast_to([128, 64, 64]),
                                        ALU.mult), "vS", 1)
                        P.w(v, "vS")
                        if full:
                            P.inc(v.tensor_tensor(xrd.rearrange("p (h q) -> p h q", q=64), x3,
                                            D_b[:, :, None].broadcast_to([128, 64, 64]), ALU.mult), "vS", 1)
                            P.w(v, "vS")
                        if not skip_state:
                            P.inc(v.tensor_tensor(
                                xch.rearrange("p (s f) -> p s f", s=2)[:, sl, :].rearrange("p (h q) -> p h q", q=64),
                                xd.rearrange("p (h q) -> p h q", q=64),
                                dsT_all[:, cc * 64:(cc + 1) * 64][:, :, None].broadcast_to([128, 64, 64]),
                                ALU.mult), "vS", 1)
                    xdd_ap = xch[:, sl * INTER:(sl + 1) * INTER]
                    @block.tensor
                    def _(t, cc=cc):
                        P.w(t, "vS")
                        P.inc(t.transpose(pxt[:, 0:128], Bt[:, cc * 128:(cc + 1) * 128], id_bf[:]), "tS", 1)
                    @block.vector
                    def _(v):
                        P.w(v, "tS")
                        P.inc(v.tensor_copy(Bsb[:], pxt[:, 0:128]), "vS", 1)
                    if full:
                        @block.tensor
                        def _(t, cc=cc):
                            P.w(t, "vS"); P.w(t, "sS")
                            P.inc(t.matmul(psE[:, 1024:1152], Bt[:, cc * 128:(cc + 1) * 128],
                                           Ctn[:, cc * 128:(cc + 1) * 128], start=True, stop=True), "tS", 1)
                        @block.scalar
                        def _(s):
                            P.w(s, "tS")
                            P.inc(s.copy(cb_b[:], psE[:, 1024:1152]), "sS", 1)
                        e_free[0] = e_free[1] = P.c["sS"]
                    for g in range(8):
                        G = cc * 8 + g
                        par = g % 2
                        if full:
                            @block.tensor
                            def _(t, cc=cc, sl=sl, g=g, par=par, G=G):
                                P.w(t, "d3" if par == 0 else "d4", flat_rdy[G])
                                P.w(t, "sS", e_free[par])
                                for i in range(8):
                                    h = g * 8 + i
                                    dst = psE[:, par * 1024 + i * 128: par * 1024 + (i + 1) * 128]
                                    t.matmul(dst, id_bf[:], maskT[:], start=(i % 4 == 0), stop=False)
                                    t.matmul(dst, ones_rb[:],
                                             flhl[0:1, par * 2048 + i * 128: par * 2048 + (i + 1) * 128],
                                             start=False, stop=False)
                                    ins = t.matmul(dst, ones_rb[:],
                                                   flhl[0:1, par * 2048 + 1024 + i * 128: par * 2048 + 1024 + (i + 1) * 128],
                                                   start=False, stop=(i % 4 == 3))
                                flat_free[par] = P.inc(ins, "tS", 1)
                            if G + 2 < NG:
                                @block.sync
                                def _(sync, G=G, par=par):
                                    P.w(sync, "tS", flat_free[par])
                                    G2 = G + 2
                                    cG, gG = G2 // 8, G2 % 8
                                    P.inc(sync.dma_start(out=flhl[0:1, par * 2048: par * 2048 + 1024],
                                                   in_=scrH[cG:cG + 1, gG * 1024:(gG + 1) * 1024]), "d3" if par == 0 else "d4", 16)
                                    flat_rdy[G2] = P.inc(sync.dma_start(
                                        out=flhl[0:1, par * 2048 + 1024: par * 2048 + 2048],
                                        in_=scrL[cG:cG + 1, gG * 1024:(gG + 1) * 1024]), "d3" if par == 0 else "d4", 16)
                            @block.scalar
                            def _(s, cc=cc, g=g, par=par):
                                P.w(s, "tS")
                                P.w(s, "vS")
                                for i in range(8):
                                    h = g * 8 + i
                                    ins = s.activation(Eb[:, i * 128:(i + 1) * 128],
                                                       psE[:, par * 1024 + i * 128: par * 1024 + (i + 1) * 128],
                                                       AF.Exp, bias=negA_all[:, cc * 64 + h: cc * 64 + h + 1])
                                e_free[par] = P.inc(ins, "sS", 1)
                            @block.vector
                            def _(v):
                                P.w(v, "sS")
                                P.w(v, "tS")
                                P.inc(v.tensor_tensor(Mb.rearrange("p (a f) -> p a f", f=128),
                                                      Eb.rearrange("p (a f) -> p a f", f=128),
                                                      cb_b[:, None, :].broadcast_to([128, 8, 128]),
                                                      ALU.mult), "vS", 1)
                        @block.tensor
                        def _(t, cc=cc, g=g, skip_state=skip_state):
                            P.w(t, "vS")
                            if full:
                                P.w(t, "sS", rbf_rdy[0])
                            ins = None
                            for i in range(8):
                                h = g * 8 + i
                                if full:
                                    t.matmul(pyd[:, i * 64:(i + 1) * 64], Mb[:, i * 128:(i + 1) * 128],
                                             xd[:, h * 64:(h + 1) * 64], start=(i == 0), stop=False)
                                    t.matmul(pyd[:, i * 64:(i + 1) * 64], id_bf[:],
                                             xrd[:, h * 64:(h + 1) * 64], start=False, stop=(i == 7))
                                    ins = t.matmul(pyo[:, i * 64:(i + 1) * 64], Ctn[:, cc * 128:(cc + 1) * 128],
                                                   Rbf[:, h * 64:(h + 1) * 64], start=(i == 0), stop=(i == 7))
                                if not skip_state:
                                    ins = t.matmul(pS[:, i * 64:(i + 1) * 64], Bsb[:],
                                                   xdd_ap[:, h * 64:(h + 1) * 64], start=(i == 0), stop=(i == 7))
                            P.inc(ins, "tS", 1)
                        @block.vector
                        def _(v, cc=cc, g=g, skip_state=skip_state):
                            P.w(v, "tS")
                            if full:
                                P.inc(v.tensor_tensor(t512.rearrange("p (h q) -> p h q", q=64),
                                                pyo.rearrange("p (h q) -> p h q", q=64),
                                                sdo_all[:, cc * 64 + g * 8: cc * 64 + (g + 1) * 8][:, :, None].broadcast_to([128, 8, 64]),
                                                ALU.mult), "vS", 1)
                                P.w(v, "vS")
                                P.inc(v.tensor_tensor(ybuf[:, g * 512:(g + 1) * 512], t512[:], pyd[:], ALU.add), "vS", 1)
                                P.w(v, "vS")
                            if not skip_state:
                                P.inc(v.tensor_tensor(t512.rearrange("p (h q) -> p h q", q=64),
                                                R[:, g * 512:(g + 1) * 512].rearrange("p (h q) -> p h q", q=64),
                                                etb_all[:, cc * 64 + g * 8: cc * 64 + (g + 1) * 8][:, :, None].broadcast_to([128, 8, 64]),
                                                ALU.mult), "vS", 1)
                                P.w(v, "vS")
                                P.inc(v.tensor_tensor(R[:, g * 512:(g + 1) * 512], t512[:], pS[:], ALU.add), "vS", 1)
                    xch_free[sl] = P.c["vS"]
                    if not full:
                        continue
                    if not skip_state:
                        @block.scalar
                        def _(s):
                            P.w(s, "vS")
                            rbf_rdy[0] = P.inc(s.copy(Rbf[:], R[:]), "sS", 1)
                    @block.vector
                    def _(v):
                        P.w(v, "vS")
                        P.inc(v.tensor_tensor(sqf[:], ybuf[:], ybuf[:], ALU.mult), "vS", 1)
                        P.w(v, "vS")
                        P.inc(v.reduce_sum(ssm[:, 0:1], sqf[:], axis=AX), "vS", 1)
                        P.w(v, "vS")
                        P.inc(v.tensor_scalar(ssm[:, 1:2], ssm[:, 0:1], 1.0 / INTER, EPS,
                                              ALU.mult, ALU.add), "vS", 1)
                    @block.scalar
                    def _(s):
                        P.w(s, "vS")
                        P.inc(s.activation(ssm[:, 2:3], ssm[:, 1:2], AF.Ln), "sS", 1)
                        P.w(s, "sS")
                        P.inc(s.activation(ssm[:, 3:4], ssm[:, 2:3], AF.Exp, scale=-0.5), "sS", 1)
                    @block.vector
                    def _(v, cc=cc):
                        P.w(v, "sS")
                        P.w(v, "d2", sg_rdy[cc])
                        P.inc(v.tensor_scalar(sqf[:], ybuf[:], ssm[:, 3:4], None, ALU.mult), "vS", 1)
                        P.w(v, "vS")
                        P.inc(v.tensor_tensor(ygb[:], sqf[:], sgc[:], ALU.mult), "vS", 1)
                    sg_free = P.c["vS"]
                    for q in range(4):
                        @block.tensor
                        def _(t, q=q):
                            P.w(t, "vS")
                            for j8 in range(8):
                                ins = t.transpose(pxt[:, j8 * 128:(j8 + 1) * 128],
                                                  ygb[:, (q * 8 + j8) * 128:(q * 8 + j8 + 1) * 128], id_bf[:])
                            P.inc(ins, "tS", 1)
                        @block.vector
                        def _(v, q=q):
                            P.w(v, "tS")
                            P.w(v, "w0")
                            P.inc(v.tensor_copy(ygT[:, q * 1024:(q + 1) * 1024], pxt[:]), "vS", 1)
                    @block.scalar
                    def _(s, cc=cc):
                        P.w(s, "vS")
                        yvv = ygt_sp.rearrange("(i p) j -> p i j", p=128)
                        for q in range(4):
                            P.inc(s.dma_start(out=yvv[:, q * 8:(q + 1) * 8, cc * 128:(cc + 1) * 128],
                                              in_=ygT.rearrange("p (i j) -> p i j", j=128)[:, q * 8:(q + 1) * 8, :]),
                                  "w0", 16)
                if not full:
                    @block.vector
                    def _(v):
                        P.w(v, "vS")
                        P.inc(v.tensor_scalar(R[:], R[:], rmask_c[:], None, ALU.mult), "vS", 1)

        def emit_p2(xnT):
            with nc.sbuf_tensor("p2_w", [128, NKD * 1024], BF16, allow_name_mangling=True) as wq, \
                 nc.sbuf_tensor("p2_t", [128, 1024], F32, allow_name_mangling=True) as tmpf, \
                 nc.sbuf_tensor("p2_s", [128, 2 * 1024], BF16, allow_name_mangling=True) as sgt, \
                 nc.psum_tensor("p2_ps", [128, 2 * 1024], F32, allow_name_mangling=True) as ps2:
                barrier()
                free_w = 0
                w_rdy = [0] * 4
                free_sg = [0, 0]
                free_ps = [0, 0]
                free_tmp = 0
                xv = xnT.rearrange("p (k t) -> p k t", t=T)
                wv2 = wip.rearrange("(k p) c -> p k c", p=128)
                for q in range(4):
                    @block.gpsimd
                    def _(g, q=q):
                        P.w(g, "tS", free_w)
                        last = 0
                        for piece in range(4):
                            last = P.inc(g.dma_start(
                                out=wq.rearrange("p (k c) -> p k c", c=1024)[:, piece * 4:(piece + 1) * 4, :],
                                in_=wv2[:, piece * 4:(piece + 1) * 4, q * 1024:(q + 1) * 1024]), "g0", 16)
                        w_rdy[q] = last
                    for t_i in range(NTT):
                        sl = t_i % 2
                        @block.tensor
                        def _(t, q=q, t_i=t_i, sl=sl):
                            P.w(t, "g0", w_rdy[q])
                            P.w(t, "sS", free_ps[sl])
                            for k in range(NKD):
                                for n in range(2):
                                    ins = t.matmul(ps2[:, sl * 1024 + n * 512: sl * 1024 + (n + 1) * 512],
                                                   xv[:, k, t_i * 128:(t_i + 1) * 128],
                                                   wq.rearrange("p (k c) -> p k c", c=1024)[:, k, n * 512:(n + 1) * 512],
                                                   start=(k == 0), stop=(k == NKD - 1))
                            P.inc(ins, "tS", 1)
                        if t_i == NTT - 1:
                            free_w = P.c["tS"]
                        @block.scalar
                        def _(s, sl=sl):
                            P.w(s, "tS")
                            P.w(s, "vS", free_tmp)
                            free_ps[sl] = P.inc(s.activation(tmpf[:], ps2[:, sl * 1024:(sl + 1) * 1024],
                                                             AF.Silu), "sS", 1)
                        @block.vector
                        def _(v, q=q, sl=sl):
                            P.w(v, "sS")
                            P.w(v, "w0" if sl == 0 else "w1", free_sg[sl])
                            P.inc(v.tensor_tensor(sgt[:, sl * 1024:(sl + 1) * 1024], tmpf[:],
                                                  NWb[:, q * 1024:(q + 1) * 1024], ALU.mult), "vS", 1)
                        free_tmp = P.c["vS"]
                        @block.scalar
                        def _(s, q=q, t_i=t_i, sl=sl):
                            P.w(s, "vS")
                            free_sg[sl] = P.inc(s.dma_start(
                                out=sg_sp[t_i * 128:(t_i + 1) * 128, q * 1024:(q + 1) * 1024],
                                in_=sgt[:, sl * 1024:(sl + 1) * 1024]), "w0" if sl == 0 else "w1", 16)

        def emit_p4(xnT, xsrb):
            with nc.sbuf_tensor("p4_wg", [128, 2 * 2048], BF16, allow_name_mangling=True) as wg, \
                 nc.sbuf_tensor("p4_wu", [128, 2 * 2048], BF16, allow_name_mangling=True) as wu, \
                 nc.sbuf_tensor("p4_g", [128, HID], F32, allow_name_mangling=True) as gsf, \
                 nc.sbuf_tensor("p4_g2", [128, HID], F32, allow_name_mangling=True) as gs2, \
                 nc.sbuf_tensor("p4_gu", [128, 2 * HID], BF16, allow_name_mangling=True) as gub, \
                 nc.psum_tensor("p4_ps", [128, 2 * 2048], F32, allow_name_mangling=True) as ps4:
                barrier()
                w_rdy = [0] * NHT
                free_w = [0, 0]
                free_gu = [0, 0]
                mmg = [0] * NHT
                mmu = [0] * NHT
                xv = xnT.rearrange("p (k t) -> p k t", t=T)
                for ht in range(NHT + 1):
                    if ht < NHT:
                        @block.gpsimd
                        def _(g, ht=ht):
                            P.w(g, "tS", free_w[ht % 2])
                            last = 0
                            for k in range(NKD):
                                last = P.inc(g.dma_start(
                                    out=wg[:, (ht % 2) * 2048 + k * 128:(ht % 2) * 2048 + (k + 1) * 128],
                                    in_=wgt[k * 128:(k + 1) * 128, ht * 128:(ht + 1) * 128]), "g0" if ht % 2 == 0 else "g1", 16)
                                last = P.inc(g.dma_start(
                                    out=wu[:, (ht % 2) * 2048 + k * 128:(ht % 2) * 2048 + (k + 1) * 128],
                                    in_=wut[k * 128:(k + 1) * 128, ht * 128:(ht + 1) * 128]), "g0" if ht % 2 == 0 else "g1", 16)
                            w_rdy[ht] = last
                    if ht == 0:
                        continue
                    hh = ht - 1
                    sl = hh % 2
                    @block.tensor
                    def _(t, hh=hh, sl=sl):
                        P.w(t, "g0" if hh % 2 == 0 else "g1", w_rdy[hh])
                        P.w(t, "vS")
                        for (wbuf, half) in [(wg, 0), (wu, 1)]:
                            for k in range(NKD):
                                for n in range(4):
                                    t.matmul(ps4[:, half * 2048 + n * 512: half * 2048 + (n + 1) * 512],
                                             wbuf[:, sl * 2048 + k * 128: sl * 2048 + (k + 1) * 128],
                                             xv[:, k, n * 512:(n + 1) * 512],
                                             start=(k == 0), stop=False)
                            for n in range(4):
                                ins = t.matmul(ps4[:, half * 2048 + n * 512: half * 2048 + (n + 1) * 512],
                                               n128r[:], xsrb[0:1, n * 512:(n + 1) * 512],
                                               start=False, stop=True)
                            if half == 0:
                                mmg[hh] = P.inc(ins, "tS", 1)
                            else:
                                mmu[hh] = P.inc(ins, "tS", 1)
                        free_w[sl] = P.c["tS"]
                    @block.vector
                    def _(v, hh=hh):
                        P.w(v, "tS", mmg[hh])
                        P.inc(v.tensor_scalar(gsf[:], ps4[:, 0:2048], sg_c[:, hh:hh + 1], None, ALU.mult), "vS", 1)
                    @block.scalar
                    def _(s):
                        P.w(s, "vS")
                        P.inc(s.activation(gs2[:], gsf[:], AF.Silu), "sS", 1)
                    @block.vector
                    def _(v, hh=hh, sl=sl):
                        P.w(v, "tS", mmu[hh])
                        P.w(v, "sS")
                        P.inc(v.tensor_scalar(gsf[:], ps4[:, 2048:4096], su_c[:, hh:hh + 1], None, ALU.mult), "vS", 1)
                        P.w(v, "vS")
                        P.w(v, "w0" if sl == 0 else "w1", free_gu[sl])
                        P.inc(v.tensor_tensor(gub[:, sl * HID:(sl + 1) * HID], gs2[:], gsf[:], ALU.mult), "vS", 1)
                    @block.scalar
                    def _(s, hh=hh, sl=sl):
                        P.w(s, "vS")
                        free_gu[sl] = P.inc(s.dma_start(out=gu_sp[hh * 128:(hh + 1) * 128, :],
                                                        in_=gub[:, sl * HID:(sl + 1) * HID]), "w0" if sl == 0 else "w1", 16)

        def emit_p5():
            KH = 11
            with nc.sbuf_tensor("p5_wd", [128, KH * 2048], BF16, allow_name_mangling=True) as wdb, \
                 nc.sbuf_tensor("p5_gs", [128, 2 * KH * 128], BF16, allow_name_mangling=True) as gst, \
                 nc.sbuf_tensor("p5_pl", [128, 2 * HID], F32, allow_name_mangling=True) as pld, \
                 nc.sbuf_tensor("p5_tf", [128, HID], F32, allow_name_mangling=True) as tf5, \
                 nc.sbuf_tensor("p5_gc", [128, 20], F32, allow_name_mangling=True) as gcol, \
                 nc.sbuf_tensor("p5_pd", [128, 2 * HID], F32, allow_name_mangling=True) as pdst, \
                 nc.sbuf_tensor("p5_gb", [128, 2 * HID], BF16, allow_name_mangling=True) as gb5, \
                 nc.sbuf_tensor("p5_sd", [128, HID], F32, allow_name_mangling=True) as SDb, \
                 nc.sbuf_tensor("p5_r", [1, HID], F32, allow_name_mangling=True) as r_sd5, \
                 nc.psum_tensor("p5_ps", [128, 2 * 1024], F32, allow_name_mangling=True) as ps5, \
                 nc.psum_tensor("p5_pg", [128, 512], F32, allow_name_mangling=True) as psg5:
                barrier()
                pass
                @block.sync
                def _(sync):
                    P.inc(sync.dma_start(out=r_sd5[:], in_=sdr), "d0", 16)
                sd_rdy = P.c["d0"]
                for jj in range(HID // 512):
                    @block.tensor
                    def _(t, jj=jj):
                        P.w(t, "d0", sd_rdy)
                        P.w(t, "vS")
                        P.inc(t.matmul(psg5[:, 0:512], ones_r[:],
                                       r_sd5[0:1, jj * 512:(jj + 1) * 512], start=True, stop=True), "tS", 1)
                    @block.vector
                    def _(v, jj=jj):
                        P.w(v, "tS")
                        P.inc(v.tensor_copy(SDb[:, jj * 512:(jj + 1) * 512], psg5[:, 0:512]), "vS", 1)
                wv5 = wdt.rearrange("(k p) c -> p k c", p=128)
                free_gst = [0, 0]
                free_pl = [0, 0]
                free_pd = [0, 0]
                free_gb = [0, 0]
                ev_ps = [("vS", 0), ("vS", 0)]
                for pz in range(4):
                    w_rdy = [0]
                    @block.gpsimd
                    def _(g, pz=pz):
                        P.w(g, "tS")
                        for piece in range(3):
                            k0 = piece * 4
                            kn = min(4, KH - k0)
                            w_rdy[0] = P.inc(g.dma_start(
                                out=wdb.rearrange("p (k c) -> p k c", c=2048)[:, k0:k0 + kn, :],
                                in_=wv5[:, pz * KH + k0: pz * KH + k0 + kn, :]), "g0", 16)
                    if pz > 0:
                        @block.sync
                        def _(sync):
                            P.w(sync, "w0"); P.w(sync, "w1")
                    gst_rdy = [0] * NTT
                    pl_rdy = [0] * NTT
                    for t_o in range(NTT + 1):
                        if t_o < NTT:
                            t_i, sl = t_o, t_o % 2
                            @block.sync
                            def _(sync, pz=pz, t_i=t_i, sl=sl):
                                P.w(sync, "tS", free_gst[sl])
                                last = 0
                                for k in range(KH):
                                    last = P.inc(sync.dma_start(
                                        out=gst[:, sl * (KH * 128) + k * 128: sl * (KH * 128) + (k + 1) * 128],
                                        in_=gu_sp[(pz * KH + k) * 128:(pz * KH + k + 1) * 128,
                                                  t_i * 128:(t_i + 1) * 128]), "d1" if sl == 0 else "d2", 16)
                                gst_rdy[t_i] = last
                                if pz > 0:
                                    P.w(sync, "vS", free_pl[sl])
                                    pl_rdy[t_i] = P.inc(sync.dma_start(
                                        out=pld[:, sl * HID:(sl + 1) * HID],
                                        in_=pp_sp[t_i * 128:(t_i + 1) * 128, :]), "d3" if sl == 0 else "d4", 16)
                        if t_o == 0:
                            continue
                        t_i = t_o - 1
                        sl = t_i % 2
                        for dh in range(2):
                            @block.tensor
                            def _(t, pz=pz, t_i=t_i, sl=sl, dh=dh):
                                P.w(t, "g0", w_rdy[0])
                                P.w(t, "d1" if sl == 0 else "d2", gst_rdy[t_i])
                                P.w(t, ev_ps[sl][0], ev_ps[sl][1])
                                for k in range(KH):
                                    for n in range(2):
                                        ins = t.matmul(
                                            ps5[:, sl * 1024 + n * 512: sl * 1024 + (n + 1) * 512],
                                            gst[:, sl * (KH * 128) + k * 128: sl * (KH * 128) + (k + 1) * 128],
                                            wdb.rearrange("p (k c) -> p k c", c=2048)[:, k, dh * 1024 + n * 512: dh * 1024 + (n + 1) * 512],
                                            start=(k == 0), stop=(k == KH - 1))
                                P.inc(ins, "tS", 1)
                            if pz == 0:
                                @block.scalar
                                def _(s, t_i=t_i, sl=sl, dh=dh):
                                    P.w(s, "tS")
                                    if dh == 0:
                                        P.w(s, "w0" if sl == 0 else "w1", free_pd[sl])
                                    c = P.inc(s.copy(pdst[:, sl * HID + dh * 1024: sl * HID + (dh + 1) * 1024],
                                                     ps5[:, sl * 1024: sl * 1024 + 1024]), "sS", 1)
                                    ev_ps[sl] = ("sS", c)
                            elif pz < 3:
                                @block.vector
                                def _(v, t_i=t_i, sl=sl, dh=dh):
                                    P.w(v, "tS")
                                    P.w(v, "d3" if sl == 0 else "d4", pl_rdy[t_i])
                                    if dh == 0:
                                        P.w(v, "w0" if sl == 0 else "w1", free_pd[sl])
                                    c = P.inc(v.tensor_tensor(
                                        pdst[:, sl * HID + dh * 1024: sl * HID + (dh + 1) * 1024],
                                        ps5[:, sl * 1024: sl * 1024 + 1024],
                                        pld[:, sl * HID + dh * 1024: sl * HID + (dh + 1) * 1024],
                                        ALU.add), "vS", 1)
                                    ev_ps[sl] = ("vS", c)
                                    if dh == 1:
                                        free_pl[sl] = c
                            else:
                                @block.vector
                                def _(v, t_i=t_i, sl=sl, dh=dh):
                                    P.w(v, "tS")
                                    P.w(v, "d3" if sl == 0 else "d4", pl_rdy[t_i])
                                    P.w(v, "sS")
                                    c = P.inc(v.tensor_tensor(
                                        tf5[:, dh * 1024:(dh + 1) * 1024],
                                        ps5[:, sl * 1024: sl * 1024 + 1024],
                                        pld[:, sl * HID + dh * 1024: sl * HID + (dh + 1) * 1024],
                                        ALU.add), "vS", 1)
                                    ev_ps[sl] = ("vS", c)
                                    if dh == 1:
                                        free_pl[sl] = c
                        @block.tensor
                        def _(t, pz=pz, sl=sl):
                            P.w(t, "vS")
                            for k in range(KH):
                                ins = t.matmul(psg5[:, 0:1],
                                               gst[:, sl * (KH * 128) + k * 128: sl * (KH * 128) + (k + 1) * 128],
                                               ones_cb[:], start=(k == 0), stop=(k == KH - 1))
                            P.inc(ins, "tS", 1)
                            free_gst[sl] = P.c["tS"]
                        if pz == 0:
                            @block.vector
                            def _(v, t_i=t_i):
                                P.w(v, "tS")
                                P.inc(v.tensor_copy(gcol[:, t_i:t_i + 1], psg5[:, 0:1]), "vS", 1)
                        else:
                            @block.vector
                            def _(v, t_i=t_i):
                                P.w(v, "tS")
                                P.inc(v.tensor_tensor(gcol[:, t_i:t_i + 1], psg5[:, 0:1],
                                                      gcol[:, t_i:t_i + 1], ALU.add), "vS", 1)
                        if pz < 3:
                            @block.scalar
                            def _(s, pz=pz, t_i=t_i, sl=sl):
                                P.w(s, ev_ps[sl][0], ev_ps[sl][1])
                                free_pd[sl] = P.inc(s.dma_start(out=pp_sp[t_i * 128:(t_i + 1) * 128, :],
                                                                in_=pdst[:, sl * HID:(sl + 1) * HID]), "w0" if sl == 0 else "w1", 16)
                        else:
                            @block.vector
                            def _(v, t_i=t_i, sl=sl):
                                P.w(v, "vS")
                                P.inc(v.tensor_scalar(gcol[:, 17:18], gcol[:, t_i:t_i + 1], 128.0, None, ALU.mult), "vS", 1)
                                P.w(v, "vS")
                                P.inc(v.tensor_scalar(tf5[:], tf5[:], gcol[:, 17:18], None, ALU.subtract), "vS", 1)
                                P.w(v, "vS")
                                P.inc(v.tensor_tensor(tf5[:], tf5[:], SDb[:], ALU.mult), "vS", 1)
                            @block.scalar
                            def _(s, t_i=t_i, sl=sl):
                                P.w(s, "vS")
                                P.w(s, "w2" if sl == 0 else "w3", free_gb[sl])
                                P.inc(s.activation(gb5[:, sl * HID:(sl + 1) * HID], tf5[:], AF.Silu), "sS", 1)
                                P.w(s, "sS")
                                free_gb[sl] = P.inc(s.dma_start(out=g_sp[t_i * 128:(t_i + 1) * 128, :],
                                                                in_=gb5[:, sl * HID:(sl + 1) * HID]), "w2" if sl == 0 else "w3", 16)

        def emit_p7():
            with nc.sbuf_tensor("p7_w", [128, NKD * 2048], BF16, allow_name_mangling=True) as woh, \
                 nc.sbuf_tensor("p7_yg", [128, 2 * 2048], BF16, allow_name_mangling=True) as ygs, \
                 nc.sbuf_tensor("p7_pl", [128, 2 * HID], F32, allow_name_mangling=True) as pol, \
                 nc.sbuf_tensor("p7_g", [128, 2 * HID], BF16, allow_name_mangling=True) as gl7, \
                 nc.sbuf_tensor("p7_x", [128, 2 * HID], F32, allow_name_mangling=True) as xl7, \
                 nc.sbuf_tensor("p7_o", [128, 2 * HID], F32, allow_name_mangling=True) as ob7, \
                 nc.psum_tensor("p7_ps", [128, 2 * 2048], F32, allow_name_mangling=True) as ps7:
                barrier()
                wv7 = wout.rearrange("(k p) c -> p k c", p=128)
                free_st = [0, 0]
                free_ps = [("sS", 0), ("sS", 0)]
                free_ld = [0, 0]
                free_ob = [("w0", 0), ("w1", 0)]
                for half in range(2):
                    w_rdy = [0]
                    @block.gpsimd
                    def _(g, half=half):
                        P.w(g, "tS")
                        last = 0
                        for piece in range(4):
                            last = P.inc(g.dma_start(
                                out=woh.rearrange("p (k c) -> p k c", c=2048)[:, piece * 4:(piece + 1) * 4, :],
                                in_=wv7[:, half * NKD + piece * 4: half * NKD + (piece + 1) * 4, :]), "g0", 16)
                        w_rdy[0] = last
                    st_rdy = [0] * NTT
                    ld_rdy = [0] * NTT
                    for t_o in range(NTT + 1):
                        if t_o < NTT:
                            t_i, sl = t_o, t_o % 2
                            emit_ld = True
                        else:
                            emit_ld = False
                        if emit_ld:
                            @block.sync
                            def _(sync, half=half, t_i=t_i, sl=sl):
                                P.w(sync, "tS", free_st[sl])
                                last = 0
                                for k in range(NKD):
                                    last = P.inc(sync.dma_start(
                                        out=ygs[:, sl * 2048 + k * 128: sl * 2048 + (k + 1) * 128],
                                        in_=ygt_sp[(half * NKD + k) * 128:(half * NKD + k + 1) * 128,
                                                   t_i * 128:(t_i + 1) * 128]), "d0" if sl == 0 else "d1", 16)
                                st_rdy[t_i] = last
                                if half == 1:
                                    if t_i == 0:
                                        P.w(sync, "w0"); P.w(sync, "w1")
                                    P.w(sync, "vS", free_ld[sl])
                                    P.inc(sync.dma_start(out=pol[:, sl * HID:(sl + 1) * HID],
                                                         in_=pp_sp[t_i * 128:(t_i + 1) * 128, :]), "d2" if sl == 0 else "d3", 16)
                                    P.inc(sync.dma_start(out=gl7[:, sl * HID:(sl + 1) * HID],
                                                         in_=g_sp[t_i * 128:(t_i + 1) * 128, :]), "d2" if sl == 0 else "d3", 16)
                                    ld_rdy[t_i] = P.inc(sync.dma_start(out=xl7[:, sl * HID:(sl + 1) * HID],
                                                                       in_=xo[t_i * 128:(t_i + 1) * 128, :]), "d2" if sl == 0 else "d3", 16)
                        if t_o == 0:
                            continue
                        t_i = t_o - 1
                        sl = t_i % 2
                        @block.tensor
                        def _(t, half=half, t_i=t_i, sl=sl):
                            P.w(t, "g0", w_rdy[0])
                            P.w(t, "d0" if sl == 0 else "d1", st_rdy[t_i])
                            P.w(t, free_ps[sl][0], free_ps[sl][1])
                            for k in range(NKD):
                                for n in range(4):
                                    ins = t.matmul(ps7[:, sl * 2048 + n * 512: sl * 2048 + (n + 1) * 512],
                                                   ygs[:, sl * 2048 + k * 128: sl * 2048 + (k + 1) * 128],
                                                   woh.rearrange("p (k c) -> p k c", c=2048)[:, k, n * 512:(n + 1) * 512],
                                                   start=(k == 0), stop=(k == NKD - 1))
                            P.inc(ins, "tS", 1)
                            free_st[sl] = P.c["tS"]
                        if half == 0:
                            @block.scalar
                            def _(s, t_i=t_i, sl=sl):
                                P.w(s, "tS")
                                P.w(s, free_ob[sl][0], free_ob[sl][1])
                                P.inc(s.copy(ob7[:, sl * HID:(sl + 1) * HID], ps7[:, sl * 2048: sl * 2048 + 2048]), "sS", 1)
                                free_ps[sl] = ("sS", P.c["sS"])
                                P.w(s, "sS")
                                sem = "w0" if sl == 0 else "w1"
                                free_ob[sl] = (sem, P.inc(s.dma_start(out=pp_sp[t_i * 128:(t_i + 1) * 128, :],
                                                                      in_=ob7[:, sl * HID:(sl + 1) * HID]), sem, 16))
                        else:
                            @block.vector
                            def _(v, t_i=t_i, sl=sl):
                                P.w(v, "tS")
                                P.w(v, "d2" if sl == 0 else "d3", ld_rdy[t_i])
                                P.w(v, free_ob[sl][0], free_ob[sl][1])
                                obs = ob7[:, sl * HID:(sl + 1) * HID]
                                P.inc(v.tensor_tensor(obs, ps7[:, sl * 2048: sl * 2048 + 2048],
                                                pol[:, sl * HID:(sl + 1) * HID], ALU.add), "vS", 1)
                                P.w(v, "vS")
                                free_ps[sl] = ("vS", P.c["vS"])
                                P.inc(v.tensor_tensor(obs, obs, gl7[:, sl * HID:(sl + 1) * HID], ALU.mult), "vS", 1)
                                P.w(v, "vS")
                                P.inc(v.tensor_tensor(obs, obs, xl7[:, sl * HID:(sl + 1) * HID], ALU.add), "vS", 1)
                                free_ld[sl] = P.c["vS"]
                            @block.scalar
                            def _(s, t_i=t_i, sl=sl):
                                P.w(s, "vS")
                                sem = "w2" if sl == 0 else "w3"
                                free_ob[sl] = (sem, P.inc(s.dma_start(out=outd[t_i * 128:(t_i + 1) * 128, :],
                                                                      in_=ob7[:, sl * HID:(sl + 1) * HID]), sem, 16))

        with nc.sbuf_tensor("xnT", [128, NKD * T], BF16, allow_name_mangling=True) as xnT, \
             nc.sbuf_tensor("xsrb", [1, T], BF16, allow_name_mangling=True) as xsrb:
            emit_ln(xc, False, xnT, xsrb)
            emit_inprojT(True, xnT)
            with nc.sbuf_tensor("c_d1", [128, NCH * 64], F32, allow_name_mangling=True) as dtT_all, \
                 nc.sbuf_tensor("c_d2", [128, NCH * 64], F32, allow_name_mangling=True) as dsT_all, \
                 nc.sbuf_tensor("c_d3", [128, NCH * 64], F32, allow_name_mangling=True) as etb_all, \
                 nc.sbuf_tensor("c_d4", [128, NCH * 64], F32, allow_name_mangling=True) as sdo_all, \
                 nc.sbuf_tensor("c_d5", [128, NCH * 64], F32, allow_name_mangling=True) as negA_all:
                emit_prepass(dtT_all, dsT_all, etb_all, sdo_all, negA_all, False)
                emit_ssd(False, dtT_all, dsT_all, etb_all, sdo_all, negA_all)
            emit_ln(xo, True, xnT, xsrb)
            emit_p2(xnT)
            emit_inprojT(False, xnT)
            emit_p4(xnT, xsrb)
        emit_p5()
        with nc.sbuf_tensor("o_d1", [128, NCH * 64], F32, allow_name_mangling=True) as dtT_all, \
             nc.sbuf_tensor("o_d2", [128, NCH * 64], F32, allow_name_mangling=True) as dsT_all, \
             nc.sbuf_tensor("o_d3", [128, NCH * 64], F32, allow_name_mangling=True) as etb_all, \
             nc.sbuf_tensor("o_d4", [128, NCH * 64], F32, allow_name_mangling=True) as sdo_all, \
             nc.sbuf_tensor("o_d5", [128, NCH * 64], F32, allow_name_mangling=True) as negA_all:
            emit_prepass(dtT_all, dsT_all, etb_all, sdo_all, negA_all, True)
            emit_ssd(True, dtT_all, dsT_all, etb_all, sdo_all, negA_all)
        emit_p7()

        @block.gpsimd
        def _(g):
            pass
            for s in ("d0", "d1", "d2", "d3", "d4", "g0", "g1", "w0", "w1", "w2", "w3", "tS", "vS", "sS"): P.w(g, s)
            pass

    es.close()
    return nc


def kernel(**inputs):
    xs = np.ascontiguousarray(np.asarray(inputs["L0_attn_hidden_states"], dtype=np.float32))
    wip = np.ascontiguousarray(np.asarray(inputs["L0_attn_in_proj_w"], dtype=np.float32))
    convw = np.ascontiguousarray(np.asarray(inputs["L0_attn_conv_w"], dtype=np.float32))
    convb = np.ascontiguousarray(np.asarray(inputs["L0_attn_conv_b"], dtype=np.float32).reshape(1, -1))
    dtb = np.ascontiguousarray(np.asarray(inputs["L0_attn_dt_bias"], dtype=np.float32).reshape(1, -1))
    alog = np.ascontiguousarray(np.asarray(inputs["L0_attn_A_log"], dtype=np.float32).reshape(1, -1))
    dvec = np.ascontiguousarray(np.asarray(inputs["L0_attn_D"], dtype=np.float32).reshape(1, -1))
    nw = np.ascontiguousarray(np.asarray(inputs["L0_attn_norm_weight"], dtype=np.float32).reshape(1, -1))
    wout = np.ascontiguousarray(np.asarray(inputs["L0_attn_out_proj_w"], dtype=np.float32))
    wgt = np.ascontiguousarray(np.asarray(inputs["L0_ffn_wq_gate"], dtype=np.int32).T)
    wut = np.ascontiguousarray(np.asarray(inputs["L0_ffn_wq_up"], dtype=np.int32).T)
    wdt = np.ascontiguousarray(np.asarray(inputs["L0_ffn_wq_down"], dtype=np.int32).T)
    sg = np.ascontiguousarray(np.asarray(inputs["L0_ffn_s_gate"], dtype=np.float32).reshape(1, -1))
    su = np.ascontiguousarray(np.asarray(inputs["L0_ffn_s_up"], dtype=np.float32).reshape(1, -1))
    sd = np.ascontiguousarray(np.asarray(inputs["L0_ffn_s_down"], dtype=np.float32).reshape(1, -1))

    idbf = np.eye(128, dtype=np.float32).astype(BF)
    idf = np.eye(128, dtype=np.float32)
    s_i = np.arange(128)[:, None]
    l_i = np.arange(128)[None, :]
    trip = (s_i <= l_i).astype(np.float32)
    trin = -trip
    maskt = np.where(s_i > l_i, np.float32(-1e9), np.float32(0.0)).astype(BF)
    onesr = np.ones((1, 128), np.float32)
    onesrb = np.ones((1, 128), np.float32).astype(BF)
    onescb = np.ones((128, 1), np.float32).astype(BF)
    n128 = np.full((1, 128), -128.0, np.float32).astype(BF)

    zeros_ctx = np.zeros((T, HID), np.float32)
    nc = build_program()
    in_maps = []
    for core in range(8):
        b, h = core // 2, core % 2
        xo_ = np.ascontiguousarray(xs[b, h * T:(h + 1) * T, :])
        xc_ = np.ascontiguousarray(xs[b, 0:T, :]) if h == 1 else zeros_ctx
        rm = np.array([[1.0 if h == 1 else 0.0]], np.float32)
        in_maps.append({
            "xo": xo_, "xc": xc_, "rmaskd": rm, "wip": wip, "convw": convw,
            "convbr": convb, "dtbr": dtb, "alogr": alog, "dvecr": dvec, "nwr": nw,
            "wout": wout, "wgt": wgt, "wut": wut, "wdt": wdt, "sgr": sg, "sur": su,
            "sdr": sd, "idbf_d": idbf, "idf_d": idf, "trip_d": trip, "trin_d": trin,
            "maskt_d": maskt, "onesr_d": onesr, "onesrb_d": onesrb,
            "onescb_d": onescb, "n128_d": n128,
        })
    res = run_bass_kernel_spmd(nc, in_maps, core_ids=list(range(8)))
    out = np.empty((4, 4096, HID), np.float32)
    for core in range(8):
        b, h = core // 2, core % 2
        out[b, h * T:(h + 1) * T, :] = res.results[core]["outd"]
    return out


# version 05 (final)
# speedup vs baseline: 1.0271x; 1.0150x over previous
import numpy as np
import ml_dtypes
from contextlib import ExitStack
import concourse.bass as bass
import concourse.mybir as mybir
from concourse.bass_utils import run_bass_kernel_spmd

F32 = mybir.dt.float32
BF16 = mybir.dt.bfloat16
I32 = mybir.dt.int32
AF = mybir.ActivationFunctionType
ALU = mybir.AluOpType
BF = ml_dtypes.bfloat16
AX = mybir.AxisListType.X

HID = 2048
INTER = 4096
NH = 64
FFH = 5632
T = 2048
CS = 128
NCH = T // CS
NTT = T // 128
NKD = HID // 128
NHT = FFH // 128
EPS = 1e-5


class Pr:
    def __init__(self, nc, es):
        self.nc, self.es = nc, es
        self.h, self.c = {}, {}

    def sem(self, *names):
        for n in names:
            self.h[n] = self.es.enter_context(self.nc.semaphore(n))
            self.c[n] = 0

    def inc(self, instr, s, n=1):
        instr.then_inc(self.h[s], n)
        self.c[s] += n
        return self.c[s]

    def w(self, eng, s, v=None):
        v = self.c[s] if v is None else v
        if v > 0:
            eng.wait_ge(self.h[s], v)


def build_program():
    nc = bass.Bass("TRN2", target_bir_lowering=False)
    dt_ = nc.dram_tensor
    xo = dt_("xo", [T, HID], F32, kind="ExternalInput").ap()
    xc = dt_("xc", [T, HID], F32, kind="ExternalInput").ap()
    rmaskd = dt_("rmaskd", [1, 1], F32, kind="ExternalInput").ap()
    wip = dt_("wip", [HID, 8512], F32, kind="ExternalInput").ap()
    convw = dt_("convw", [4352, 4], F32, kind="ExternalInput").ap()
    convbr = dt_("convbr", [1, 4352], F32, kind="ExternalInput").ap()
    dtbr = dt_("dtbr", [1, NH], F32, kind="ExternalInput").ap()
    alogr = dt_("alogr", [1, NH], F32, kind="ExternalInput").ap()
    dvecr = dt_("dvecr", [1, NH], F32, kind="ExternalInput").ap()
    nwr = dt_("nwr", [1, INTER], F32, kind="ExternalInput").ap()
    wout = dt_("wout", [INTER, HID], F32, kind="ExternalInput").ap()
    wgt = dt_("wgt", [HID, FFH], I32, kind="ExternalInput").ap()
    wut = dt_("wut", [HID, FFH], I32, kind="ExternalInput").ap()
    wdt = dt_("wdt", [FFH, HID], I32, kind="ExternalInput").ap()
    sgr = dt_("sgr", [1, FFH], F32, kind="ExternalInput").ap()
    sur = dt_("sur", [1, FFH], F32, kind="ExternalInput").ap()
    sdr = dt_("sdr", [1, HID], F32, kind="ExternalInput").ap()
    idbf_d = dt_("idbf_d", [128, 128], BF16, kind="ExternalInput").ap()
    idf_d = dt_("idf_d", [128, 128], F32, kind="ExternalInput").ap()
    trip_d = dt_("trip_d", [128, 128], F32, kind="ExternalInput").ap()
    trin_d = dt_("trin_d", [128, 128], F32, kind="ExternalInput").ap()
    maskt_d = dt_("maskt_d", [128, 128], BF16, kind="ExternalInput").ap()
    onesr_d = dt_("onesr_d", [1, 128], F32, kind="ExternalInput").ap()
    onesrb_d = dt_("onesrb_d", [1, 128], BF16, kind="ExternalInput").ap()
    onescb_d = dt_("onescb_d", [128, 1], BF16, kind="ExternalInput").ap()
    n128_d = dt_("n128_d", [1, 128], BF16, kind="ExternalInput").ap()
    x_sp = dt_("x_sp", [INTER, T], BF16, kind="Internal").ap()
    gu_sp = dt_("gu_sp", [FFH, T], BF16, kind="Internal").ap()
    sg_sp = dt_("sg_sp", [T, INTER], BF16, kind="Internal").ap()
    g_sp = dt_("g_sp", [T, HID], BF16, kind="Internal").ap()
    pp_sp = dt_("pp_sp", [T, HID], F32, kind="Internal").ap()
    ygt_sp = dt_("ygt_sp", [INTER, T], BF16, kind="Internal").ap()
    scrH = dt_("scrH", [NCH, NH * CS], BF16, kind="Internal").ap()
    scrL = dt_("scrL", [NCH, NH * CS], BF16, kind="Internal").ap()
    scrX = dt_("scrX", [16, 128], F32, kind="Internal").ap()
    outd = dt_("outd", [T, HID], F32, kind="ExternalOutput").ap()

    es = ExitStack()
    sbo = lambda name, shape, dt: es.enter_context(nc.sbuf_tensor(name, shape, dt))

    id_bf = sbo("id_bf", [128, 128], BF16)
    id_f = sbo("id_f", [128, 128], F32)
    triP = sbo("triP", [128, 128], F32)
    triN = sbo("triN", [128, 128], F32)
    maskT = sbo("maskT", [128, 128], BF16)
    ones_r = sbo("ones_r", [1, 128], F32)
    ones_rb = sbo("ones_rb", [1, 128], BF16)
    ones_cb = sbo("ones_cb", [128, 1], BF16)
    n128r = sbo("n128r", [1, 128], BF16)
    ones11 = sbo("ones11", [1, 1], F32)
    convw_s = sbo("convw_s", [128, 140], F32)
    convb_c = sbo("convb_c", [128, 35], F32)
    sg_c = sbo("sg_c", [128, NHT], F32)
    su_c = sbo("su_c", [128, NHT], F32)
    dtb_c = sbo("dtb_c", [64, 1], F32)
    A_c = sbo("A_c", [64, 1], F32)
    D_b = sbo("D_b", [128, 64], F32)
    rmask_c = sbo("rmask_c", [128, 1], F32)
    NWb = sbo("NWb", [128, INTER], F32)
    Bt = sbo("Bt", [128, T], BF16)
    Ctn = sbo("Ctn", [128, T], BF16)
    dts = sbo("dts", [64, T], F32)
    adts = sbo("adts", [64, T], F32)
    R = sbo("Rst", [128, INTER], F32)
    hist = sbo("hist", [128, 105], F32)

    P = Pr(nc, es)
    with nc.Block() as block:
        P.sem("d0", "d1", "d2", "d3", "d4", "g0", "g1", "w0", "w1", "w2", "w3", "tS", "vS", "sS")

        def barrier():
            for engname in ("sync", "gpsimd", "vector", "scalar", "tensor"):
                @getattr(block, engname)
                def _(e):
                    for s in ("d0", "d1", "d2", "d3", "d4", "g0", "g1", "w0", "w1", "w2", "w3", "tS", "vS", "sS"):
                        P.w(e, s)

        with nc.sbuf_tensor("r_cb", [1, 4352], F32, allow_name_mangling=True) as r_cb, \
             nc.sbuf_tensor("r_sg", [1, FFH], F32, allow_name_mangling=True) as r_sg, \
             nc.sbuf_tensor("r_su", [1, FFH], F32, allow_name_mangling=True) as r_su, \
             nc.sbuf_tensor("r_sm", [1, 193], F32, allow_name_mangling=True) as r_sm, \
             nc.sbuf_tensor("r_nw", [1, INTER], F32, allow_name_mangling=True) as r_nw, \
             nc.sbuf_tensor("t_c64", [64, 1], F32, allow_name_mangling=True) as t_c64, \
             nc.psum_tensor("ps_set", [128, 512], F32, allow_name_mangling=True) as ps_set:

            @block.sync
            def _(sync):
                for dst, src in [(id_bf, idbf_d), (id_f, idf_d), (triP, trip_d),
                                 (triN, trin_d), (maskT, maskt_d), (ones_r, onesr_d),
                                 (ones_rb, onesrb_d), (ones_cb, onescb_d), (n128r, n128_d),
                                 (r_cb, convbr), (r_sg, sgr), (r_su, sur), (r_nw, nwr)]:
                    P.inc(sync.dma_start(out=dst[:], in_=src), "d0", 16)
                P.inc(sync.dma_start(out=r_sm[:, 0:64], in_=dtbr), "d0", 16)
                P.inc(sync.dma_start(out=r_sm[:, 64:128], in_=alogr), "d0", 16)
                P.inc(sync.dma_start(out=r_sm[:, 128:192], in_=dvecr), "d0", 16)
                P.inc(sync.dma_start(out=r_sm[:, 192:193], in_=rmaskd), "d0", 16)
                for j in range(34):
                    P.inc(sync.dma_start(out=convw_s[:, j * 4:(j + 1) * 4],
                                         in_=convw[j * 128:(j + 1) * 128, :]), "d0", 16)
            dma_setup = P.c["d0"]

            @block.vector
            def _(v):
                P.w(v, "d0", dma_setup)
                v.memset(ones11[:], 1.0)
                v.memset(hist[:], 0.0)
                P.inc(v.memset(R[:], 0.0), "vS", 1)

            def rows_to_cols(row_ap, n, dst_fn):
                j = 0
                while j < n:
                    m = min(8, n - j)
                    @block.tensor
                    def _(t, j=j, m=m):
                        P.w(t, "vS")
                        for i in range(m):
                            P.inc(t.matmul(ps_set[:, j + i:j + i + 1],
                                           row_ap[0:1, (j + i) * 128:(j + i + 1) * 128],
                                           ones11[:], start=True, stop=True), "tS", 1)
                    @block.vector
                    def _(v, j=j, m=m):
                        P.w(v, "tS")
                        for i in range(m):
                            P.inc(v.tensor_copy(dst_fn(j + i), ps_set[:, j + i:j + i + 1]), "vS", 1)
                    j += m

            rows_to_cols(r_cb, 34, lambda j: convb_c[:, j:j + 1])
            rows_to_cols(r_sg, NHT, lambda j: sg_c[:, j:j + 1])
            rows_to_cols(r_su, NHT, lambda j: su_c[:, j:j + 1])

            @block.tensor
            def _(t):
                P.w(t, "vS")
                P.inc(t.matmul(ps_set[0:64, 0:1], r_sm[0:1, 0:64], ones11[:], start=True, stop=True), "tS", 1)
                P.inc(t.matmul(ps_set[0:64, 1:2], r_sm[0:1, 64:128], ones11[:], start=True, stop=True), "tS", 1)
                P.inc(t.matmul(ps_set[:, 64:128], ones_r[:], r_sm[0:1, 128:192], start=True, stop=True), "tS", 1)
                P.inc(t.matmul(ps_set[:, 128:129], ones_r[:], r_sm[0:1, 192:193], start=True, stop=True), "tS", 1)
            @block.vector
            def _(v):
                P.w(v, "tS")
                v.tensor_copy(dtb_c[:], ps_set[0:64, 0:1])
                v.tensor_copy(t_c64[:], ps_set[0:64, 1:2])
                v.tensor_copy(D_b[:], ps_set[:, 64:128])
                P.inc(v.tensor_copy(rmask_c[:], ps_set[:, 128:129]), "vS", 1)
            @block.scalar
            def _(s):
                P.w(s, "vS")
                P.inc(s.activation(A_c[:], t_c64[:], AF.Exp), "sS", 1)
            @block.vector
            def _(v):
                P.w(v, "sS")
                P.inc(v.tensor_scalar(A_c[:], A_c[:], -1.0, None, ALU.mult), "vS", 1)
            for jj in range(INTER // 512):
                @block.tensor
                def _(t, jj=jj):
                    P.w(t, "vS")
                    P.inc(t.matmul(ps_set[:, 0:512], ones_r[:],
                                   r_nw[0:1, jj * 512:(jj + 1) * 512], start=True, stop=True), "tS", 1)
                @block.vector
                def _(v, jj=jj):
                    P.w(v, "tS")
                    P.inc(v.tensor_copy(NWb[:, jj * 512:(jj + 1) * 512], ps_set[:, 0:512]), "vS", 1)

        def emit_ln(x_dram, collect_xsum, xnT, xsrb):
            with nc.sbuf_tensor("ln_xb", [128, 2 * HID], F32, allow_name_mangling=True) as xb, \
                 nc.sbuf_tensor("ln_xc", [128, 2 * HID], F32, allow_name_mangling=True) as xcb, \
                 nc.sbuf_tensor("ln_xn", [128, 2 * HID], BF16, allow_name_mangling=True) as xnb, \
                 nc.sbuf_tensor("ln_sm", [128, 8], F32, allow_name_mangling=True) as sm, \
                 nc.sbuf_tensor("ln_xs", [128, 16], F32, allow_name_mangling=True) as xsc, \
                 nc.sbuf_tensor("ln_xr", [16, 128], F32, allow_name_mangling=True) as xrow, \
                 nc.sbuf_tensor("ln_x1", [1, T], F32, allow_name_mangling=True) as xsr, \
                 nc.psum_tensor("ln_pt", [128, 2048], BF16, allow_name_mangling=True) as ps_t, \
                 nc.psum_tensor("ln_pf", [128, 128], F32, allow_name_mangling=True) as ps_f:
                barrier()
                free_x = [0, 0]
                free_xc = [0, 0]
                free_xn = [0, 0]
                free_ps = 0
                rdy = [0] * NTT
                for i in range(NTT):
                    sl = i % 2
                    @block.sync
                    def _(sync, i=i, sl=sl):
                        P.w(sync, "vS", free_x[sl])
                        rdy[i] = P.inc(sync.dma_start(out=xb[:, sl * HID:(sl + 1) * HID],
                                                      in_=x_dram[i * 128:(i + 1) * 128, :]), "d0" if sl == 0 else "d1", 16)
                    @block.vector
                    def _(v, i=i, sl=sl):
                        P.w(v, "d0" if sl == 0 else "d1", rdy[i])
                        xbs = xb[:, sl * HID:(sl + 1) * HID]
                        P.inc(v.reduce_sum(sm[:, 0:1], xbs, axis=AX), "vS", 1)
                        P.w(v, "vS")
                        P.inc(v.tensor_scalar(sm[:, 1:2], sm[:, 0:1], -1.0 / HID, None, ALU.mult), "vS", 1)
                        P.w(v, "vS")
                        P.w(v, "vS", free_xc[sl])
                        P.inc(v.tensor_scalar(xcb[:, sl * HID:(sl + 1) * HID], xbs,
                                              sm[:, 1:2], None, ALU.add), "vS", 1)
                    @block.vector
                    def _(v, sl=sl):
                        P.w(v, "vS")
                        P.inc(v.tensor_tensor(xb[:, sl * HID:(sl + 1) * HID],
                                              xcb[:, sl * HID:(sl + 1) * HID],
                                              xcb[:, sl * HID:(sl + 1) * HID], ALU.mult), "vS", 1)
                        P.w(v, "vS")
                        P.inc(v.reduce_sum(sm[:, 2:3], xb[:, sl * HID:(sl + 1) * HID], axis=AX), "vS", 1)
                        P.w(v, "vS")
                        free_x[sl] = P.inc(v.tensor_scalar(sm[:, 3:4], sm[:, 2:3],
                                                           1.0 / HID, EPS, ALU.mult, ALU.add), "vS", 1)
                    @block.scalar
                    def _(s):
                        P.w(s, "vS")
                        P.inc(s.activation(sm[:, 4:5], sm[:, 3:4], AF.Ln), "sS", 1)
                        P.w(s, "sS")
                        P.inc(s.activation(sm[:, 5:6], sm[:, 4:5], AF.Exp, scale=-0.5), "sS", 1)
                    @block.vector
                    def _(v, i=i, sl=sl):
                        P.w(v, "sS")
                        P.w(v, "tS", free_xn[sl])
                        free_xc[sl] = P.inc(v.tensor_scalar(xnb[:, sl * HID:(sl + 1) * HID],
                                                            xcb[:, sl * HID:(sl + 1) * HID],
                                                            sm[:, 5:6], None, ALU.mult), "vS", 1)
                        if collect_xsum:
                            P.w(v, "vS")
                            P.inc(v.reduce_sum(xsc[:, i:i + 1], xnb[:, sl * HID:(sl + 1) * HID],
                                               axis=AX), "vS", 1)
                    @block.tensor
                    def _(t, sl=sl):
                        P.w(t, "vS")
                        P.w(t, "sS", free_ps)
                        for j in range(16):
                            ins = t.transpose(ps_t[:, j * 128:(j + 1) * 128],
                                              xnb[:, sl * HID + j * 128: sl * HID + (j + 1) * 128], id_bf[:])
                        P.inc(ins, "tS", 1)
                        free_xn[sl] = P.c["tS"]
                    @block.scalar
                    def _(s, i=i):
                        P.w(s, "tS")
                        xv = xnT.rearrange("p (k t) -> p k t", t=T)
                        P.inc(s.copy(xv[:, :, i * 128:(i + 1) * 128],
                                     ps_t.rearrange("p (k t) -> p k t", t=128)), "sS", 1)
                    free_ps = P.c["sS"]
                if collect_xsum:
                    @block.tensor
                    def _(t):
                        P.w(t, "vS")
                        P.inc(t.transpose(ps_f[0:16, :], xsc[:], id_f[:]), "tS", 1)
                    @block.vector
                    def _(v):
                        P.w(v, "tS")
                        P.inc(v.tensor_copy(xrow[:], ps_f[0:16, :]), "vS", 1)
                    @block.scalar
                    def _(s):
                        P.w(s, "vS")
                        P.inc(s.dma_start(out=scrX, in_=xrow[:]), "d2", 16)
                        P.w(s, "d2")
                        P.inc(s.dma_start(out=xsr[:], in_=scrX.rearrange("i p -> (i p)")[None, :]), "d2", 16)
                    @block.vector
                    def _(v):
                        P.w(v, "d2")
                        P.inc(v.tensor_copy(xsrb[:], xsr[:]), "vS", 1)

        def emit_inprojT(is_ctx, xnT):
            with nc.sbuf_tensor("ipt_w", [128, 2 * 2048], BF16, allow_name_mangling=True) as wst, \
                 nc.sbuf_tensor("ipt_xr", [128, 2 * 2051], F32, allow_name_mangling=True) as xbcr, \
                 nc.sbuf_tensor("ipt_cv", [128, 2048], F32, allow_name_mangling=True) as cvt, \
                 nc.sbuf_tensor("ipt_cs", [128, 2 * 2048], BF16, allow_name_mangling=True) as convs, \
                 nc.sbuf_tensor("ipt_t1", [64, 2048], F32, allow_name_mangling=True) as tA, \
                 nc.sbuf_tensor("ipt_t2", [64, 2048], F32, allow_name_mangling=True) as tB, \
                 nc.psum_tensor("ipt_ps", [128, 2 * 2048], F32, allow_name_mangling=True) as ps_ip:
                barrier()
                stat_rdy = [0] * 35
                free_w = [0, 0]
                free_ps = [0, 0]
                free_cv = 0
                free_cs = [0, 0]
                mm_done = [0] * 35
                ev_done = [0] * 35
                xv = xnT.rearrange("p (k t) -> p k t", t=T)
                for j in range(36):
                    if j < 35:
                        cw = 128 if j < 34 else 64
                        @block.gpsimd
                        def _(g, j=j, cw=cw):
                            P.w(g, "tS", free_w[j % 2])
                            last = 0
                            for k in range(NKD):
                                last = P.inc(g.dma_start(
                                    out=wst[:, (j % 2) * 2048 + k * 128:(j % 2) * 2048 + k * 128 + cw],
                                    in_=wip[k * 128:(k + 1) * 128, INTER + j * 128: INTER + j * 128 + cw]), "g0" if j % 2 == 0 else "g1", 16)
                            stat_rdy[j] = last
                    if j == 0:
                        continue
                    jj = j - 1
                    cw = 128 if jj < 34 else 64
                    sl = jj % 2
                    @block.tensor
                    def _(t, jj=jj, cw=cw, sl=sl):
                        P.w(t, "g0" if jj % 2 == 0 else "g1", stat_rdy[jj])
                        P.w(t, "sS", free_ps[sl])
                        for k in range(NKD):
                            for n in range(4):
                                ins = t.matmul(ps_ip[0:cw, sl * 2048 + n * 512: sl * 2048 + (n + 1) * 512],
                                               wst[:, sl * 2048 + k * 128: sl * 2048 + k * 128 + cw],
                                               xv[:, k, n * 512:(n + 1) * 512],
                                               start=(k == 0), stop=(k == NKD - 1))
                        mm_done[jj] = P.inc(ins, "tS", 1)
                        free_w[sl] = P.c["tS"]
                    @block.scalar
                    def _(s, jj=jj, cw=cw, sl=sl):
                        P.w(s, "tS", mm_done[jj])
                        free_ps[sl] = P.inc(s.copy(xbcr[0:cw, sl * 2051 + 3: sl * 2051 + 2051],
                                                   ps_ip[0:cw, sl * 2048: sl * 2048 + 2048]), "sS", 1)
                        ev_done[jj] = P.c["sS"]
                    if jj < 34:
                        @block.vector
                        def _(v, jj=jj, sl=sl):
                            P.w(v, "sS", ev_done[jj])
                            P.inc(v.tensor_copy(xbcr[:, sl * 2051: sl * 2051 + 3], hist[:, jj * 3:(jj + 1) * 3]), "vS", 1)
                            P.w(v, "vS")
                            if is_ctx:
                                P.inc(v.tensor_copy(hist[:, jj * 3:(jj + 1) * 3],
                                              xbcr[:, sl * 2051 + 2048: sl * 2051 + 2051]), "vS", 1)
                                P.w(v, "vS")
                            P.w(v, "sS", free_cv)
                            P.inc(v.tensor_scalar(cvt[:], xbcr[:, sl * 2051: sl * 2051 + 2048],
                                            convw_s[:, jj * 4:jj * 4 + 1], None, ALU.mult), "vS", 1)
                            P.w(v, "vS")
                            for kk in range(1, 4):
                                P.inc(v.scalar_tensor_tensor(cvt[:], xbcr[:, sl * 2051 + kk: sl * 2051 + kk + 2048],
                                                       convw_s[:, jj * 4 + kk:jj * 4 + kk + 1], cvt[:],
                                                       ALU.mult, ALU.add), "vS", 1)
                                P.w(v, "vS")
                        @block.scalar
                        def _(s, jj=jj, sl=sl):
                            P.w(s, "vS")
                            if jj == 32:
                                dst = Bt[:]
                            elif jj == 33:
                                dst = Ctn[:]
                            else:
                                P.w(s, "w0" if sl == 0 else "w1", free_cs[sl])
                                dst = convs[:, sl * 2048:(sl + 1) * 2048]
                            free_cv = P.inc(s.activation(dst, cvt[:], AF.Silu,
                                                         bias=convb_c[:, jj:jj + 1]), "sS", 1)
                            if jj < 32:
                                P.w(s, "sS")
                                free_cs[sl] = P.inc(s.dma_start(
                                    out=x_sp[jj * 128:(jj + 1) * 128, :],
                                    in_=convs[:, sl * 2048:(sl + 1) * 2048]), "w0" if sl == 0 else "w1", 16)
                    else:
                        @block.vector
                        def _(v, sl=sl):
                            P.w(v, "sS", ev_done[34])
                            srcz = xbcr[0:64, sl * 2051 + 3: sl * 2051 + 2051]
                            P.inc(v.tensor_scalar(tB[:], srcz, dtb_c[:], None, ALU.add), "vS", 1)
                            P.w(v, "vS")
                            P.inc(v.tensor_scalar(tA[:], tB[:], -1.0, None, ALU.mult), "vS", 1)
                            P.w(v, "vS")
                            P.inc(v.tensor_tensor(tA[:], tA[:], tB[:], ALU.max), "vS", 1)
                            P.w(v, "vS")
                            P.inc(v.tensor_scalar(tB[:], tB[:], 0.0, None, ALU.max), "vS", 1)
                        @block.scalar
                        def _(s):
                            P.w(s, "vS")
                            P.inc(s.activation(tA[:], tA[:], AF.Exp, scale=-1.0), "sS", 1)
                            P.w(s, "sS")
                            P.inc(s.activation(tA[:], tA[:], AF.Ln, bias=1.0), "sS", 1)
                        @block.vector
                        def _(v):
                            P.w(v, "sS")
                            P.inc(v.tensor_tensor(dts[:], tA[:], tB[:], ALU.add), "vS", 1)
                            P.w(v, "vS")
                            P.inc(v.tensor_scalar(adts[:], dts[:], A_c[:], None, ALU.mult), "vS", 1)

        def emit_prepass(dtT_all, dsT_all, etb_all, sdo_all, negA_all, need_flats):
            with nc.sbuf_tensor("pp_aT", [128, 64], F32, allow_name_mangling=True) as adtT_s, \
                 nc.sbuf_tensor("pp_tc", [64, 1], F32, allow_name_mangling=True) as tc, \
                 nc.sbuf_tensor("pp_tr", [1, 64], F32, allow_name_mangling=True) as totr, \
                 nc.sbuf_tensor("pp_acf", [64, 128], F32, allow_name_mangling=True) as acf, \
                 nc.sbuf_tensor("pp_ach", [64, 128], BF16, allow_name_mangling=True) as ach, \
                 nc.sbuf_tensor("pp_acg", [64, 128], F32, allow_name_mangling=True) as achf, \
                 nc.sbuf_tensor("pp_acl", [64, 128], BF16, allow_name_mangling=True) as acl, \
                 nc.psum_tensor("pp_ps", [128, 512], F32, allow_name_mangling=True) as pp:
                barrier()
                for c in range(NCH):
                    @block.tensor
                    def _(t, c=c):
                        P.w(t, "vS"); P.w(t, "sS")
                        t.transpose(pp[:, 0:64], dts[0:64, c * 128:(c + 1) * 128], id_f[0:64, 0:64])
                        P.inc(t.transpose(pp[:, 64:128], adts[0:64, c * 128:(c + 1) * 128],
                                          id_f[0:64, 0:64]), "tS", 1)
                    @block.vector
                    def _(v, c=c):
                        P.w(v, "tS")
                        v.tensor_copy(dtT_all[:, c * 64:(c + 1) * 64], pp[:, 0:64])
                        v.tensor_copy(adtT_s[:], pp[:, 64:128])
                        P.inc(v.reduce_sum(tc[:], adts[0:64, c * 128:(c + 1) * 128], axis=AX), "vS", 1)
                    @block.tensor
                    def _(t):
                        P.w(t, "vS")
                        P.inc(t.matmul(pp[0:1, 128:192], tc[:], id_f[0:64, 0:64], start=True, stop=True), "tS", 1)
                    @block.vector
                    def _(v):
                        P.w(v, "tS")
                        P.inc(v.tensor_copy(totr[:], pp[0:1, 128:192]), "vS", 1)
                    @block.tensor
                    def _(t):
                        P.w(t, "vS")
                        t.matmul(pp[:, 192:256], triN[:], adtT_s[:], start=True, stop=False)
                        t.matmul(pp[:, 192:256], ones_r[:], totr[:], start=False, stop=True)
                        t.matmul(pp[:, 256:320], ones_r[:], totr[:], start=True, stop=True)
                        t.matmul(pp[:, 320:384], triN[:], adtT_s[:], start=True, stop=True)
                        P.inc(t.matmul(pp[0:64, 384:512], adtT_s[:], triP[:], start=True, stop=True), "tS", 1)
                    @block.scalar
                    def _(s, c=c):
                        P.w(s, "tS")
                        s.activation(dsT_all[:, c * 64:(c + 1) * 64], pp[:, 192:256], AF.Exp)
                        P.inc(s.activation(etb_all[:, c * 64:(c + 1) * 64], pp[:, 256:320], AF.Exp), "sS", 1)
                    @block.vector
                    def _(v, c=c):
                        P.w(v, "tS")
                        v.tensor_copy(negA_all[:, c * 64:(c + 1) * 64], pp[:, 320:384])
                        P.inc(v.tensor_copy(acf[:], pp[0:64, 384:512]), "vS", 1)
                    @block.scalar
                    def _(s, c=c):
                        P.w(s, "vS")
                        P.inc(s.activation(sdo_all[:, c * 64:(c + 1) * 64],
                                           negA_all[:, c * 64:(c + 1) * 64], AF.Exp, scale=-1.0), "sS", 1)
                    if need_flats:
                        @block.vector
                        def _(v):
                            P.w(v, "vS")
                            P.w(v, "w0")
                            P.inc(v.tensor_copy(ach[:], acf[:]), "vS", 1)
                        @block.scalar
                        def _(s):
                            P.w(s, "vS")
                            P.inc(s.copy(achf[:], ach[:]), "sS", 1)
                        @block.vector
                        def _(v):
                            P.w(v, "sS")
                            P.inc(v.tensor_tensor(acl[:], acf[:], achf[:], ALU.subtract), "vS", 1)
                        @block.scalar
                        def _(s, c=c):
                            P.w(s, "vS")
                            P.inc(s.dma_start(out=scrH[c:c + 1, :].rearrange("a (h l) -> (a h) l", l=128),
                                              in_=ach[:]), "w0", 16)
                            P.inc(s.dma_start(out=scrL[c:c + 1, :].rearrange("a (h l) -> (a h) l", l=128),
                                              in_=acl[:]), "w0", 16)

        def emit_ssd(full, dtT_all, dsT_all, etb_all, sdo_all, negA_all):
            with ExitStack() as ss:
                sbl = lambda name, shape, dt: ss.enter_context(nc.sbuf_tensor(name, shape, dt, allow_name_mangling=True))
                psl = lambda name, shape, dt: ss.enter_context(nc.psum_tensor(name, shape, dt, allow_name_mangling=True))
                xch = sbl("sd_xch", [128, 2 * INTER], BF16)
                xchT = sbl("sd_xchT", [128, INTER], BF16)
                xd = sbl("sd_xd", [128, INTER], BF16)
                Bsb = sbl("sd_Bsb", [128, 128], BF16)
                t512 = sbl("sd_t5", [128, 512], F32)
                pS = psl("sd_pS", [128, 512], F32)
                pxt = psl("sd_pxt", [128, 1024], BF16)
                barrier()
                if full:
                    xrd = sbl("sd_xrd", [128, INTER], BF16)
                    cb_b = sbl("sd_cb", [128, 128], BF16)
                    Eb = sbl("sd_E", [128, 1024], BF16)
                    Mb = sbl("sd_M", [128, 1024], BF16)
                    flhl = sbl("sd_f", [1, 2 * 2048], BF16)
                    ybuf = sbl("sd_y", [128, INTER], F32)
                    sgc = sbl("sd_sg", [128, INTER], BF16)
                    sqf = sbl("sd_sq", [128, INTER], F32)
                    ygb = sbl("sd_yg", [128, INTER], BF16)
                    ygT = sbl("sd_ygT", [128, INTER], BF16)
                    ssm = sbl("sd_sm", [128, 8], F32)
                    Rbf = sbl("sd_Rb", [128, INTER], BF16)
                    psE = psl("sd_pE", [128, 2048], F32)
                    pyd = psl("sd_pyd", [128, 512], F32)
                    pyo = psl("sd_pyo", [128, 512], F32)
                xch_rdy = [0] * NCH
                sg_rdy = [0] * NCH
                xch_free = [0, 0]
                sg_free = 0
                pass
                NG = NCH * 8
                flat_rdy = [0] * (NG + 2)
                flat_free = [0, 0]
                rbf_rdy = [0]
                e_free = [0, 0]
                if full:
                    @block.scalar
                    def _(s):
                        P.w(s, "vS")
                        rbf_rdy[0] = P.inc(s.copy(Rbf[:], R[:]), "sS", 1)
                    @block.sync
                    def _(sync):
                        pass
                        for G in range(2):
                            par = G % 2
                            sem = "d3" if par == 0 else "d4"
                            cG, gG = G // 8, G % 8
                            P.inc(sync.dma_start(out=flhl[0:1, par * 2048: par * 2048 + 1024],
                                           in_=scrH[cG:cG + 1, gG * 1024:(gG + 1) * 1024]), sem, 16)
                            flat_rdy[G] = P.inc(sync.dma_start(
                                out=flhl[0:1, par * 2048 + 1024: par * 2048 + 2048],
                                in_=scrL[cG:cG + 1, gG * 1024:(gG + 1) * 1024]), sem, 16)
                for c in range(NCH + 1):
                    if c < NCH:
                        sl = c % 2
                        @block.sync
                        def _(sync, c=c, sl=sl):
                            if c == 0:
                                pass
                            P.w(sync, "vS", xch_free[sl])
                            xvv = x_sp.rearrange("(i p) j -> p i j", p=128)
                            last = 0
                            for q in range(4):
                                last = P.inc(sync.dma_start(
                                    out=xch.rearrange("p (s i j) -> p s i j", s=2, j=128)[:, sl, q * 8:(q + 1) * 8, :],
                                    in_=xvv[:, q * 8:(q + 1) * 8, c * 128:(c + 1) * 128]), "d0" if sl == 0 else "d1", 16)
                            xch_rdy[c] = last
                    if c == 0:
                        continue
                    cc = c - 1
                    if full:
                        @block.sync
                        def _(sync, cc=cc):
                            P.w(sync, "vS", sg_free)
                            sg_rdy[cc] = P.inc(sync.dma_start(out=sgc[:],
                                                              in_=sg_sp[cc * 128:(cc + 1) * 128, :]), "d2", 16)
                    sl = cc % 2
                    skip_state = full and (cc == NCH - 1)
                    for q in range(4):
                        @block.tensor
                        def _(t, cc=cc, sl=sl, q=q):
                            P.w(t, "d0" if sl == 0 else "d1", xch_rdy[cc])
                            P.w(t, "vS")
                            for j8 in range(8):
                                ins = t.transpose(pxt[:, j8 * 128:(j8 + 1) * 128],
                                                  xch[:, sl * INTER + (q * 8 + j8) * 128:
                                                      sl * INTER + (q * 8 + j8 + 1) * 128], id_bf[:])
                            P.inc(ins, "tS", 1)
                        @block.vector
                        def _(v, q=q):
                            P.w(v, "tS")
                            P.inc(v.tensor_copy(xchT[:, q * 1024:(q + 1) * 1024], pxt[:]), "vS", 1)
                    @block.vector
                    def _(v, cc=cc, sl=sl, skip_state=skip_state):
                        P.w(v, "vS")
                        x3 = xchT.rearrange("p (h q) -> p h q", q=64)
                        P.inc(v.tensor_tensor(xd.rearrange("p (h q) -> p h q", q=64), x3,
                                        dtT_all[:, cc * 64:(cc + 1) * 64][:, :, None].broadcast_to([128, 64, 64]),
                                        ALU.mult), "vS", 1)
                        P.w(v, "vS")
                        if full:
                            P.inc(v.tensor_tensor(xrd.rearrange("p (h q) -> p h q", q=64), x3,
                                            D_b[:, :, None].broadcast_to([128, 64, 64]), ALU.mult), "vS", 1)
                            P.w(v, "vS")
                        if not skip_state:
                            P.inc(v.tensor_tensor(
                                xch.rearrange("p (s f) -> p s f", s=2)[:, sl, :].rearrange("p (h q) -> p h q", q=64),
                                xd.rearrange("p (h q) -> p h q", q=64),
                                dsT_all[:, cc * 64:(cc + 1) * 64][:, :, None].broadcast_to([128, 64, 64]),
                                ALU.mult), "vS", 1)
                    xdd_ap = xch[:, sl * INTER:(sl + 1) * INTER]
                    @block.tensor
                    def _(t, cc=cc):
                        P.w(t, "vS")
                        P.inc(t.transpose(pxt[:, 0:128], Bt[:, cc * 128:(cc + 1) * 128], id_bf[:]), "tS", 1)
                    @block.vector
                    def _(v):
                        P.w(v, "tS")
                        P.inc(v.tensor_copy(Bsb[:], pxt[:, 0:128]), "vS", 1)
                    if full:
                        @block.tensor
                        def _(t, cc=cc):
                            P.w(t, "vS"); P.w(t, "sS")
                            P.inc(t.matmul(psE[:, 1024:1152], Bt[:, cc * 128:(cc + 1) * 128],
                                           Ctn[:, cc * 128:(cc + 1) * 128], start=True, stop=True), "tS", 1)
                        @block.scalar
                        def _(s):
                            P.w(s, "tS")
                            P.inc(s.copy(cb_b[:], psE[:, 1024:1152]), "sS", 1)
                        e_free[0] = e_free[1] = P.c["sS"]
                    for g in range(8):
                        G = cc * 8 + g
                        par = g % 2
                        if full:
                            @block.tensor
                            def _(t, cc=cc, sl=sl, g=g, par=par, G=G):
                                P.w(t, "d3" if par == 0 else "d4", flat_rdy[G])
                                P.w(t, "sS", e_free[par])
                                for i in range(8):
                                    h = g * 8 + i
                                    dst = psE[:, par * 1024 + i * 128: par * 1024 + (i + 1) * 128]
                                    t.matmul(dst, id_bf[:], maskT[:], start=(i % 4 == 0), stop=False)
                                    t.matmul(dst, ones_rb[:],
                                             flhl[0:1, par * 2048 + i * 128: par * 2048 + (i + 1) * 128],
                                             start=False, stop=False)
                                    ins = t.matmul(dst, ones_rb[:],
                                                   flhl[0:1, par * 2048 + 1024 + i * 128: par * 2048 + 1024 + (i + 1) * 128],
                                                   start=False, stop=(i % 4 == 3))
                                flat_free[par] = P.inc(ins, "tS", 1)
                            if G + 2 < NG:
                                @block.sync
                                def _(sync, G=G, par=par):
                                    P.w(sync, "tS", flat_free[par])
                                    G2 = G + 2
                                    cG, gG = G2 // 8, G2 % 8
                                    P.inc(sync.dma_start(out=flhl[0:1, par * 2048: par * 2048 + 1024],
                                                   in_=scrH[cG:cG + 1, gG * 1024:(gG + 1) * 1024]), "d3" if par == 0 else "d4", 16)
                                    flat_rdy[G2] = P.inc(sync.dma_start(
                                        out=flhl[0:1, par * 2048 + 1024: par * 2048 + 2048],
                                        in_=scrL[cG:cG + 1, gG * 1024:(gG + 1) * 1024]), "d3" if par == 0 else "d4", 16)
                            @block.scalar
                            def _(s, cc=cc, g=g, par=par):
                                P.w(s, "tS")
                                P.w(s, "vS")
                                for i in range(8):
                                    h = g * 8 + i
                                    ins = s.activation(Eb[:, i * 128:(i + 1) * 128],
                                                       psE[:, par * 1024 + i * 128: par * 1024 + (i + 1) * 128],
                                                       AF.Exp, bias=negA_all[:, cc * 64 + h: cc * 64 + h + 1])
                                e_free[par] = P.inc(ins, "sS", 1)
                            @block.vector
                            def _(v):
                                P.w(v, "sS")
                                P.w(v, "tS")
                                P.inc(v.tensor_tensor(Mb.rearrange("p (a f) -> p a f", f=128),
                                                      Eb.rearrange("p (a f) -> p a f", f=128),
                                                      cb_b[:, None, :].broadcast_to([128, 8, 128]),
                                                      ALU.mult), "vS", 1)
                        @block.tensor
                        def _(t, cc=cc, g=g, skip_state=skip_state):
                            P.w(t, "vS")
                            if full:
                                P.w(t, "sS", rbf_rdy[0])
                            ins = None
                            for i in range(8):
                                h = g * 8 + i
                                if full:
                                    t.matmul(pyd[:, i * 64:(i + 1) * 64], Mb[:, i * 128:(i + 1) * 128],
                                             xd[:, h * 64:(h + 1) * 64], start=(i == 0), stop=False)
                                    t.matmul(pyd[:, i * 64:(i + 1) * 64], id_bf[:],
                                             xrd[:, h * 64:(h + 1) * 64], start=False, stop=(i == 7))
                                    ins = t.matmul(pyo[:, i * 64:(i + 1) * 64], Ctn[:, cc * 128:(cc + 1) * 128],
                                                   Rbf[:, h * 64:(h + 1) * 64], start=(i == 0), stop=(i == 7))
                                if not skip_state:
                                    ins = t.matmul(pS[:, i * 64:(i + 1) * 64], Bsb[:],
                                                   xdd_ap[:, h * 64:(h + 1) * 64], start=(i == 0), stop=(i == 7))
                            P.inc(ins, "tS", 1)
                        @block.vector
                        def _(v, cc=cc, g=g, skip_state=skip_state):
                            P.w(v, "tS")
                            if full:
                                P.inc(v.tensor_tensor(t512.rearrange("p (h q) -> p h q", q=64),
                                                pyo.rearrange("p (h q) -> p h q", q=64),
                                                sdo_all[:, cc * 64 + g * 8: cc * 64 + (g + 1) * 8][:, :, None].broadcast_to([128, 8, 64]),
                                                ALU.mult), "vS", 1)
                                P.w(v, "vS")
                                P.inc(v.tensor_tensor(ybuf[:, g * 512:(g + 1) * 512], t512[:], pyd[:], ALU.add), "vS", 1)
                                P.w(v, "vS")
                            if not skip_state:
                                P.inc(v.tensor_tensor(t512.rearrange("p (h q) -> p h q", q=64),
                                                R[:, g * 512:(g + 1) * 512].rearrange("p (h q) -> p h q", q=64),
                                                etb_all[:, cc * 64 + g * 8: cc * 64 + (g + 1) * 8][:, :, None].broadcast_to([128, 8, 64]),
                                                ALU.mult), "vS", 1)
                                P.w(v, "vS")
                                P.inc(v.tensor_tensor(R[:, g * 512:(g + 1) * 512], t512[:], pS[:], ALU.add), "vS", 1)
                    xch_free[sl] = P.c["vS"]
                    if not full:
                        continue
                    if not skip_state:
                        @block.scalar
                        def _(s):
                            P.w(s, "vS")
                            rbf_rdy[0] = P.inc(s.copy(Rbf[:], R[:]), "sS", 1)
                    @block.vector
                    def _(v):
                        P.w(v, "vS")
                        P.inc(v.tensor_tensor(sqf[:], ybuf[:], ybuf[:], ALU.mult), "vS", 1)
                        P.w(v, "vS")
                        P.inc(v.reduce_sum(ssm[:, 0:1], sqf[:], axis=AX), "vS", 1)
                        P.w(v, "vS")
                        P.inc(v.tensor_scalar(ssm[:, 1:2], ssm[:, 0:1], 1.0 / INTER, EPS,
                                              ALU.mult, ALU.add), "vS", 1)
                    @block.scalar
                    def _(s):
                        P.w(s, "vS")
                        P.inc(s.activation(ssm[:, 2:3], ssm[:, 1:2], AF.Ln), "sS", 1)
                        P.w(s, "sS")
                        P.inc(s.activation(ssm[:, 3:4], ssm[:, 2:3], AF.Exp, scale=-0.5), "sS", 1)
                    @block.vector
                    def _(v, cc=cc):
                        P.w(v, "sS")
                        P.w(v, "d2", sg_rdy[cc])
                        P.inc(v.tensor_scalar(sqf[:], ybuf[:], ssm[:, 3:4], None, ALU.mult), "vS", 1)
                        P.w(v, "vS")
                        P.inc(v.tensor_tensor(ygb[:], sqf[:], sgc[:], ALU.mult), "vS", 1)
                    sg_free = P.c["vS"]
                    for q in range(4):
                        @block.tensor
                        def _(t, q=q):
                            P.w(t, "vS")
                            for j8 in range(8):
                                ins = t.transpose(pxt[:, j8 * 128:(j8 + 1) * 128],
                                                  ygb[:, (q * 8 + j8) * 128:(q * 8 + j8 + 1) * 128], id_bf[:])
                            P.inc(ins, "tS", 1)
                        @block.vector
                        def _(v, q=q):
                            P.w(v, "tS")
                            P.w(v, "w0")
                            P.inc(v.tensor_copy(ygT[:, q * 1024:(q + 1) * 1024], pxt[:]), "vS", 1)
                    @block.scalar
                    def _(s, cc=cc):
                        P.w(s, "vS")
                        yvv = ygt_sp.rearrange("(i p) j -> p i j", p=128)
                        for q in range(4):
                            P.inc(s.dma_start(out=yvv[:, q * 8:(q + 1) * 8, cc * 128:(cc + 1) * 128],
                                              in_=ygT.rearrange("p (i j) -> p i j", j=128)[:, q * 8:(q + 1) * 8, :]),
                                  "w0", 16)
                if not full:
                    @block.vector
                    def _(v):
                        P.w(v, "vS")
                        P.inc(v.tensor_scalar(R[:], R[:], rmask_c[:], None, ALU.mult), "vS", 1)

        def emit_p2(xnT):
            with nc.sbuf_tensor("p2_w", [128, NKD * 1024], BF16, allow_name_mangling=True) as wq, \
                 nc.sbuf_tensor("p2_t", [128, 1024], F32, allow_name_mangling=True) as tmpf, \
                 nc.sbuf_tensor("p2_s", [128, 2 * 1024], BF16, allow_name_mangling=True) as sgt, \
                 nc.psum_tensor("p2_ps", [128, 2 * 1024], F32, allow_name_mangling=True) as ps2:
                barrier()
                free_w = 0
                w_rdy = [0] * 4
                free_sg = [0, 0]
                free_ps = [0, 0]
                free_tmp = 0
                xv = xnT.rearrange("p (k t) -> p k t", t=T)
                wv2 = wip.rearrange("(k p) c -> p k c", p=128)
                for q in range(4):
                    @block.gpsimd
                    def _(g, q=q):
                        P.w(g, "tS", free_w)
                        last = 0
                        for piece in range(4):
                            last = P.inc(g.dma_start(
                                out=wq.rearrange("p (k c) -> p k c", c=1024)[:, piece * 4:(piece + 1) * 4, :],
                                in_=wv2[:, piece * 4:(piece + 1) * 4, q * 1024:(q + 1) * 1024]), "g0", 16)
                        w_rdy[q] = last
                    for t_i in range(NTT):
                        sl = t_i % 2
                        @block.tensor
                        def _(t, q=q, t_i=t_i, sl=sl):
                            P.w(t, "g0", w_rdy[q])
                            P.w(t, "sS", free_ps[sl])
                            for k in range(NKD):
                                for n in range(2):
                                    ins = t.matmul(ps2[:, sl * 1024 + n * 512: sl * 1024 + (n + 1) * 512],
                                                   xv[:, k, t_i * 128:(t_i + 1) * 128],
                                                   wq.rearrange("p (k c) -> p k c", c=1024)[:, k, n * 512:(n + 1) * 512],
                                                   start=(k == 0), stop=(k == NKD - 1))
                            P.inc(ins, "tS", 1)
                        if t_i == NTT - 1:
                            free_w = P.c["tS"]
                        @block.scalar
                        def _(s, sl=sl):
                            P.w(s, "tS")
                            P.w(s, "vS", free_tmp)
                            free_ps[sl] = P.inc(s.activation(tmpf[:], ps2[:, sl * 1024:(sl + 1) * 1024],
                                                             AF.Silu), "sS", 1)
                        @block.vector
                        def _(v, q=q, sl=sl):
                            P.w(v, "sS")
                            P.w(v, "w0" if sl == 0 else "w1", free_sg[sl])
                            P.inc(v.tensor_tensor(sgt[:, sl * 1024:(sl + 1) * 1024], tmpf[:],
                                                  NWb[:, q * 1024:(q + 1) * 1024], ALU.mult), "vS", 1)
                        free_tmp = P.c["vS"]
                        @block.scalar
                        def _(s, q=q, t_i=t_i, sl=sl):
                            P.w(s, "vS")
                            free_sg[sl] = P.inc(s.dma_start(
                                out=sg_sp[t_i * 128:(t_i + 1) * 128, q * 1024:(q + 1) * 1024],
                                in_=sgt[:, sl * 1024:(sl + 1) * 1024]), "w0" if sl == 0 else "w1", 16)

        def emit_p4(xnT, xsrb):
            with nc.sbuf_tensor("p4_wg", [128, 2 * 2048], BF16, allow_name_mangling=True) as wg, \
                 nc.sbuf_tensor("p4_wu", [128, 2 * 2048], BF16, allow_name_mangling=True) as wu, \
                 nc.sbuf_tensor("p4_g", [128, HID], F32, allow_name_mangling=True) as gsf, \
                 nc.sbuf_tensor("p4_g2", [128, HID], F32, allow_name_mangling=True) as gs2, \
                 nc.sbuf_tensor("p4_gu", [128, 2 * HID], BF16, allow_name_mangling=True) as gub, \
                 nc.psum_tensor("p4_ps", [128, 2 * 2048], F32, allow_name_mangling=True) as ps4:
                barrier()
                w_rdy = [0] * NHT
                free_w = [0, 0]
                free_gu = [0, 0]
                mmg = [0] * NHT
                mmu = [0] * NHT
                xv = xnT.rearrange("p (k t) -> p k t", t=T)
                for ht in range(NHT + 1):
                    if ht < NHT:
                        @block.gpsimd
                        def _(g, ht=ht):
                            P.w(g, "tS", free_w[ht % 2])
                            last = 0
                            for k in range(NKD):
                                last = P.inc(g.dma_start(
                                    out=wg[:, (ht % 2) * 2048 + k * 128:(ht % 2) * 2048 + (k + 1) * 128],
                                    in_=wgt[k * 128:(k + 1) * 128, ht * 128:(ht + 1) * 128]), "g0" if ht % 2 == 0 else "g1", 16)
                                last = P.inc(g.dma_start(
                                    out=wu[:, (ht % 2) * 2048 + k * 128:(ht % 2) * 2048 + (k + 1) * 128],
                                    in_=wut[k * 128:(k + 1) * 128, ht * 128:(ht + 1) * 128]), "g0" if ht % 2 == 0 else "g1", 16)
                            w_rdy[ht] = last
                    if ht == 0:
                        continue
                    hh = ht - 1
                    sl = hh % 2
                    @block.tensor
                    def _(t, hh=hh, sl=sl):
                        P.w(t, "g0" if hh % 2 == 0 else "g1", w_rdy[hh])
                        P.w(t, "vS")
                        for (wbuf, half) in [(wg, 0), (wu, 1)]:
                            for k in range(NKD):
                                for n in range(4):
                                    t.matmul(ps4[:, half * 2048 + n * 512: half * 2048 + (n + 1) * 512],
                                             wbuf[:, sl * 2048 + k * 128: sl * 2048 + (k + 1) * 128],
                                             xv[:, k, n * 512:(n + 1) * 512],
                                             start=(k == 0), stop=False)
                            for n in range(4):
                                ins = t.matmul(ps4[:, half * 2048 + n * 512: half * 2048 + (n + 1) * 512],
                                               n128r[:], xsrb[0:1, n * 512:(n + 1) * 512],
                                               start=False, stop=True)
                            if half == 0:
                                mmg[hh] = P.inc(ins, "tS", 1)
                            else:
                                mmu[hh] = P.inc(ins, "tS", 1)
                        free_w[sl] = P.c["tS"]
                    @block.vector
                    def _(v, hh=hh):
                        P.w(v, "tS", mmg[hh])
                        P.inc(v.tensor_scalar(gsf[:], ps4[:, 0:2048], sg_c[:, hh:hh + 1], None, ALU.mult), "vS", 1)
                    @block.scalar
                    def _(s):
                        P.w(s, "vS")
                        P.inc(s.activation(gs2[:], gsf[:], AF.Silu), "sS", 1)
                    @block.vector
                    def _(v, hh=hh, sl=sl):
                        P.w(v, "tS", mmu[hh])
                        P.w(v, "sS")
                        P.inc(v.tensor_scalar(gsf[:], ps4[:, 2048:4096], su_c[:, hh:hh + 1], None, ALU.mult), "vS", 1)
                        P.w(v, "vS")
                        P.w(v, "w0" if sl == 0 else "w1", free_gu[sl])
                        P.inc(v.tensor_tensor(gub[:, sl * HID:(sl + 1) * HID], gs2[:], gsf[:], ALU.mult), "vS", 1)
                    @block.scalar
                    def _(s, hh=hh, sl=sl):
                        P.w(s, "vS")
                        free_gu[sl] = P.inc(s.dma_start(out=gu_sp[hh * 128:(hh + 1) * 128, :],
                                                        in_=gub[:, sl * HID:(sl + 1) * HID]), "w0" if sl == 0 else "w1", 16)

        def emit_p5():
            KH = 11
            with nc.sbuf_tensor("p5_wd", [128, KH * 2048], BF16, allow_name_mangling=True) as wdb, \
                 nc.sbuf_tensor("p5_gs", [128, 2 * KH * 128], BF16, allow_name_mangling=True) as gst, \
                 nc.sbuf_tensor("p5_pl", [128, 2 * HID], F32, allow_name_mangling=True) as pld, \
                 nc.sbuf_tensor("p5_tf", [128, HID], F32, allow_name_mangling=True) as tf5, \
                 nc.sbuf_tensor("p5_gc", [128, 20], F32, allow_name_mangling=True) as gcol, \
                 nc.sbuf_tensor("p5_pd", [128, 2 * HID], F32, allow_name_mangling=True) as pdst, \
                 nc.sbuf_tensor("p5_gb", [128, 2 * HID], BF16, allow_name_mangling=True) as gb5, \
                 nc.sbuf_tensor("p5_sd", [128, HID], F32, allow_name_mangling=True) as SDb, \
                 nc.sbuf_tensor("p5_r", [1, HID], F32, allow_name_mangling=True) as r_sd5, \
                 nc.psum_tensor("p5_ps", [128, 3 * 1024], F32, allow_name_mangling=True) as ps5, \
                 nc.psum_tensor("p5_pg", [128, 512], F32, allow_name_mangling=True) as psg5:
                barrier()
                pass
                @block.sync
                def _(sync):
                    P.inc(sync.dma_start(out=r_sd5[:], in_=sdr), "d0", 16)
                sd_rdy = P.c["d0"]
                for jj in range(HID // 512):
                    @block.tensor
                    def _(t, jj=jj):
                        P.w(t, "d0", sd_rdy)
                        P.w(t, "vS")
                        P.inc(t.matmul(psg5[:, 0:512], ones_r[:],
                                       r_sd5[0:1, jj * 512:(jj + 1) * 512], start=True, stop=True), "tS", 1)
                    @block.vector
                    def _(v, jj=jj):
                        P.w(v, "tS")
                        P.inc(v.tensor_copy(SDb[:, jj * 512:(jj + 1) * 512], psg5[:, 0:512]), "vS", 1)
                wv5 = wdt.rearrange("(k p) c -> p k c", p=128)
                free_gst = [0, 0]
                free_pl = [0, 0]
                free_pd = [0, 0]
                free_gb = [0, 0]
                ev_ps = [("vS", 0), ("vS", 0), ("vS", 0)]
                for pz in range(4):
                    w_rdy = [0]
                    @block.gpsimd
                    def _(g, pz=pz):
                        P.w(g, "tS")
                        for piece in range(3):
                            k0 = piece * 4
                            kn = min(4, KH - k0)
                            w_rdy[0] = P.inc(g.dma_start(
                                out=wdb.rearrange("p (k c) -> p k c", c=2048)[:, k0:k0 + kn, :],
                                in_=wv5[:, pz * KH + k0: pz * KH + k0 + kn, :]), "g0", 16)
                    if pz > 0:
                        @block.sync
                        def _(sync):
                            P.w(sync, "w0"); P.w(sync, "w1")
                    gst_rdy = [0] * NTT
                    pl_rdy = [0] * NTT
                    for t_o in range(NTT + 1):
                        if t_o < NTT:
                            t_i, sl = t_o, t_o % 2
                            @block.sync
                            def _(sync, pz=pz, t_i=t_i, sl=sl):
                                P.w(sync, "tS", free_gst[sl])
                                last = 0
                                for k in range(KH):
                                    last = P.inc(sync.dma_start(
                                        out=gst[:, sl * (KH * 128) + k * 128: sl * (KH * 128) + (k + 1) * 128],
                                        in_=gu_sp[(pz * KH + k) * 128:(pz * KH + k + 1) * 128,
                                                  t_i * 128:(t_i + 1) * 128]), "d1" if sl == 0 else "d2", 16)
                                gst_rdy[t_i] = last
                                if pz > 0:
                                    P.w(sync, "vS", free_pl[sl])
                                    pl_rdy[t_i] = P.inc(sync.dma_start(
                                        out=pld[:, sl * HID:(sl + 1) * HID],
                                        in_=pp_sp[t_i * 128:(t_i + 1) * 128, :]), "d3" if sl == 0 else "d4", 16)
                        if t_o == 0:
                            continue
                        t_i = t_o - 1
                        sl = t_i % 2
                        for dh in range(2):
                            r5 = (2 * t_i + dh) % 3
                            @block.tensor
                            def _(t, pz=pz, t_i=t_i, sl=sl, dh=dh, r5=r5):
                                P.w(t, "g0", w_rdy[0])
                                P.w(t, "d1" if sl == 0 else "d2", gst_rdy[t_i])
                                P.w(t, ev_ps[r5][0], ev_ps[r5][1])
                                for k in range(KH):
                                    for n in range(2):
                                        ins = t.matmul(
                                            ps5[:, r5 * 1024 + n * 512: r5 * 1024 + (n + 1) * 512],
                                            gst[:, sl * (KH * 128) + k * 128: sl * (KH * 128) + (k + 1) * 128],
                                            wdb.rearrange("p (k c) -> p k c", c=2048)[:, k, dh * 1024 + n * 512: dh * 1024 + (n + 1) * 512],
                                            start=(k == 0), stop=(k == KH - 1))
                                P.inc(ins, "tS", 1)
                            if pz == 0:
                                @block.scalar
                                def _(s, t_i=t_i, sl=sl, dh=dh, r5=r5):
                                    P.w(s, "tS")
                                    if dh == 0:
                                        P.w(s, "w0" if sl == 0 else "w1", free_pd[sl])
                                    c = P.inc(s.copy(pdst[:, sl * HID + dh * 1024: sl * HID + (dh + 1) * 1024],
                                                     ps5[:, r5 * 1024: r5 * 1024 + 1024]), "sS", 1)
                                    ev_ps[r5] = ("sS", c)
                            elif pz < 3:
                                @block.vector
                                def _(v, t_i=t_i, sl=sl, dh=dh, r5=r5):
                                    P.w(v, "tS")
                                    P.w(v, "d3" if sl == 0 else "d4", pl_rdy[t_i])
                                    if dh == 0:
                                        P.w(v, "w0" if sl == 0 else "w1", free_pd[sl])
                                    c = P.inc(v.tensor_tensor(
                                        pdst[:, sl * HID + dh * 1024: sl * HID + (dh + 1) * 1024],
                                        ps5[:, r5 * 1024: r5 * 1024 + 1024],
                                        pld[:, sl * HID + dh * 1024: sl * HID + (dh + 1) * 1024],
                                        ALU.add), "vS", 1)
                                    ev_ps[r5] = ("vS", c)
                                    if dh == 1:
                                        free_pl[sl] = c
                            else:
                                @block.vector
                                def _(v, t_i=t_i, sl=sl, dh=dh, r5=r5):
                                    P.w(v, "tS")
                                    P.w(v, "d3" if sl == 0 else "d4", pl_rdy[t_i])
                                    P.w(v, "sS")
                                    c = P.inc(v.tensor_tensor(
                                        tf5[:, dh * 1024:(dh + 1) * 1024],
                                        ps5[:, r5 * 1024: r5 * 1024 + 1024],
                                        pld[:, sl * HID + dh * 1024: sl * HID + (dh + 1) * 1024],
                                        ALU.add), "vS", 1)
                                    ev_ps[r5] = ("vS", c)
                                    if dh == 1:
                                        free_pl[sl] = c
                        @block.tensor
                        def _(t, pz=pz, sl=sl):
                            P.w(t, "vS")
                            for k in range(KH):
                                ins = t.matmul(psg5[:, 0:1],
                                               gst[:, sl * (KH * 128) + k * 128: sl * (KH * 128) + (k + 1) * 128],
                                               ones_cb[:], start=(k == 0), stop=(k == KH - 1))
                            P.inc(ins, "tS", 1)
                            free_gst[sl] = P.c["tS"]
                        if pz == 0:
                            @block.vector
                            def _(v, t_i=t_i):
                                P.w(v, "tS")
                                P.inc(v.tensor_copy(gcol[:, t_i:t_i + 1], psg5[:, 0:1]), "vS", 1)
                        else:
                            @block.vector
                            def _(v, t_i=t_i):
                                P.w(v, "tS")
                                P.inc(v.tensor_tensor(gcol[:, t_i:t_i + 1], psg5[:, 0:1],
                                                      gcol[:, t_i:t_i + 1], ALU.add), "vS", 1)
                        if pz < 3:
                            @block.scalar
                            def _(s, pz=pz, t_i=t_i, sl=sl):
                                r5b = (2 * t_i + 1) % 3
                                P.w(s, ev_ps[r5b][0], ev_ps[r5b][1])
                                free_pd[sl] = P.inc(s.dma_start(out=pp_sp[t_i * 128:(t_i + 1) * 128, :],
                                                                in_=pdst[:, sl * HID:(sl + 1) * HID]), "w0" if sl == 0 else "w1", 16)
                        else:
                            @block.vector
                            def _(v, t_i=t_i, sl=sl):
                                P.w(v, "vS")
                                P.inc(v.tensor_scalar(gcol[:, 17:18], gcol[:, t_i:t_i + 1], 128.0, None, ALU.mult), "vS", 1)
                                P.w(v, "vS")
                                P.inc(v.tensor_scalar(tf5[:], tf5[:], gcol[:, 17:18], None, ALU.subtract), "vS", 1)
                                P.w(v, "vS")
                                P.inc(v.tensor_tensor(tf5[:], tf5[:], SDb[:], ALU.mult), "vS", 1)
                            @block.scalar
                            def _(s, t_i=t_i, sl=sl):
                                P.w(s, "vS")
                                P.w(s, "w2" if sl == 0 else "w3", free_gb[sl])
                                P.inc(s.activation(gb5[:, sl * HID:(sl + 1) * HID], tf5[:], AF.Silu), "sS", 1)
                                P.w(s, "sS")
                                free_gb[sl] = P.inc(s.dma_start(out=g_sp[t_i * 128:(t_i + 1) * 128, :],
                                                                in_=gb5[:, sl * HID:(sl + 1) * HID]), "w2" if sl == 0 else "w3", 16)

        def emit_p7():
            with nc.sbuf_tensor("p7_w", [128, NKD * 2048], BF16, allow_name_mangling=True) as woh, \
                 nc.sbuf_tensor("p7_yg", [128, 2 * 2048], BF16, allow_name_mangling=True) as ygs, \
                 nc.sbuf_tensor("p7_pl", [128, 2 * HID], F32, allow_name_mangling=True) as pol, \
                 nc.sbuf_tensor("p7_g", [128, 2 * HID], BF16, allow_name_mangling=True) as gl7, \
                 nc.sbuf_tensor("p7_x", [128, 2 * HID], F32, allow_name_mangling=True) as xl7, \
                 nc.sbuf_tensor("p7_o", [128, 2 * HID], F32, allow_name_mangling=True) as ob7, \
                 nc.psum_tensor("p7_ps", [128, 2 * 2048], F32, allow_name_mangling=True) as ps7:
                barrier()
                wv7 = wout.rearrange("(k p) c -> p k c", p=128)
                free_st = [0, 0]
                free_ps = [("sS", 0), ("sS", 0)]
                free_ld = [0, 0]
                free_ob = [("w0", 0), ("w1", 0)]
                for half in range(2):
                    w_rdy = [0]
                    @block.gpsimd
                    def _(g, half=half):
                        P.w(g, "tS")
                        last = 0
                        for piece in range(4):
                            last = P.inc(g.dma_start(
                                out=woh.rearrange("p (k c) -> p k c", c=2048)[:, piece * 4:(piece + 1) * 4, :],
                                in_=wv7[:, half * NKD + piece * 4: half * NKD + (piece + 1) * 4, :]), "g0", 16)
                        w_rdy[0] = last
                    st_rdy = [0] * NTT
                    ld_rdy = [0] * NTT
                    for t_o in range(NTT + 1):
                        if t_o < NTT:
                            t_i, sl = t_o, t_o % 2
                            emit_ld = True
                        else:
                            emit_ld = False
                        if emit_ld:
                            @block.sync
                            def _(sync, half=half, t_i=t_i, sl=sl):
                                P.w(sync, "tS", free_st[sl])
                                last = 0
                                for k in range(NKD):
                                    last = P.inc(sync.dma_start(
                                        out=ygs[:, sl * 2048 + k * 128: sl * 2048 + (k + 1) * 128],
                                        in_=ygt_sp[(half * NKD + k) * 128:(half * NKD + k + 1) * 128,
                                                   t_i * 128:(t_i + 1) * 128]), "d0" if sl == 0 else "d1", 16)
                                st_rdy[t_i] = last
                                if half == 1:
                                    if t_i == 0:
                                        P.w(sync, "w0"); P.w(sync, "w1")
                                    P.w(sync, "vS", free_ld[sl])
                                    P.inc(sync.dma_start(out=pol[:, sl * HID:(sl + 1) * HID],
                                                         in_=pp_sp[t_i * 128:(t_i + 1) * 128, :]), "d2" if sl == 0 else "d3", 16)
                                    P.inc(sync.dma_start(out=gl7[:, sl * HID:(sl + 1) * HID],
                                                         in_=g_sp[t_i * 128:(t_i + 1) * 128, :]), "d2" if sl == 0 else "d3", 16)
                                    ld_rdy[t_i] = P.inc(sync.dma_start(out=xl7[:, sl * HID:(sl + 1) * HID],
                                                                       in_=xo[t_i * 128:(t_i + 1) * 128, :]), "d2" if sl == 0 else "d3", 16)
                        if t_o == 0:
                            continue
                        t_i = t_o - 1
                        sl = t_i % 2
                        @block.tensor
                        def _(t, half=half, t_i=t_i, sl=sl):
                            P.w(t, "g0", w_rdy[0])
                            P.w(t, "d0" if sl == 0 else "d1", st_rdy[t_i])
                            P.w(t, free_ps[sl][0], free_ps[sl][1])
                            for k in range(NKD):
                                for n in range(4):
                                    ins = t.matmul(ps7[:, sl * 2048 + n * 512: sl * 2048 + (n + 1) * 512],
                                                   ygs[:, sl * 2048 + k * 128: sl * 2048 + (k + 1) * 128],
                                                   woh.rearrange("p (k c) -> p k c", c=2048)[:, k, n * 512:(n + 1) * 512],
                                                   start=(k == 0), stop=(k == NKD - 1))
                            P.inc(ins, "tS", 1)
                            free_st[sl] = P.c["tS"]
                        if half == 0:
                            @block.scalar
                            def _(s, t_i=t_i, sl=sl):
                                P.w(s, "tS")
                                P.w(s, free_ob[sl][0], free_ob[sl][1])
                                P.inc(s.copy(ob7[:, sl * HID:(sl + 1) * HID], ps7[:, sl * 2048: sl * 2048 + 2048]), "sS", 1)
                                free_ps[sl] = ("sS", P.c["sS"])
                                P.w(s, "sS")
                                sem = "w0" if sl == 0 else "w1"
                                free_ob[sl] = (sem, P.inc(s.dma_start(out=pp_sp[t_i * 128:(t_i + 1) * 128, :],
                                                                      in_=ob7[:, sl * HID:(sl + 1) * HID]), sem, 16))
                        else:
                            @block.vector
                            def _(v, t_i=t_i, sl=sl):
                                P.w(v, "tS")
                                P.w(v, "d2" if sl == 0 else "d3", ld_rdy[t_i])
                                P.w(v, free_ob[sl][0], free_ob[sl][1])
                                obs = ob7[:, sl * HID:(sl + 1) * HID]
                                P.inc(v.tensor_tensor(obs, ps7[:, sl * 2048: sl * 2048 + 2048],
                                                pol[:, sl * HID:(sl + 1) * HID], ALU.add), "vS", 1)
                                P.w(v, "vS")
                                free_ps[sl] = ("vS", P.c["vS"])
                                P.inc(v.tensor_tensor(obs, obs, gl7[:, sl * HID:(sl + 1) * HID], ALU.mult), "vS", 1)
                                P.w(v, "vS")
                                P.inc(v.tensor_tensor(obs, obs, xl7[:, sl * HID:(sl + 1) * HID], ALU.add), "vS", 1)
                                free_ld[sl] = P.c["vS"]
                            @block.scalar
                            def _(s, t_i=t_i, sl=sl):
                                P.w(s, "vS")
                                sem = "w2" if sl == 0 else "w3"
                                free_ob[sl] = (sem, P.inc(s.dma_start(out=outd[t_i * 128:(t_i + 1) * 128, :],
                                                                      in_=ob7[:, sl * HID:(sl + 1) * HID]), sem, 16))

        with nc.sbuf_tensor("xnT", [128, NKD * T], BF16, allow_name_mangling=True) as xnT, \
             nc.sbuf_tensor("xsrb", [1, T], BF16, allow_name_mangling=True) as xsrb:
            emit_ln(xc, False, xnT, xsrb)
            emit_inprojT(True, xnT)
            with nc.sbuf_tensor("c_d1", [128, NCH * 64], F32, allow_name_mangling=True) as dtT_all, \
                 nc.sbuf_tensor("c_d2", [128, NCH * 64], F32, allow_name_mangling=True) as dsT_all, \
                 nc.sbuf_tensor("c_d3", [128, NCH * 64], F32, allow_name_mangling=True) as etb_all, \
                 nc.sbuf_tensor("c_d4", [128, NCH * 64], F32, allow_name_mangling=True) as sdo_all, \
                 nc.sbuf_tensor("c_d5", [128, NCH * 64], F32, allow_name_mangling=True) as negA_all:
                emit_prepass(dtT_all, dsT_all, etb_all, sdo_all, negA_all, False)
                emit_ssd(False, dtT_all, dsT_all, etb_all, sdo_all, negA_all)
            emit_ln(xo, True, xnT, xsrb)
            emit_p2(xnT)
            emit_inprojT(False, xnT)
            emit_p4(xnT, xsrb)
        emit_p5()
        with nc.sbuf_tensor("o_d1", [128, NCH * 64], F32, allow_name_mangling=True) as dtT_all, \
             nc.sbuf_tensor("o_d2", [128, NCH * 64], F32, allow_name_mangling=True) as dsT_all, \
             nc.sbuf_tensor("o_d3", [128, NCH * 64], F32, allow_name_mangling=True) as etb_all, \
             nc.sbuf_tensor("o_d4", [128, NCH * 64], F32, allow_name_mangling=True) as sdo_all, \
             nc.sbuf_tensor("o_d5", [128, NCH * 64], F32, allow_name_mangling=True) as negA_all:
            emit_prepass(dtT_all, dsT_all, etb_all, sdo_all, negA_all, True)
            emit_ssd(True, dtT_all, dsT_all, etb_all, sdo_all, negA_all)
        emit_p7()

        @block.gpsimd
        def _(g):
            pass
            for s in ("d0", "d1", "d2", "d3", "d4", "g0", "g1", "w0", "w1", "w2", "w3", "tS", "vS", "sS"): P.w(g, s)
            pass

    es.close()
    return nc


def kernel(**inputs):
    xs = np.ascontiguousarray(np.asarray(inputs["L0_attn_hidden_states"], dtype=np.float32))
    wip = np.ascontiguousarray(np.asarray(inputs["L0_attn_in_proj_w"], dtype=np.float32))
    convw = np.ascontiguousarray(np.asarray(inputs["L0_attn_conv_w"], dtype=np.float32))
    convb = np.ascontiguousarray(np.asarray(inputs["L0_attn_conv_b"], dtype=np.float32).reshape(1, -1))
    dtb = np.ascontiguousarray(np.asarray(inputs["L0_attn_dt_bias"], dtype=np.float32).reshape(1, -1))
    alog = np.ascontiguousarray(np.asarray(inputs["L0_attn_A_log"], dtype=np.float32).reshape(1, -1))
    dvec = np.ascontiguousarray(np.asarray(inputs["L0_attn_D"], dtype=np.float32).reshape(1, -1))
    nw = np.ascontiguousarray(np.asarray(inputs["L0_attn_norm_weight"], dtype=np.float32).reshape(1, -1))
    wout = np.ascontiguousarray(np.asarray(inputs["L0_attn_out_proj_w"], dtype=np.float32))
    wgt = np.ascontiguousarray(np.asarray(inputs["L0_ffn_wq_gate"], dtype=np.int32).T)
    wut = np.ascontiguousarray(np.asarray(inputs["L0_ffn_wq_up"], dtype=np.int32).T)
    wdt = np.ascontiguousarray(np.asarray(inputs["L0_ffn_wq_down"], dtype=np.int32).T)
    sg = np.ascontiguousarray(np.asarray(inputs["L0_ffn_s_gate"], dtype=np.float32).reshape(1, -1))
    su = np.ascontiguousarray(np.asarray(inputs["L0_ffn_s_up"], dtype=np.float32).reshape(1, -1))
    sd = np.ascontiguousarray(np.asarray(inputs["L0_ffn_s_down"], dtype=np.float32).reshape(1, -1))

    idbf = np.eye(128, dtype=np.float32).astype(BF)
    idf = np.eye(128, dtype=np.float32)
    s_i = np.arange(128)[:, None]
    l_i = np.arange(128)[None, :]
    trip = (s_i <= l_i).astype(np.float32)
    trin = -trip
    maskt = np.where(s_i > l_i, np.float32(-1e9), np.float32(0.0)).astype(BF)
    onesr = np.ones((1, 128), np.float32)
    onesrb = np.ones((1, 128), np.float32).astype(BF)
    onescb = np.ones((128, 1), np.float32).astype(BF)
    n128 = np.full((1, 128), -128.0, np.float32).astype(BF)

    zeros_ctx = np.zeros((T, HID), np.float32)
    nc = build_program()
    in_maps = []
    for core in range(8):
        b, h = core // 2, core % 2
        xo_ = np.ascontiguousarray(xs[b, h * T:(h + 1) * T, :])
        xc_ = np.ascontiguousarray(xs[b, 0:T, :]) if h == 1 else zeros_ctx
        rm = np.array([[1.0 if h == 1 else 0.0]], np.float32)
        in_maps.append({
            "xo": xo_, "xc": xc_, "rmaskd": rm, "wip": wip, "convw": convw,
            "convbr": convb, "dtbr": dtb, "alogr": alog, "dvecr": dvec, "nwr": nw,
            "wout": wout, "wgt": wgt, "wut": wut, "wdt": wdt, "sgr": sg, "sur": su,
            "sdr": sd, "idbf_d": idbf, "idf_d": idf, "trip_d": trip, "trin_d": trin,
            "maskt_d": maskt, "onesr_d": onesr, "onesrb_d": onesrb,
            "onescb_d": onescb, "n128_d": n128,
        })
    res = run_bass_kernel_spmd(nc, in_maps, core_ids=list(range(8)))
    out = np.empty((4, 4096, HID), np.float32)
    for core in range(8):
        b, h = core // 2, core % 2
        out[b, h * T:(h + 1) * T, :] = res.results[core]["outd"]
    return out
